# version 05
# baseline (speedup 1.0000x reference)
import contextlib
import numpy as np
import concourse.bass as bass
import concourse.mybir as mybir
from concourse.bass_utils import run_bass_kernel_spmd

F32 = mybir.dt.float32
BF16 = mybir.dt.bfloat16
I32 = mybir.dt.int32
AF = mybir.ActivationFunctionType
OP = mybir.AluOpType
AX = mybir.AxisListType

H = 2048
TPAD = 8320
NT = 8192
TOWN = 4096
TSUB = 2048
NE = 32
FF = 768
FS = 4096
NN = 16
TC = 512
NDT = 16
NCH = NT // TC
GP_DT = set()
BIG = 1.0e30
MAXPOLL = 500000
PHASES = "all"
DBG_LOCAL = False
_LAST_CNT = {}


class Sched:

    def __init__(self, nc, sems, eng, pools):
        self.nc = nc
        self.sems = sems
        self.eng = eng
        self.pool = pools
        self.cnt = {k: 0 for k in sems}
        self.idx = {k: 0 for k in sems}
        self.pu = {k: [0] * len(v) for k, v in pools.items()}
        self.pi = {k: 0 for k in pools}
        self.writes = {}
        self.reads = {}
        self.waited = {}

    @staticmethod
    def _nm(a):
        if a is None or isinstance(a, str):
            return a
        t = getattr(a, "tensor", None)
        if t is not None:
            return t.name
        return a.name

    def _sem(self, tok):
        if isinstance(tok, tuple):
            return self.pool[tok[0]][tok[1]]
        return self.sems[tok]

    def _wait(self, wk, tok, v):
        if v <= 0:
            return
        if self.waited.get((wk, tok), 0) >= v:
            return
        self.eng[wk].wait_ge(self._sem(tok), v)
        self.waited[(wk, tok)] = v

    def pre(self, key, reads, writes, dma=False):
        for t in map(self._nm, reads):
            for tok, (wc, wdma, widx, wkey) in self.writes.get(t, {}).items():
                if wdma:
                    self._wait(key, tok, wc)
                elif wkey == key:
                    if key != 'T':
                        self._wait(key, key, wc)
                else:
                    self._wait(key, tok, wc)
        for t in map(self._nm, writes):
            for rtok, (rc, rdma, ridx, rkey) in self.reads.get(t, {}).items():
                if rdma:
                    self._wait(key, rtok, rc)
                elif rkey != key:
                    self._wait(key, rtok, rc)
                elif key != 'T':
                    self._wait(key, key, rc)
            for tok, (wc, wdma, widx, wkey) in self.writes.get(t, {}).items():
                if wdma:
                    self._wait(key, tok, wc)
                elif wkey != key:
                    self._wait(key, tok, wc)
                elif key != 'T':
                    self._wait(key, key, wc)
        if dma:
            i = self.pi[key]
            self.pi[key] = (i + 1) % len(self.pool[key])
            if self.pu[key][i] > 0:
                self._wait(key, (key, i), 16 * self.pu[key][i])
            self._pend = (key, i)
            return self._pend
        return None

    def post(self, key, inst, reads, writes, dma, inc, tok=None):
        self.idx[key] += 1
        if dma:
            assert inc
            k, i = tok if tok is not None else self._pend
            if inst is not None:
                inst.then_inc(self.pool[k][i], 16)
            self.pu[k][i] += 1
            val = 16 * self.pu[k][i]
            wtok = (k, i)
        else:
            if inc:
                inst.then_inc(self.sems[key], 1)
                self.cnt[key] += 1
            val = self.cnt[key] if inc else self.cnt[key] + 1
            wtok = key
        for t in map(self._nm, reads):
            self.reads.setdefault(t, {})[wtok] = (val, dma, self.idx[key], key)
        for t in map(self._nm, writes):
            self.writes.setdefault(t, {})[wtok] = (val, dma, self.idx[key], key)
        return inst

    def EM(self, key, reads=(), writes=(), dma=False, inc=True):
        self.pre(key, reads, writes, dma=dma)
        return lambda inst: self.post(key, inst, reads, writes, dma, inc)

    def dma_pre(self, key, reads, writes):
        tok = self.pre(key, reads, writes, dma=True)
        return self.pool[tok[0]][tok[1]], tok

    def dma_post(self, key, tok, reads, writes):
        self.post(key, None, reads, writes, True, True, tok=tok)

    def mark_write(self, name, key):
        self.writes.setdefault(self._nm(name), {})[key] = (
            self.cnt[key], False, self.idx[key], key)

    def wait_all_dma(self, wk, keys=None):
        for pk in (keys if keys is not None else self.pool):
            for i, u in enumerate(self.pu[pk]):
                self._wait(wk, (pk, i), 16 * u)

    def barrier(self):
        for wk in self.sems:
            for tk in self.sems:
                if wk != tk:
                    self._wait(wk, tk, self.cnt[tk])
            self.wait_all_dma(wk)


def build_nc():
    nc = bass.Bass("TRN2", target_bir_lowering=False, debug=False)
    dram = nc.dram_tensor

    x_in = dram("x_in", [TPAD, H], F32, kind="ExternalInput").ap()
    x_ffn = dram("x_ffn", [TSUB, H], F32, kind="ExternalInput").ap()
    wip_x = dram("wip_x", [H, H], F32, kind="ExternalInput").ap()
    wip_g = dram("wip_g", [H, H], F32, kind="ExternalInput").ap()
    convw = dram("convw", [H, 4], F32, kind="ExternalInput").ap()
    convb = dram("convb", [H], F32, kind="ExternalInput").ap()
    xpw = dram("xpw", [160, H], F32, kind="ExternalInput").ap()
    dtw = dram("dtw", [H, 128], F32, kind="ExternalInput").ap()
    dtb = dram("dtb", [H], F32, kind="ExternalInput").ap()
    alog = dram("alog", [H, NN], F32, kind="ExternalInput").ap()
    dvec = dram("dvec", [H], F32, kind="ExternalInput").ap()
    wop = dram("wop", [H, 2 * H], F32, kind="ExternalInput").ap()
    rw = dram("rw", [NE, H], F32, kind="ExternalInput").ap()
    wg = dram("wg", [NE, H, FF], F32, kind="ExternalInput").ap()
    wu = dram("wu", [NE, H, FF], F32, kind="ExternalInput").ap()
    wd = dram("wd", [NE, FF, H], F32, kind="ExternalInput").ap()
    swg = dram("swg", [H, FS], F32, kind="ExternalInput").ap()
    swu = dram("swu", [H, FS], F32, kind="ExternalInput").ap()
    swd = dram("swd", [FS, H], F32, kind="ExternalInput").ap()
    sgw = dram("sgw", [H, 1], F32, kind="ExternalInput").ap()
    ident = dram("ident", [128, 128], F32, kind="ExternalInput").ap()
    pmask = dram("pmask", [128, 1], F32, kind="ExternalInput").ap()
    rolev = dram("rolev", [1, 4], I32, kind="ExternalInput").ap()
    noncev = dram("noncev", [1, 4], I32, kind="ExternalInput").ap()
    constv = dram("constv", [128, 8], F32, kind="ExternalInput").ap()
    out = dram("out", [TSUB, H], F32, kind="ExternalOutput").ap()

    xt_sc = dram("xt_sc", [H, TPAD], BF16).ap()
    x_sc = dram("x_sc", [H, NT], BF16).ap()
    dt_sc = dram("dt_sc", [H, NT], BF16).ap()
    gate_sc = dram("gate_sc", [H, TOWN], BF16).ap()
    act_sc = dram("act_sc", [NE, FF, TSUB], BF16).ap()
    mid_sc = dram("mid_sc", [FS, TSUB], BF16).ap()
    wopT_sc = dram("wopT_sc", [2 * H, H], BF16).ap()
    bc_sc = dram("bc_sc", [32, NT], BF16).ap()
    combT_sc = dram("combT_sc", [NE, TSUB], F32).ap()
    sgT_sc = dram("sgT_sc", [1, TSUB], F32).ap()
    _shared = {} if DBG_LOCAL else {"addr_space": "Shared"}
    sh_ygg = dram("sh_ygg", [2, H, TOWN], BF16, **_shared).ap()
    sh_flag = dram("sh_flag", [1, 16], I32, **_shared).ap()

    es = contextlib.ExitStack()
    sb = lambda n, s, d: es.enter_context(nc.sbuf_tensor(n, s, d))

    identf = sb("identf", [128, 128], F32)
    identb = sb("identb", [128, 128], BF16)
    A_bf = sb("A_bf", [128, NDT * NN], BF16)
    D_sb = sb("D_sb", [128, NDT], F32)
    convw_sb = sb("convw_sb", [128, NDT * 4], F32)
    convb_sb = sb("convb_sb", [128, NDT], F32)
    dtb_sb = sb("dtb_sb", [128, NDT], F32)
    pmask_sb = sb("pmask_sb", [128, 1], F32)
    role_sb = sb("role_sb", [1, 4], I32)
    nonce_sb = sb("nonce_sb", [1, 4], I32)
    pollw = sb("pollw", [1, 8], I32)
    sgw_sb = sb("sgw_sb", [128, NDT], F32)
    carry_sb = sb("carry_sb", [128, NDT * NN], F32)
    sgbc = sb("sgbc", [128, TSUB], BF16)
    const_sb = sb("const_sb", [128, 8], F32)
    nc.const_aps.register(F32, 1e-5, const_sb[:, 0:1])
    nc.const_aps.register(F32, 1.0, const_sb[:, 1:2])
    nc.const_aps.register(F32, -1.0, const_sb[:, 2:3])
    nc.const_aps.register(F32, 1.0 / H, const_sb[:, 3:4])
    nc.const_aps.register(F32, 0.0, const_sb[:, 4:5])

    xtstack = contextlib.ExitStack()
    xtown = xtstack.enter_context(nc.sbuf_tensor("xtown", [128, NDT * TSUB], BF16))
    wstack = contextlib.ExitStack()
    wsb = lambda n, s, d: wstack.enter_context(nc.sbuf_tensor(n, s, d))
    xpwT_sb = wsb("xpwT_sb", [128, NDT * 160], BF16)
    wdtT_sb = wsb("wdtT_sb", [128, NDT * 128], BF16)
    rwT_sb = wsb("rwT_sb", [128, NDT * NE], F32)

    sems = {k: es.enter_context(nc.semaphore("sem_" + k)) for k in
            ("T", "V", "S", "G", "Q")}
    pools = {
        "Q": [es.enter_context(nc.semaphore(f"pq{i}")) for i in range(24)],
        "G": [es.enter_context(nc.semaphore(f"pg{i}")) for i in range(12)],
        "S": [es.enter_context(nc.semaphore(f"ps{i}")) for i in range(8)],
    }
    psem = es.enter_context(nc.semaphore("sem_poll"))
    block = es.enter_context(nc.Block())

    @block.sync
    def _(sync):
        T, V, S, G, Q = "T", "V", "S", "G", "Q"
        t_e, v_e, s_e, g_e, q_e = nc.tensor, nc.vector, nc.scalar, nc.gpsimd, nc.sync
        sc = Sched(nc, sems, {"T": t_e, "V": v_e, "S": s_e, "G": g_e, "Q": q_e},
                   pools)
        EM = sc.EM

        q_role = es.enter_context(q_e.register("q_role"))

        def role_dma(key, out0, out1, in_ap, reg):
            sem, tok = sc.dma_pre(key, [in_ap], [out0, out1])
            eng = sc.eng[key]
            with eng.If_eq(reg, 0):
                eng.dma_start(out=out0, in_=in_ap).then_inc(sem, 16)
            with eng.Else():
                eng.dma_start(out=out1, in_=in_ap).then_inc(sem, 16)
            sc.dma_post(key, tok, [in_ap], [out0, out1])

        with contextlib.ExitStack() as ph:
            p_tp = ph.enter_context(nc.psum_tensor("p0_tp", [128, 128], BF16))
            p_tpf = ph.enter_context(nc.psum_tensor("p0_tpf", [128, 128], F32))
            stage = ph.enter_context(nc.sbuf_tensor("st0", [128, 2048], F32))
            stageb = ph.enter_context(nc.sbuf_tensor("st0b", [128, 2048], BF16))

            EM(Q, [constv], [const_sb], dma=True)(q_e.dma_start(out=const_sb[:], in_=constv))
            EM(Q, [ident], [identf], dma=True)(q_e.dma_start(out=identf[:], in_=ident))
            sc.barrier()
            EM(G, [ident], [identb], dma=True)(g_e.dma_start(out=identb[:], in_=ident))
            EM(Q, [pmask], [pmask_sb], dma=True)(q_e.dma_start(out=pmask_sb[:], in_=pmask))
            EM(Q, [rolev], [role_sb], dma=True)(q_e.dma_start(out=role_sb[:], in_=rolev))
            EM(Q, [noncev], [nonce_sb], dma=True)(q_e.dma_start(out=nonce_sb[:], in_=noncev))
            EM(Q, [alog], [stage], dma=True)(q_e.dma_start(
                out=stage[:, 0:NDT * NN].rearrange("p (a n) -> p a n", a=NDT),
                in_=alog.rearrange("(a p) n -> p a n", p=128)))
            EM(S, [stage], [stage])(s_e.activation(
                out=stage[:, 0:NDT * NN], in_=stage[:, 0:NDT * NN], func=AF.Exp))
            EM(V, [stage], [A_bf])(v_e.tensor_scalar_mul(
                A_bf[:], stage[:, 0:NDT * NN], -1.0))
            EM(Q, [dvec], [D_sb], dma=True)(q_e.dma_start(
                out=D_sb[:].unsqueeze(2),
                in_=dvec.rearrange("(a p) -> p a", p=128).unsqueeze(2),
                allow_slow_non_contiguous=True))
            EM(Q, [convw], [convw_sb], dma=True)(q_e.dma_start(
                out=convw_sb[:].rearrange("p (a k) -> p a k", a=NDT),
                in_=convw.rearrange("(a p) k -> p a k", p=128)))
            EM(Q, [convb], [convb_sb], dma=True)(q_e.dma_start(
                out=convb_sb[:].unsqueeze(2),
                in_=convb.rearrange("(a p) -> p a", p=128).unsqueeze(2),
                allow_slow_non_contiguous=True))
            EM(Q, [dtb], [dtb_sb], dma=True)(q_e.dma_start(
                out=dtb_sb[:].unsqueeze(2),
                in_=dtb.rearrange("(a p) -> p a", p=128).unsqueeze(2),
                allow_slow_non_contiguous=True))
            EM(Q, [sgw], [sgw_sb], dma=True)(q_e.dma_start(
                out=sgw_sb[:].unsqueeze(2),
                in_=sgw.rearrange("(a p) o -> p a o", p=128),
                allow_slow_non_contiguous=True))
            EM(V, [], [carry_sb])(v_e.memset(carry_sb[:], 0.0))

            EM(G, [xpw], [stageb], dma=True)(g_e.dma_start(
                out=stageb[:], in_=xpw[0:128, :]))
            for d in range(NDT):
                EM(T, [stageb, identb], [p_tp])(t_e.transpose(
                    p_tp[:], stageb[:, d * 128:(d + 1) * 128], identb[:]))
                EM(V, [p_tp], [xpwT_sb])(v_e.tensor_copy(
                    xpwT_sb[:, d * 160:d * 160 + 128], p_tp[:]))
            EM(G, [xpw], [stageb], dma=True)(g_e.dma_start(
                out=stageb[0:32, :], in_=xpw[128:160, :]))
            for d in range(NDT):
                EM(T, [stageb, identb], [p_tp])(t_e.transpose(
                    p_tp[:, 0:32], stageb[0:32, d * 128:(d + 1) * 128],
                    identb[0:32, 0:32]))
                EM(V, [p_tp], [xpwT_sb])(v_e.tensor_copy(
                    xpwT_sb[:, d * 160 + 128:d * 160 + 160], p_tp[:, 0:32]))
            EM(G, [dtw], [stageb], dma=True)(g_e.dma_start(
                out=stageb[:].rearrange("p (a r) -> p a r", a=NDT),
                in_=dtw.rearrange("(a p) r -> p a r", p=128)))
            for d in range(NDT):
                EM(T, [stageb, identb], [p_tp])(t_e.transpose(
                    p_tp[:], stageb[:, d * 128:(d + 1) * 128], identb[:]))
                EM(V, [p_tp], [wdtT_sb])(v_e.tensor_copy(
                    wdtT_sb[:, d * 128:(d + 1) * 128], p_tp[:]))
            EM(Q, [rw], [stage], dma=True)(q_e.dma_start(out=stage[0:32, :], in_=rw))
            for d in range(NDT):
                EM(T, [stage, identf], [p_tpf])(t_e.transpose(
                    p_tpf[:, 0:32], stage[0:32, d * 128:(d + 1) * 128],
                    identf[0:32, 0:32]))
                EM(V, [p_tpf], [rwT_sb])(v_e.tensor_copy(
                    rwT_sb[:, d * NE:(d + 1) * NE], p_tpf[:, 0:32]))
            wotst = ph.enter_context(nc.sbuf_tensor("wotst", [128, 2048], BF16))
            for hmt in range(NDT):
                for ihalf in range(2):
                    EM(G, [wop], [stageb], dma=True)(g_e.dma_start(
                        out=stageb[:],
                        in_=wop[hmt * 128:(hmt + 1) * 128,
                                ihalf * 2048:(ihalf + 1) * 2048]))
                    for ib in range(16):
                        EM(T, [stageb, identb], [p_tp])(t_e.transpose(
                            p_tp[:], stageb[:, ib * 128:(ib + 1) * 128], identb[:]))
                        EM(V, [p_tp], [wotst])(v_e.tensor_copy(
                            wotst[:, ib * 128:(ib + 1) * 128], p_tp[:]))
                    EM(Q, [wotst], [wopT_sc], dma=True)(q_e.dma_start(
                        out=wopT_sc.rearrange("(a p) c -> p a c", p=128)
                        [:, ihalf * 16:(ihalf + 1) * 16,
                         hmt * 128:(hmt + 1) * 128],
                        in_=wotst[:].rearrange("p (a t) -> p a t", a=16)))
            sc.pre(Q, [role_sb], [])
            q_e.reg_load(q_role, role_sb[0:1, 0:1])

        sc.barrier()
        with contextlib.ExitStack() as ph:
            p_tp = ph.enter_context(nc.psum_tensor("pa_tp", [128, 128], BF16))
            p_tpf = ph.enter_context(nc.psum_tensor("pa_tpf", [128, 128], F32))
            p_rt = ph.enter_context(nc.psum_tensor("pa_rt", [128, NE], F32))
            p_sg = ph.enter_context(nc.psum_tensor("pa_sg", [128, 8], F32))
            p_tc = ph.enter_context(nc.psum_tensor("pa_tc", [32, 128], F32))
            xtile = ph.enter_context(nc.sbuf_tensor("a_xt", [128, H], F32))
            xm = ph.enter_context(nc.sbuf_tensor("a_xm", [128, H], F32))
            sq = ph.enter_context(nc.sbuf_tensor("a_sq", [128, H], F32))
            cols = ph.enter_context(nc.sbuf_tensor("a_cols", [128, 8], F32))
            xnb = ph.enter_context(nc.sbuf_tensor("a_xnb", [128, H], BF16))
            xts = ph.enter_context(nc.sbuf_tensor("a_xts", [128, NDT * 128], BF16))
            xnTf = ph.enter_context(nc.sbuf_tensor("a_xnTf", [128, NDT * 128], F32))
            prb = ph.enter_context(nc.sbuf_tensor("a_prb", [128, NE], F32))
            tk = ph.enter_context(nc.sbuf_tensor("a_tk", [128, 8], F32))
            comb = ph.enter_context(nc.sbuf_tensor("a_comb", [128, NE], F32))
            eqm = ph.enter_context(nc.sbuf_tensor("a_eqm", [128, NE], F32))
            combT = ph.enter_context(nc.sbuf_tensor("a_combT", [32, 128], F32))

            def ln_tile(src_ap, xsc_col, ffn_tile):
                EM(Q, [src_ap], [xtile], dma=True)(q_e.dma_start(out=xtile[:], in_=src_ap))
                EM(V, [xtile], [cols])(v_e.reduce_sum(cols[:, 0:1], xtile[:], axis=AX.X))
                EM(V, [cols], [cols])(v_e.tensor_scalar_mul(cols[:, 1:2], cols[:, 0:1], 1.0 / H))
                EM(V, [xtile, cols], [xm])(v_e.tensor_scalar(
                    out=xm[:], in0=xtile[:], scalar1=cols[:, 1:2], scalar2=None,
                    op0=OP.subtract))
                EM(S, [xm], [sq, cols])(s_e.activation(
                    out=sq[:], in_=xm[:], func=AF.Square, accum_out=cols[:, 2:3]))
                EM(S, [cols], [cols])(s_e.activation(
                    out=cols[:, 3:4], in_=cols[:, 2:3], func=AF.Sqrt,
                    scale=1.0 / H, bias=1e-5))
                EM(V, [cols], [cols])(v_e.reciprocal(cols[:, 4:5], cols[:, 3:4]))
                EM(V, [xm, cols], [xm])(v_e.tensor_scalar(
                    out=xm[:], in0=xm[:], scalar1=cols[:, 4:5], scalar2=None,
                    op0=OP.mult))
                EM(V, [xm], [xnb])(v_e.tensor_copy(xnb[:], xm[:]))
                for ht in range(NDT):
                    EM(T, [xnb, identb], [p_tp])(t_e.transpose(
                        p_tp[:], xnb[:, ht * 128:(ht + 1) * 128], identb[:]))
                    EM(V, [p_tp], [xts])(v_e.tensor_copy(
                        xts[:, ht * 128:(ht + 1) * 128], p_tp[:]))
                if xsc_col is not None:
                    EM(Q, [xts], [xt_sc], dma=True)(q_e.dma_start(
                        out=xt_sc.rearrange("(a p) c -> p a c", p=128)
                        [:, :, xsc_col:xsc_col + 128],
                        in_=xts[:].rearrange("p (a t) -> p a t", a=NDT)))
                if ffn_tile is None:
                    return
                ft = ffn_tile
                for ht in range(NDT):
                    EM(V, [xts], [xtown])(v_e.tensor_copy(
                        xtown[:, ht * TSUB + ft * 128:ht * TSUB + (ft + 1) * 128],
                        xts[:, ht * 128:(ht + 1) * 128]))
                for ht in range(NDT):
                    EM(T, [xm, identf], [p_tpf])(t_e.transpose(
                        p_tpf[:], xm[:, ht * 128:(ht + 1) * 128], identf[:]))
                    EM(V, [p_tpf], [xnTf])(v_e.tensor_copy(
                        xnTf[:, ht * 128:(ht + 1) * 128], p_tpf[:]))
                for ht in range(NDT):
                    EM(T, [xnTf, rwT_sb], [p_rt], inc=(ht == NDT - 1))(t_e.matmul(
                        p_rt[:], xnTf[:, ht * 128:(ht + 1) * 128],
                        rwT_sb[:, ht * NE:(ht + 1) * NE],
                        start=(ht == 0), stop=(ht == NDT - 1)))
                for ht in range(NDT):
                    EM(T, [xnTf, sgw_sb], [p_sg], inc=(ht == NDT - 1))(t_e.matmul(
                        p_sg[:, 0:1], xnTf[:, ht * 128:(ht + 1) * 128],
                        sgw_sb[:, ht:ht + 1],
                        start=(ht == 0), stop=(ht == NDT - 1)))
                EM(V, [p_rt], [cols])(v_e.reduce_max(cols[:, 5:6], p_rt[:], axis=AX.X))
                EM(V, [cols], [cols])(v_e.tensor_scalar_mul(cols[:, 6:7], cols[:, 5:6], -1.0))
                EM(S, [p_rt, cols], [prb, cols])(s_e.activation(
                    out=prb[:], in_=p_rt[:], func=AF.Exp, bias=cols[:, 6:7],
                    accum_out=cols[:, 7:8]))
                EM(V, [cols], [cols])(v_e.reciprocal(cols[:, 0:1], cols[:, 7:8]))
                EM(V, [prb, cols], [prb])(v_e.tensor_scalar(
                    out=prb[:], in0=prb[:], scalar1=cols[:, 0:1], scalar2=None,
                    op0=OP.mult))
                EM(V, [], [comb])(v_e.memset(comb[:], 0.0))
                EM(V, [], [tk])(v_e.memset(tk[:, 0:1], 0.0))
                for r in range(4):
                    EM(V, [prb], [tk])(v_e.reduce_max(tk[:, 1:2], prb[:], axis=AX.X))
                    EM(V, [prb, tk], [eqm])(v_e.tensor_scalar(
                        out=eqm[:], in0=prb[:], scalar1=tk[:, 1:2], scalar2=None,
                        op0=OP.is_equal))
                    EM(V, [tk], [tk])(v_e.tensor_add(tk[:, 0:1], tk[:, 0:1], tk[:, 1:2]))
                    EM(V, [eqm, tk, comb], [comb])(v_e.scalar_tensor_tensor(
                        out=comb[:], in0=eqm[:], scalar=tk[:, 1:2], in1=comb[:],
                        op0=OP.mult, op1=OP.add))
                    EM(V, [eqm, prb], [prb])(v_e.scalar_tensor_tensor(
                        out=prb[:], in0=eqm[:], scalar=-BIG, in1=prb[:],
                        op0=OP.mult, op1=OP.add))
                EM(V, [tk], [tk])(v_e.reciprocal(tk[:, 2:3], tk[:, 0:1]))
                EM(V, [comb, tk], [comb])(v_e.tensor_scalar(
                    out=comb[:], in0=comb[:], scalar1=tk[:, 2:3], scalar2=None,
                    op0=OP.mult))
                EM(T, [comb, identf], [p_tc])(t_e.transpose(p_tc[:], comb[:], identf[:]))
                EM(V, [p_tc], [combT])(v_e.tensor_copy(combT[:], p_tc[:]))
                EM(Q, [combT], [combT_sc], dma=True)(q_e.dma_start(
                    out=combT_sc[:, ft * 128:(ft + 1) * 128], in_=combT[:]))
                EM(S, [p_sg], [cols])(s_e.activation(
                    out=cols[:, 1:2], in_=p_sg[:, 0:1], func=AF.Sigmoid))
                EM(Q, [cols], [sgT_sc], dma=True)(q_e.dma_start(
                    out=sgT_sc[0:1, ft * 128:(ft + 1) * 128].rearrange("o t -> t o"),
                    in_=cols[:, 1:2], allow_slow_non_contiguous=True))

            for tt in range(TPAD // 128):
                ln_tile(x_in[tt * 128:(tt + 1) * 128, :], tt * 128, None)
            for ft in range(TSUB // 128):
                ln_tile(x_ffn[ft * 128:(ft + 1) * 128, :], None, ft)

        sc.barrier()
        if PHASES != "toA":
         with contextlib.ExitStack() as ph:
            p_m = [ph.enter_context(nc.psum_tensor(f"pc{i}", [128, 512], F32))
                   for i in range(2)]
            p_tp = ph.enter_context(nc.psum_tensor("pc_tp", [128, 128], BF16))
            wstage = ph.enter_context(nc.sbuf_tensor("c_ws", [128, 4 * H], BF16))
            lhsT = ph.enter_context(nc.sbuf_tensor("c_lh", [128, 64 * 128], BF16))
            xtt = [ph.enter_context(nc.sbuf_tensor(f"c_xt{i}", [128, NDT * 512], BF16))
                   for i in range(2)]
            xcat = ph.enter_context(nc.sbuf_tensor("c_xc", [128, 4 * 516], F32))
            cacc = ph.enter_context(nc.sbuf_tensor("c_ca", [128, 512], F32))
            xbf = ph.enter_context(nc.sbuf_tensor("c_xb", [128, 512], BF16))
            gbf = ph.enter_context(nc.sbuf_tensor("c_gb", [128, 512], BF16))

            TBL = [(i * 512, 512) for i in range(16)] + [(8192, 128)]

            def load_lhsT(wsrc, ob):
                EM(G, [wsrc], [wstage], dma=True)(g_e.dma_start(
                    out=wstage[:].rearrange("p (a h) -> p a h", a=4),
                    in_=wsrc[ob * 512:(ob + 1) * 512, :]
                    .rearrange("(a p) h -> p a h", p=128)))
                for ot in range(4):
                    for ht in range(NDT):
                        EM(T, [wstage, identb], [p_tp])(t_e.transpose(
                            p_tp[:], wstage[:, ot * H + ht * 128:ot * H + (ht + 1) * 128],
                            identb[:]))
                        EM(V, [p_tp], [lhsT])(v_e.tensor_copy(
                            lhsT[:, (ot * 16 + ht) * 128:(ot * 16 + ht + 1) * 128],
                            p_tp[:]))

            for ob in range(4):
                load_lhsT(wip_x, ob)
                EM(V, [], [xcat])(v_e.memset(xcat[:], 0.0))
                for tbi, (c0, nt) in enumerate(TBL):
                    xb = xtt[tbi % 2]
                    EM(Q, [xt_sc], [xb], dma=True)(q_e.dma_start(
                        out=xb[:].rearrange("p (a t) -> p a t", a=NDT)[:, :, 0:nt],
                        in_=xt_sc.rearrange("(a p) c -> p a c", p=128)[:, :, c0:c0 + nt]))
                    for ot in range(4):
                        ota = ob * 4 + ot
                        pm = p_m[ot % 2]
                        for ht in range(NDT):
                            EM(T, [lhsT, xb], [pm], inc=(ht == NDT - 1))(t_e.matmul(
                                pm[:, 0:nt],
                                lhsT[:, (ot * 16 + ht) * 128:(ot * 16 + ht + 1) * 128],
                                xb[:, ht * 512:ht * 512 + nt],
                                start=(ht == 0), stop=(ht == NDT - 1)))
                        EM(V, [pm], [xcat])(v_e.tensor_copy(
                            xcat[:, ot * 516 + 3:ot * 516 + 3 + nt], pm[:, 0:nt]))
                        EM(V, [xcat, convw_sb], [cacc])(v_e.tensor_scalar(
                            out=cacc[:, 0:nt], in0=xcat[:, ot * 516:ot * 516 + nt],
                            scalar1=convw_sb[:, ota * 4:ota * 4 + 1], scalar2=None,
                            op0=OP.mult))
                        for k in range(1, 4):
                            EM(V, [xcat, convw_sb, cacc], [cacc])(v_e.scalar_tensor_tensor(
                                out=cacc[:, 0:nt],
                                in0=xcat[:, ot * 516 + k:ot * 516 + k + nt],
                                scalar=convw_sb[:, ota * 4 + k:ota * 4 + k + 1],
                                in1=cacc[:, 0:nt], op0=OP.mult, op1=OP.add))
                        EM(V, [xcat], [xcat])(v_e.tensor_copy(
                            xcat[:, ot * 516:ot * 516 + 3],
                            xcat[:, ot * 516 + nt:ot * 516 + nt + 3]))
                        EM(S, [cacc, convb_sb], [xbf])(s_e.activation(
                            out=xbf[:, 0:nt], in_=cacc[:, 0:nt], func=AF.Silu,
                            bias=convb_sb[:, ota:ota + 1]))
                        if c0 + nt > 128:
                            s0 = max(0, 128 - c0)
                            EM(Q, [xbf], [x_sc], dma=True)(q_e.dma_start(
                                out=x_sc[ota * 128:(ota + 1) * 128,
                                         c0 + s0 - 128:c0 + nt - 128],
                                in_=xbf[:, s0:nt]))

            for ob in range(4):
                load_lhsT(wip_g, ob)
                for tbi in range(8):
                    c0 = 4224 + tbi * 512
                    xb = xtt[tbi % 2]
                    EM(Q, [xt_sc], [xb], dma=True)(q_e.dma_start(
                        out=xb[:].rearrange("p (a t) -> p a t", a=NDT)[:, :, 0:512],
                        in_=xt_sc.rearrange("(a p) c -> p a c", p=128)[:, :, c0:c0 + 512]))
                    for ot in range(4):
                        ota = ob * 4 + ot
                        pm = p_m[ot % 2]
                        for ht in range(NDT):
                            EM(T, [lhsT, xb], [pm], inc=(ht == NDT - 1))(t_e.matmul(
                                pm[:],
                                lhsT[:, (ot * 16 + ht) * 128:(ot * 16 + ht + 1) * 128],
                                xb[:, ht * 512:(ht + 1) * 512],
                                start=(ht == 0), stop=(ht == NDT - 1)))
                        EM(V, [pm], [gbf])(v_e.tensor_copy(gbf[:], pm[:]))
                        EM(Q, [gbf], [gate_sc], dma=True)(q_e.dma_start(
                            out=gate_sc[ota * 128:(ota + 1) * 128,
                                        c0 - 4224:c0 - 4224 + 512],
                            in_=gbf[:]))

        sc.barrier()
        if PHASES not in ("toA", "toC"):
         with contextlib.ExitStack() as ph:
            p_xp = ph.enter_context(nc.psum_tensor("pe_xp", [128, 512], F32))
            p_xp2 = ph.enter_context(nc.psum_tensor("pe_xp2", [32, 512], F32))
            p_dt = [ph.enter_context(nc.psum_tensor(f"pe_dt{i}", [128, 512], F32))
                    for i in range(2)]
            exi = ph.enter_context(nc.sbuf_tensor("e_xi", [128, NDT * 512], BF16))
            dtin = ph.enter_context(nc.sbuf_tensor("e_dtin", [128, 512], BF16))
            bcrows = ph.enter_context(nc.sbuf_tensor("e_bc", [32, 512], BF16))
            dtt1 = ph.enter_context(nc.sbuf_tensor("e_t1", [128, 512], F32))
            dtt2 = ph.enter_context(nc.sbuf_tensor("e_t2", [128, 512], F32))
            dtbf = ph.enter_context(nc.sbuf_tensor("e_dtb", [128, 512], BF16))

            for tb in range(16):
                EM(Q, [x_sc], [exi], dma=True)(q_e.dma_start(
                    out=exi[:].rearrange("p (a t) -> p a t", a=NDT),
                    in_=x_sc.rearrange("(a p) c -> p a c", p=128)
                    [:, :, tb * 512:(tb + 1) * 512]))
                for d in range(NDT):
                    EM(T, [xpwT_sb, exi], [p_xp], inc=(d == NDT - 1))(t_e.matmul(
                        p_xp[:], xpwT_sb[:, d * 160:d * 160 + 128],
                        exi[:, d * 512:(d + 1) * 512],
                        start=(d == 0), stop=(d == NDT - 1)))
                for d in range(NDT):
                    EM(T, [xpwT_sb, exi], [p_xp2], inc=(d == NDT - 1))(t_e.matmul(
                        p_xp2[:], xpwT_sb[:, d * 160 + 128:d * 160 + 160],
                        exi[:, d * 512:(d + 1) * 512],
                        start=(d == 0), stop=(d == NDT - 1)))
                EM(V, [p_xp2], [bcrows])(v_e.tensor_copy(bcrows[:], p_xp2[:]))
                EM(Q, [bcrows], [bc_sc], dma=True)(q_e.dma_start(
                    out=bc_sc[:, tb * 512:(tb + 1) * 512], in_=bcrows[:]))
                EM(V, [p_xp], [dtin])(v_e.tensor_copy(dtin[:], p_xp[:]))
                for d in range(NDT):
                    pm = p_dt[d % 2]
                    EM(T, [wdtT_sb, dtin], [pm])(t_e.matmul(
                        pm[:], wdtT_sb[:, d * 128:(d + 1) * 128], dtin[:],
                        start=True, stop=True))
                    EM(S, [pm, dtb_sb], [dtt1])(s_e.activation(
                        out=dtt1[:], in_=pm[:], func=AF.Relu,
                        bias=dtb_sb[:, d:d + 1]))
                    EM(S, [pm, dtb_sb], [dtt2])(s_e.activation(
                        out=dtt2[:], in_=pm[:], func=AF.Abs,
                        bias=dtb_sb[:, d:d + 1]))
                    EM(S, [dtt2], [dtt2])(s_e.activation(
                        out=dtt2[:], in_=dtt2[:], func=AF.Exp, scale=-1.0))
                    EM(S, [dtt2], [dtt2])(s_e.activation(
                        out=dtt2[:], in_=dtt2[:], func=AF.Ln, bias=1.0))
                    EM(V, [dtt1, dtt2], [dtbf])(v_e.tensor_add(dtbf[:], dtt1[:], dtt2[:]))
                    EM(Q, [dtbf], [dt_sc], dma=True)(q_e.dma_start(
                        out=dt_sc[d * 128:(d + 1) * 128, tb * 512:(tb + 1) * 512],
                        in_=dtbf[:]))

        sc.barrier()
        wstack.close()
        HALF = NCH // 2
        if PHASES not in ("toE", "toA", "toC"):
         with contextlib.ExitStack() as ph:
            p_g = [ph.enter_context(nc.psum_tensor(f"pg{i}", [128, 1024], F32))
                   for i in range(2)]
            p_u = [ph.enter_context(nc.psum_tensor(f"pu{i}", [128, 1024], F32))
                   for i in range(2)]
            wgt = [ph.enter_context(nc.sbuf_tensor(f"g_wg{i}", [128, NDT * 128], BF16))
                   for i in range(2)]
            wut = [ph.enter_context(nc.sbuf_tensor(f"g_wu{i}", [128, NDT * 128], BF16))
                   for i in range(2)]
            sgl = [ph.enter_context(nc.sbuf_tensor(f"g_sgl{i}", [128, 1024], BF16))
                   for i in range(2)]
            ucb = [ph.enter_context(nc.sbuf_tensor(f"g_ucb{i}", [128, 1024], BF16))
                   for i in range(2)]
            actst = [ph.enter_context(nc.sbuf_tensor(f"g_act{i}", [128, TSUB], BF16))
                     for i in range(2)]
            cbbc = [ph.enter_context(nc.sbuf_tensor(f"g_cb{i}", [128, TSUB], BF16))
                    for i in range(2)]

            def g_unit(bi, wsrc_g, wsrc_u, col0, cbc, dst, dst_r0):
                EM(G, [wsrc_g], [wgt[bi]], dma=True)(g_e.dma_start(
                    out=wgt[bi][:].rearrange("p (a f) -> p a f", a=NDT),
                    in_=wsrc_g[:, col0:col0 + 128]
                    .rearrange("(a p) f -> p a f", p=128)))
                EM(G, [wsrc_u], [wut[bi]], dma=True)(g_e.dma_start(
                    out=wut[bi][:].rearrange("p (a f) -> p a f", a=NDT),
                    in_=wsrc_u[:, col0:col0 + 128]
                    .rearrange("(a p) f -> p a f", p=128)))
                for tb in range(2):
                    pg, pu = p_g[tb], p_u[tb]
                    for hf in range(2):
                        c0 = tb * 1024 + hf * 512
                        for ht in range(NDT):
                            EM(T, [wgt[bi], xtown], [pg],
                               inc=(ht == NDT - 1))(t_e.matmul(
                                pg[:, hf * 512:(hf + 1) * 512],
                                wgt[bi][:, ht * 128:(ht + 1) * 128],
                                xtown[:, ht * TSUB + c0:ht * TSUB + c0 + 512],
                                start=(ht == 0), stop=(ht == NDT - 1)))
                        for ht in range(NDT):
                            EM(T, [wut[bi], xtown], [pu],
                               inc=(ht == NDT - 1))(t_e.matmul(
                                pu[:, hf * 512:(hf + 1) * 512],
                                wut[bi][:, ht * 128:(ht + 1) * 128],
                                xtown[:, ht * TSUB + c0:ht * TSUB + c0 + 512],
                                start=(ht == 0), stop=(ht == NDT - 1)))
                    EM(S, [pg], [sgl[tb]])(s_e.activation(
                        out=sgl[tb][:], in_=pg[:], func=AF.Silu))
                    EM(V, [pu, cbc], [ucb[tb]])(v_e.tensor_mul(
                        ucb[tb][:], pu[:], cbc[:, tb * 1024:(tb + 1) * 1024]))
                    EM(V, [sgl[tb], ucb[tb]], [actst[bi]])(v_e.tensor_mul(
                        actst[bi][:, tb * 1024:(tb + 1) * 1024],
                        sgl[tb][:], ucb[tb][:]))
                EM(Q, [actst[bi]], [dst], dma=True)(q_e.dma_start(
                    out=dst[dst_r0:dst_r0 + 128, :], in_=actst[bi][:]))

            def gen_G():
                EM(G, [sgT_sc], [sgbc], dma=True)(g_e.dma_start(
                    out=sgbc[:].rearrange("p (o t) -> p o t", o=1),
                    in_=sgT_sc.unsqueeze(0).broadcast_to((128, 1, TSUB))))
                for e in range(NE):
                    cb = cbbc[e % 2]
                    EM(G, [combT_sc], [cb], dma=True)(g_e.dma_start(
                        out=cb[:].rearrange("p (o t) -> p o t", o=1),
                        in_=combT_sc[e:e + 1, :].unsqueeze(0)
                        .broadcast_to((128, 1, TSUB))))
                    for ftl in range(6):
                        g_unit((e * 6 + ftl) % 2, wg[e], wu[e], ftl * 128,
                               cb, act_sc[e], ftl * 128)
                        yield
                for fsl in range(32):
                    g_unit(fsl % 2, swg, swu, fsl * 128,
                           sgbc, mid_sc, fsl * 128)
                    yield

            sbuf = nc.sbuf_tensor
            Bbc = ph.enter_context(sbuf("f_Bbc", [128, NN * TC], BF16))
            Cbc = ph.enter_context(sbuf("f_Cbc", [128, NN * TC], BF16))
            fb = dict(
                dtp=[ph.enter_context(sbuf(f"f_dt{i}", [128, TC], BF16))
                     for i in range(2)],
                xp=[ph.enter_context(sbuf(f"f_x{i}", [128, TC], BF16))
                    for i in range(2)],
                gt=[ph.enter_context(sbuf(f"f_g{i}", [128, TC], BF16))
                    for i in range(2)],
                dtx=ph.enter_context(sbuf("f_dtx", [128, TC], BF16)),
                adb=ph.enter_context(sbuf("f_adb", [128, NN * TC], BF16)),
                dAf=ph.enter_context(sbuf("f_dAf", [128, NN * TC], BF16)),
                stf=ph.enter_context(sbuf("f_stf", [128, NN * TC], BF16)),
                inj=ph.enter_context(sbuf("f_inj", [128, NN], F32)),
                y1=ph.enter_context(sbuf("f_y1", [128, TC], F32)),
                y2=ph.enter_context(sbuf("f_y2", [128, TC], BF16)),
                ygb=ph.enter_context(sbuf("f_ygb", [128, TC], BF16)),
            )

            def load_piece(ch, d, slot):
                bb = fb
                t0 = ch * TC
                dtp, xp, gt = bb["dtp"][slot], bb["xp"][slot], bb["gt"][slot]
                EM(S, [dt_sc], [dtp], dma=True)(s_e.dma_start(
                    out=dtp[:], in_=dt_sc[d * 128:(d + 1) * 128, t0:t0 + TC]))
                EM(S, [x_sc], [xp], dma=True)(s_e.dma_start(
                    out=xp[:], in_=x_sc[d * 128:(d + 1) * 128, t0:t0 + TC]))
                if ch >= HALF:
                    EM(S, [gate_sc], [gt], dma=True)(s_e.dma_start(
                        out=gt[:], in_=gate_sc[d * 128:(d + 1) * 128,
                                               t0 - TOWN:t0 - TOWN + TC]))

            def piece(ch, d, slot):
                bb = fb
                ee = v_e
                engk = V
                t0 = ch * TC
                dtp, xp, gt = bb["dtp"][slot], bb["xp"][slot], bb["gt"][slot]
                av = A_bf[:, d * NN:(d + 1) * NN].unsqueeze(2).broadcast_to((128, NN, TC))
                dtv = dtp[:].unsqueeze(1).broadcast_to((128, NN, TC))
                adbv = bb["adb"][:].rearrange("p (n t) -> p n t", n=NN)
                dAv = bb["dAf"][:].rearrange("p (n t) -> p n t", n=NN)
                EM(engk, [A_bf, dtp], [bb["adb"]])(ee.tensor_tensor(
                    adbv, av, dtv, OP.mult))
                EM(S, [bb["adb"]], [bb["dAf"]])(s_e.activation(
                    out=bb["dAf"][:], in_=bb["adb"][:], func=AF.Exp))
                EM(engk, [dtp, xp], [bb["dtx"]])(ee.tensor_mul(
                    bb["dtx"][:], dtp[:], xp[:]))
                dtxv = bb["dtx"][:].unsqueeze(1).broadcast_to((128, NN, TC))
                EM(engk, [bb["dtx"], Bbc], [bb["adb"]])(ee.tensor_tensor(
                    adbv, dtxv, Bbc[:].rearrange("p (n t) -> p n t", n=NN), OP.mult))
                cslice = carry_sb[:, d * NN:(d + 1) * NN]
                EM(engk, [bb["dAf"], carry_sb], [bb["inj"]])(ee.tensor_tensor(
                    bb["inj"][:].unsqueeze(2),
                    dAv[:, :, 0:1], cslice.unsqueeze(2), OP.mult))
                EM(engk, [bb["inj"], bb["adb"]], [bb["adb"]])(ee.tensor_tensor(
                    adbv[:, :, 0:1], bb["inj"][:].unsqueeze(2),
                    adbv[:, :, 0:1], OP.add))
                EM(engk, [], [bb["dAf"]])(ee.memset(dAv[:, :, 0:1], 0.0))
                EM(engk, [bb["dAf"], bb["adb"]], [bb["stf"]])(ee.tensor_tensor_scan(
                    out=bb["stf"][:], data0=bb["dAf"][:], data1=bb["adb"][:],
                    initial=0.0, op0=OP.mult, op1=OP.add))
                stv = bb["stf"][:].rearrange("p (n t) -> p n t", n=NN)
                EM(engk, [bb["stf"]], [carry_sb])(ee.tensor_copy(
                    cslice.unsqueeze(2), stv[:, :, TC - 1:TC]))
                if ch == HALF - 1:
                    EM(engk, [carry_sb, pmask_sb], [carry_sb])(ee.tensor_scalar(
                        out=cslice, in0=cslice, scalar1=pmask_sb[:, 0:1],
                        scalar2=None, op0=OP.mult))
                if ch < HALF:
                    return
                EM(engk, [bb["stf"], Cbc], [bb["adb"]])(ee.tensor_tensor(
                    bb["adb"][:], bb["stf"][:], Cbc[:], OP.mult))
                EM(engk, [bb["adb"]], [bb["y1"]])(ee.reduce_sum(
                    bb["y1"][:].unsqueeze(2),
                    bb["adb"][:].rearrange("p (n t) -> p t n", n=NN),
                    axis=AX.X))
                EM(engk, [xp, D_sb, bb["y1"]], [bb["y1"]])(ee.scalar_tensor_tensor(
                    out=bb["y1"][:], in0=xp[:], scalar=D_sb[:, d:d + 1],
                    in1=bb["y1"][:], op0=OP.mult, op1=OP.add))
                EM(S, [gt], [bb["y2"]])(s_e.activation(
                    out=bb["y2"][:], in_=gt[:], func=AF.Silu))
                EM(engk, [bb["y1"], bb["y2"]], [bb["ygb"]])(ee.tensor_mul(
                    bb["ygb"][:], bb["y1"][:], bb["y2"][:]))
                role_dma(Q, sh_ygg[0][d * 128:(d + 1) * 128, t0 - TOWN:t0 - TOWN + TC],
                         sh_ygg[1][d * 128:(d + 1) * 128, t0 - TOWN:t0 - TOWN + TC],
                         bb["ygb"][:], q_role)

            def gen_F():
                seq = [(ch, d) for ch in range(NCH) for d in range(NDT)]
                load_piece(*seq[0], 0)
                for i, (ch, d) in enumerate(seq):
                    if d == 0:
                        t0 = ch * TC
                        EM(S, [bc_sc], [Bbc], dma=True)(s_e.dma_start(
                            out=Bbc[:].rearrange("p (n t) -> p n t", n=NN),
                            in_=bc_sc[0:16, t0:t0 + TC].unsqueeze(0)
                            .broadcast_to((128, 16, TC))))
                        EM(S, [bc_sc], [Cbc], dma=True)(s_e.dma_start(
                            out=Cbc[:].rearrange("p (n t) -> p n t", n=NN),
                            in_=bc_sc[16:32, t0:t0 + TC].unsqueeze(0)
                            .broadcast_to((128, 16, TC))))
                    if i + 1 < len(seq):
                        load_piece(*seq[i + 1], (i + 1) % 2)
                    piece(ch, d, i % 2)
                    yield

            gens = [gen_G()]
            if PHASES != "toG":
                gens.append(gen_F())
            alive = list(gens)
            while alive:
                for gen in list(alive):
                    try:
                        next(gen)
                    except StopIteration:
                        alive.remove(gen)

        xtstack.close()
        if PHASES in ("toF", "toG", "toE", "toA", "toC"):
            sc.wait_all_dma(Q)
            globals()["_LAST_CNT"] = dict(sc.cnt)
            return
        sc.barrier()
        do_poll = PHASES not in ("nopoll",)
        if do_poll:
            sem_f, tok_f = sc.dma_pre(Q, [sh_ygg[0], sh_ygg[1], nonce_sb],
                                      [sh_flag])
            sc.wait_all_dma(Q, keys=[Q])
            with q_e.If_eq(q_role, 0):
                q_e.dma_start(out=sh_flag[0:1, 0:1],
                              in_=nonce_sb[0:1, 0:1]).then_inc(sem_f, 16)
            with q_e.Else():
                q_e.dma_start(out=sh_flag[0:1, 1:2],
                              in_=nonce_sb[0:1, 0:1]).then_inc(sem_f, 16)
            sc.dma_post(Q, tok_f, [nonce_sb], [sh_flag])

        with contextlib.ExitStack() as ph:
          if do_poll:
            r_a = ph.enter_context(g_e.register("r_a"))
            r_b = ph.enter_context(g_e.register("r_b"))
            r_2n = ph.enter_context(g_e.register("r_2n"))
            r_c = ph.enter_context(g_e.register("r_c"))
            r_it = ph.enter_context(g_e.register("r_it"))
            sc.pre(G, [nonce_sb], [pollw])
            g_e.reg_load(r_a, nonce_sb[0:1, 0:1])
            g_e.reg_add(r_2n, r_a, r_a)
            g_e.reg_mov(r_c, 0)
            g_e.reg_mov(r_it, 0)
            POOL = mybir.EngineType.Pool
            nc.br("poll_s", engines=[POOL])
            with nc.body("poll_s", valid_engines=[POOL]):
                g_e.dma_start(out=pollw[0:1, 0:2], in_=sh_flag[0:1, 0:2]) \
                   .then_inc(psem, 16)
                g_e.reg_add(r_c, r_c, 16)
                g_e.wait_ge(psem, r_c)
                g_e.reg_load(r_a, pollw[0:1, 0:1])
                g_e.reg_load(r_b, pollw[0:1, 1:2])
                g_e.reg_add(r_a, r_a, r_b)
                g_e.reg_sub(r_a, r_a, r_2n)
                g_e.reg_add(r_it, r_it, 1)
                g_e.br_eq(r_a, 0, on_true="poll_e", on_false="poll_c")
            with nc.body("poll_c", valid_engines=[POOL]):
                g_e.br_lt(r_it, MAXPOLL, on_true="poll_s", on_false="poll_f")
            with nc.body("poll_f", valid_engines=[POOL]):
                g_e.halt()
                g_e.br("poll_e")
            nc.switch_bb("poll_e")
        g_e.nop().then_inc(sems[G], 1)
        sc.cnt[G] += 1
        sc.idx[G] += 1
        sc.mark_write(sh_ygg[0], G)
        sc.mark_write(sh_ygg[1], G)

        sc.barrier()
        ENTRIES = ([("moe", e, fk) for e in range(NE) for fk in range(6)]
                   + [("sh", fs, 0) for fs in range(32)]
                   + [("op", ik, 0) for ik in range(32)])
        NSUB = 16
        with contextlib.ExitStack() as ph:
            p_mg = [ph.enter_context(nc.psum_tensor(f"pj{i}", [128, 512], F32))
                    for i in range(8)]
            lsl = [ph.enter_context(nc.sbuf_tensor(f"j_l{i}", [128, NSUB * 4 * 128],
                                                   BF16)) for i in range(2)]
            rsl = [ph.enter_context(nc.sbuf_tensor(f"j_r{i}", [128, NSUB * 1024],
                                                   BF16)) for i in range(2)]
            xres = ph.enter_context(nc.sbuf_tensor("j_xr", [128, 4 * 1024], F32))
            outs = ph.enter_context(nc.sbuf_tensor("j_o", [128, 4 * 1024], F32))

            nsubs = len(ENTRIES) // NSUB
            for hmh in range(2):
                for tg in range(4):
                    EM(Q, [x_ffn], [xres], dma=True)(q_e.dma_start(
                        out=xres[:].rearrange("p (a c) -> p a c", a=4),
                        in_=x_ffn.rearrange("(a p) c -> p a c", p=128)
                        [:, tg * 4:(tg + 1) * 4, hmh * 1024:(hmh + 1) * 1024]))
                    for sci in range(nsubs):
                        bi = sci % 2
                        ls, rs = lsl[bi], rsl[bi]
                        for k in range(NSUB):
                            kind, a1, a2 = ENTRIES[sci * NSUB + k]
                            tcols = slice(tg * 512, (tg + 1) * 512)
                            if kind == "moe":
                                EM(Q, [act_sc], [ls], dma=True)(q_e.dma_start(
                                    out=ls[:, k * 512:(k + 1) * 512],
                                    in_=act_sc[a1][a2 * 128:(a2 + 1) * 128, tcols]))
                                EM(G, [wd], [rs], dma=True)(g_e.dma_start(
                                    out=rs[:, k * 1024:(k + 1) * 1024],
                                    in_=wd[a1][a2 * 128:(a2 + 1) * 128,
                                               hmh * 1024:(hmh + 1) * 1024]))
                            elif kind == "sh":
                                EM(Q, [mid_sc], [ls], dma=True)(q_e.dma_start(
                                    out=ls[:, k * 512:(k + 1) * 512],
                                    in_=mid_sc[a1 * 128:(a1 + 1) * 128, tcols]))
                                EM(G, [swd], [rs], dma=True)(g_e.dma_start(
                                    out=rs[:, k * 1024:(k + 1) * 1024],
                                    in_=swd[a1 * 128:(a1 + 1) * 128,
                                            hmh * 1024:(hmh + 1) * 1024]))
                            else:
                                slab = sh_ygg[0] if a1 < 16 else sh_ygg[1]
                                rr = (a1 % 16)
                                sem_s, tok_s = sc.dma_pre(Q, [slab], [ls])
                                with q_e.If_eq(q_role, 0):
                                    q_e.dma_start(
                                        out=ls[:, k * 512:(k + 1) * 512],
                                        in_=slab[rr * 128:(rr + 1) * 128,
                                                 tg * 512:(tg + 1) * 512]) \
                                        .then_inc(sem_s, 16)
                                with q_e.Else():
                                    q_e.dma_start(
                                        out=ls[:, k * 512:(k + 1) * 512],
                                        in_=slab[rr * 128:(rr + 1) * 128,
                                                 TSUB + tg * 512:
                                                 TSUB + (tg + 1) * 512]) \
                                        .then_inc(sem_s, 16)
                                sc.dma_post(Q, tok_s, [slab], [ls])
                                EM(Q, [wopT_sc], [rs], dma=True)(q_e.dma_start(
                                    out=rs[:, k * 1024:(k + 1) * 1024],
                                    in_=wopT_sc[a1 * 128:(a1 + 1) * 128,
                                                hmh * 1024:(hmh + 1) * 1024]))
                        for tt in range(4):
                            for hc in range(2):
                                pm = p_mg[2 * tt + hc]
                                for k in range(NSUB):
                                    EM(T, [ls, rs], [pm],
                                       inc=(k == NSUB - 1))(t_e.matmul(
                                        pm[:],
                                        ls[:, k * 512 + tt * 128:
                                           k * 512 + (tt + 1) * 128],
                                        rs[:, k * 1024 + hc * 512:
                                           k * 1024 + (hc + 1) * 512],
                                        start=(sci == 0 and k == 0),
                                        stop=(sci == nsubs - 1
                                              and k == NSUB - 1)))
                    for tt in range(4):
                        for hc in range(2):
                            pm = p_mg[2 * tt + hc]
                            EM(V, [pm, xres], [outs])(v_e.tensor_add(
                                outs[:, tt * 1024 + hc * 512:
                                     tt * 1024 + (hc + 1) * 512],
                                pm[:],
                                xres[:, tt * 1024 + hc * 512:
                                     tt * 1024 + (hc + 1) * 512]))
                    EM(Q, [outs], [out], dma=True)(q_e.dma_start(
                        out=out.rearrange("(a p) c -> p a c", p=128)
                        [:, tg * 4:(tg + 1) * 4, hmh * 1024:(hmh + 1) * 1024],
                        in_=outs[:].rearrange("p (a c) -> p a c", a=4)))

        sc.wait_all_dma(Q)

    es.close()
    return nc


_NC_CACHE = None


def kernel(L0_attn_hidden_states, L0_attn_in_proj_w, L0_attn_conv_w,
           L0_attn_conv_b, L0_attn_x_proj_weight, L0_attn_dt_proj_weight,
           L0_attn_dt_proj_bias, L0_attn_A_log, L0_attn_D, L0_attn_out_proj_w,
           L0_ffn_router_weight, L0_ffn_w_gate, L0_ffn_w_up, L0_ffn_w_down,
           L0_ffn_sw_gate, L0_ffn_sw_up, L0_ffn_sw_down, L0_ffn_sgate_w):
    global _NC_CACHE
    f32 = np.float32
    x = np.asarray(L0_attn_hidden_states, f32)
    cc = np.ascontiguousarray
    nonce = int(np.random.randint(1, 2 ** 30))
    in_maps = []
    for c in range(8):
        b, S, j = c // 4, (c // 2) % 2, c % 2
        x_in = np.zeros((TPAD, H), f32)
        if S == 1:
            x_in[128:4224] = x[b, 0:4096]
        x_in[4224:8320] = x[b, S * 4096:(S + 1) * 4096]
        x_ffn = cc(x[b, S * 4096 + j * TSUB:S * 4096 + (j + 1) * TSUB])
        wip = np.asarray(L0_attn_in_proj_w, f32)[j * 4096:(j + 1) * 4096]
        m = dict(
            x_in=x_in, x_ffn=x_ffn,
            wip_x=cc(wip[0::2]), wip_g=cc(wip[1::2]),
            convw=cc(np.asarray(L0_attn_conv_w, f32)[j * H:(j + 1) * H]),
            convb=cc(np.asarray(L0_attn_conv_b, f32)[j * H:(j + 1) * H]),
            xpw=cc(np.asarray(L0_attn_x_proj_weight, f32)[j]),
            dtw=cc(np.asarray(L0_attn_dt_proj_weight, f32)[j]),
            dtb=cc(np.asarray(L0_attn_dt_proj_bias, f32)[j]),
            alog=cc(np.asarray(L0_attn_A_log, f32)[j]),
            dvec=cc(np.asarray(L0_attn_D, f32)[j]),
            wop=cc(np.asarray(L0_attn_out_proj_w, f32)),
            rw=cc(np.asarray(L0_ffn_router_weight, f32)),
            wg=cc(np.asarray(L0_ffn_w_gate, f32)),
            wu=cc(np.asarray(L0_ffn_w_up, f32)),
            wd=cc(np.asarray(L0_ffn_w_down, f32)),
            swg=cc(np.asarray(L0_ffn_sw_gate, f32)),
            swu=cc(np.asarray(L0_ffn_sw_up, f32)),
            swd=cc(np.asarray(L0_ffn_sw_down, f32)),
            sgw=cc(np.asarray(L0_ffn_sgate_w, f32)),
            ident=np.eye(128, dtype=f32),
            constv=np.tile(np.array([[1e-5, 1.0, -1.0, 1.0 / H, 0.0, 0.0, 0.0, 0.0]], f32), (128, 1)),
            pmask=np.full((128, 1), float(S), f32),
            rolev=np.full((1, 4), j, np.int32),
            noncev=np.full((1, 4), nonce, np.int32),
        )
        in_maps.append(m)
    if _NC_CACHE is None:
        _NC_CACHE = build_nc()
    res = run_bass_kernel_spmd(_NC_CACHE, in_maps, core_ids=list(range(8)))
    out_full = np.zeros((2, 8192, H), f32)
    for c in range(8):
        b, S, j = c // 4, (c // 2) % 2, c % 2
        r0 = S * 4096 + j * TSUB
        out_full[b, r0:r0 + TSUB, :] = res.results[c]["out"]
    return out_full


# version 06
# speedup vs baseline: 1.2517x; 1.2517x over previous
import contextlib
import numpy as np
import concourse.bass as bass
import concourse.mybir as mybir
from concourse.bass_utils import run_bass_kernel_spmd

F32 = mybir.dt.float32
BF16 = mybir.dt.bfloat16
I32 = mybir.dt.int32
AF = mybir.ActivationFunctionType
OP = mybir.AluOpType
AX = mybir.AxisListType

H = 2048
TPAD = 8320
NT = 8192
TOWN = 4096
TSUB = 2048
NE = 32
FF = 768
FS = 4096
NN = 16
TC = 512
NDT = 16
NCH = NT // TC
GP_DT = set()
BIG = 1.0e30
MAXPOLL = 500000
PHASES = "all"
DBG_LOCAL = False
_LAST_CNT = {}


class Sched:

    def __init__(self, nc, sems, eng, pools):
        self.nc = nc
        self.sems = sems
        self.eng = eng
        self.pool = pools
        self.cnt = {k: 0 for k in sems}
        self.idx = {k: 0 for k in sems}
        self.pu = {k: [0] * len(v) for k, v in pools.items()}
        self.pi = {k: 0 for k in pools}
        self.writes = {}
        self.reads = {}
        self.waited = {}

    @staticmethod
    def _nm(a):
        if a is None or isinstance(a, str):
            return a
        t = getattr(a, "tensor", None)
        if t is not None:
            return t.name
        return a.name

    def _sem(self, tok):
        if isinstance(tok, tuple):
            return self.pool[tok[0]][tok[1]]
        return self.sems[tok]

    def _wait(self, wk, tok, v):
        if v <= 0:
            return
        if self.waited.get((wk, tok), 0) >= v:
            return
        self.eng[wk].wait_ge(self._sem(tok), v)
        self.waited[(wk, tok)] = v

    def pre(self, key, reads, writes, dma=False):
        for t in map(self._nm, reads):
            for tok, (wc, wdma, widx, wkey) in self.writes.get(t, {}).items():
                if wdma:
                    self._wait(key, tok, wc)
                elif wkey == key:
                    if key != 'T':
                        self._wait(key, key, wc)
                else:
                    self._wait(key, tok, wc)
        for t in map(self._nm, writes):
            for rtok, (rc, rdma, ridx, rkey) in self.reads.get(t, {}).items():
                if rdma:
                    self._wait(key, rtok, rc)
                elif rkey != key:
                    self._wait(key, rtok, rc)
                elif key != 'T':
                    self._wait(key, key, rc)
            for tok, (wc, wdma, widx, wkey) in self.writes.get(t, {}).items():
                if wdma:
                    self._wait(key, tok, wc)
                elif wkey != key:
                    self._wait(key, tok, wc)
                elif key != 'T':
                    self._wait(key, key, wc)
        if dma:
            i = self.pi[key]
            self.pi[key] = (i + 1) % len(self.pool[key])
            if self.pu[key][i] > 0:
                self._wait(key, (key, i), 16 * self.pu[key][i])
            self._pend = (key, i)
            return self._pend
        return None

    def post(self, key, inst, reads, writes, dma, inc, tok=None):
        self.idx[key] += 1
        if dma:
            assert inc
            k, i = tok if tok is not None else self._pend
            if inst is not None:
                inst.then_inc(self.pool[k][i], 16)
            self.pu[k][i] += 1
            val = 16 * self.pu[k][i]
            wtok = (k, i)
        else:
            if inc:
                inst.then_inc(self.sems[key], 1)
                self.cnt[key] += 1
            val = self.cnt[key] if inc else self.cnt[key] + 1
            wtok = key
        for t in map(self._nm, reads):
            self.reads.setdefault(t, {})[wtok] = (val, dma, self.idx[key], key)
        for t in map(self._nm, writes):
            self.writes.setdefault(t, {})[wtok] = (val, dma, self.idx[key], key)
        return inst

    def EM(self, key, reads=(), writes=(), dma=False, inc=True):
        self.pre(key, reads, writes, dma=dma)
        return lambda inst: self.post(key, inst, reads, writes, dma, inc)

    def dma_pre(self, key, reads, writes):
        tok = self.pre(key, reads, writes, dma=True)
        return self.pool[tok[0]][tok[1]], tok

    def dma_post(self, key, tok, reads, writes):
        self.post(key, None, reads, writes, True, True, tok=tok)

    def mark_write(self, name, key):
        self.writes.setdefault(self._nm(name), {})[key] = (
            self.cnt[key], False, self.idx[key], key)

    def wait_all_dma(self, wk, keys=None):
        for pk in (keys if keys is not None else self.pool):
            for i, u in enumerate(self.pu[pk]):
                self._wait(wk, (pk, i), 16 * u)

    def barrier(self):
        for wk in self.sems:
            for tk in self.sems:
                if wk != tk:
                    self._wait(wk, tk, self.cnt[tk])
            self.wait_all_dma(wk)


def build_nc():
    nc = bass.Bass("TRN2", target_bir_lowering=False, debug=False)
    dram = nc.dram_tensor

    x_in = dram("x_in", [TPAD, H], F32, kind="ExternalInput").ap()
    x_ffn = dram("x_ffn", [TSUB, H], F32, kind="ExternalInput").ap()
    wip_x = dram("wip_x", [H, H], F32, kind="ExternalInput").ap()
    wip_g = dram("wip_g", [H, H], F32, kind="ExternalInput").ap()
    convw = dram("convw", [H, 4], F32, kind="ExternalInput").ap()
    convb = dram("convb", [H], F32, kind="ExternalInput").ap()
    xpw = dram("xpw", [160, H], F32, kind="ExternalInput").ap()
    dtw = dram("dtw", [H, 128], F32, kind="ExternalInput").ap()
    dtb = dram("dtb", [H], F32, kind="ExternalInput").ap()
    alog = dram("alog", [H, NN], F32, kind="ExternalInput").ap()
    dvec = dram("dvec", [H], F32, kind="ExternalInput").ap()
    wop = dram("wop", [H, 2 * H], F32, kind="ExternalInput").ap()
    rw = dram("rw", [NE, H], F32, kind="ExternalInput").ap()
    wg = dram("wg", [NE, H, FF], F32, kind="ExternalInput").ap()
    wu = dram("wu", [NE, H, FF], F32, kind="ExternalInput").ap()
    wd = dram("wd", [NE, FF, H], F32, kind="ExternalInput").ap()
    swg = dram("swg", [H, FS], F32, kind="ExternalInput").ap()
    swu = dram("swu", [H, FS], F32, kind="ExternalInput").ap()
    swd = dram("swd", [FS, H], F32, kind="ExternalInput").ap()
    sgw = dram("sgw", [H, 1], F32, kind="ExternalInput").ap()
    ident = dram("ident", [128, 128], F32, kind="ExternalInput").ap()
    pmask = dram("pmask", [128, 1], F32, kind="ExternalInput").ap()
    rolev = dram("rolev", [1, 4], I32, kind="ExternalInput").ap()
    noncev = dram("noncev", [1, 4], I32, kind="ExternalInput").ap()
    constv = dram("constv", [128, 8], F32, kind="ExternalInput").ap()
    out = dram("out", [TSUB, H], F32, kind="ExternalOutput").ap()

    xt_sc = dram("xt_sc", [H, TPAD], BF16).ap()
    x_sc = dram("x_sc", [H, NT], BF16).ap()
    dt_sc = dram("dt_sc", [H, NT], BF16).ap()
    gate_sc = dram("gate_sc", [H, TOWN], BF16).ap()
    act_sc = dram("act_sc", [NE, FF, TSUB], BF16).ap()
    mid_sc = dram("mid_sc", [FS, TSUB], BF16).ap()
    wopT_sc = dram("wopT_sc", [2 * H, H], BF16).ap()
    bc_sc = dram("bc_sc", [32, NT], BF16).ap()
    wd_bf = dram("wd_bf", [NE, FF, H], BF16).ap()
    swd_bf = dram("swd_bf", [FS, H], BF16).ap()
    combT_sc = dram("combT_sc", [NE, TSUB], F32).ap()
    sgT_sc = dram("sgT_sc", [1, TSUB], F32).ap()
    _shared = {} if DBG_LOCAL else {"addr_space": "Shared"}
    sh_ygg = dram("sh_ygg", [2, H, TOWN], BF16, **_shared).ap()
    sh_flag = dram("sh_flag", [1, 16], I32, **_shared).ap()

    es = contextlib.ExitStack()
    sb = lambda n, s, d: es.enter_context(nc.sbuf_tensor(n, s, d))

    identf = sb("identf", [128, 128], F32)
    identb = sb("identb", [128, 128], BF16)
    A_bf = sb("A_bf", [128, NDT * NN], BF16)
    D_sb = sb("D_sb", [128, NDT], F32)
    convw_sb = sb("convw_sb", [128, NDT * 4], F32)
    convb_sb = sb("convb_sb", [128, NDT], F32)
    dtb_sb = sb("dtb_sb", [128, NDT], F32)
    pmask_sb = sb("pmask_sb", [128, 1], F32)
    role_sb = sb("role_sb", [1, 4], I32)
    nonce_sb = sb("nonce_sb", [1, 4], I32)
    pollw = sb("pollw", [1, 8], I32)
    sgw_sb = sb("sgw_sb", [128, NDT], F32)
    carry_sb = sb("carry_sb", [128, NDT * NN], F32)
    sgbc = sb("sgbc", [128, TSUB], BF16)
    const_sb = sb("const_sb", [128, 8], F32)
    nc.const_aps.register(F32, 1e-5, const_sb[:, 0:1])
    nc.const_aps.register(F32, 1.0, const_sb[:, 1:2])
    nc.const_aps.register(F32, -1.0, const_sb[:, 2:3])
    nc.const_aps.register(F32, 1.0 / H, const_sb[:, 3:4])
    nc.const_aps.register(F32, 0.0, const_sb[:, 4:5])

    xtstack = contextlib.ExitStack()
    xtown = xtstack.enter_context(nc.sbuf_tensor("xtown", [128, NDT * TSUB], BF16))
    wstack = contextlib.ExitStack()
    wsb = lambda n, s, d: wstack.enter_context(nc.sbuf_tensor(n, s, d))
    xpwT_sb = wsb("xpwT_sb", [128, NDT * 160], BF16)
    wdtT_sb = wsb("wdtT_sb", [128, NDT * 128], BF16)
    rwT_sb = wsb("rwT_sb", [128, NDT * NE], F32)

    sems = {k: es.enter_context(nc.semaphore("sem_" + k)) for k in
            ("T", "V", "S", "G", "Q")}
    pools = {
        "Q": [es.enter_context(nc.semaphore(f"pq{i}")) for i in range(24)],
        "G": [es.enter_context(nc.semaphore(f"pg{i}")) for i in range(12)],
        "S": [es.enter_context(nc.semaphore(f"ps{i}")) for i in range(8)],
    }
    psem = es.enter_context(nc.semaphore("sem_poll"))
    block = es.enter_context(nc.Block())

    @block.sync
    def _(sync):
        T, V, S, G, Q = "T", "V", "S", "G", "Q"
        t_e, v_e, s_e, g_e, q_e = nc.tensor, nc.vector, nc.scalar, nc.gpsimd, nc.sync
        sc = Sched(nc, sems, {"T": t_e, "V": v_e, "S": s_e, "G": g_e, "Q": q_e},
                   pools)
        EM = sc.EM

        q_role = es.enter_context(q_e.register("q_role"))

        def role_dma(key, out0, out1, in_ap, reg):
            sem, tok = sc.dma_pre(key, [in_ap], [out0, out1])
            eng = sc.eng[key]
            with eng.If_eq(reg, 0):
                eng.dma_start(out=out0, in_=in_ap).then_inc(sem, 16)
            with eng.Else():
                eng.dma_start(out=out1, in_=in_ap).then_inc(sem, 16)
            sc.dma_post(key, tok, [in_ap], [out0, out1])

        with contextlib.ExitStack() as ph:
            p_tp = ph.enter_context(nc.psum_tensor("p0_tp", [128, 128], BF16))
            p_tpf = ph.enter_context(nc.psum_tensor("p0_tpf", [128, 128], F32))
            stage = ph.enter_context(nc.sbuf_tensor("st0", [128, 2048], F32))
            stageb = ph.enter_context(nc.sbuf_tensor("st0b", [128, 2048], BF16))

            EM(Q, [constv], [const_sb], dma=True)(q_e.dma_start(out=const_sb[:], in_=constv))
            EM(Q, [ident], [identf], dma=True)(q_e.dma_start(out=identf[:], in_=ident))
            sc.barrier()
            EM(G, [ident], [identb], dma=True)(g_e.dma_start(out=identb[:], in_=ident))
            EM(Q, [pmask], [pmask_sb], dma=True)(q_e.dma_start(out=pmask_sb[:], in_=pmask))
            EM(Q, [rolev], [role_sb], dma=True)(q_e.dma_start(out=role_sb[:], in_=rolev))
            EM(Q, [noncev], [nonce_sb], dma=True)(q_e.dma_start(out=nonce_sb[:], in_=noncev))
            EM(Q, [alog], [stage], dma=True)(q_e.dma_start(
                out=stage[:, 0:NDT * NN].rearrange("p (a n) -> p a n", a=NDT),
                in_=alog.rearrange("(a p) n -> p a n", p=128)))
            EM(S, [stage], [stage])(s_e.activation(
                out=stage[:, 0:NDT * NN], in_=stage[:, 0:NDT * NN], func=AF.Exp))
            EM(V, [stage], [A_bf])(v_e.tensor_scalar_mul(
                A_bf[:], stage[:, 0:NDT * NN], -1.0))
            EM(Q, [dvec], [D_sb], dma=True)(q_e.dma_start(
                out=D_sb[:].unsqueeze(2),
                in_=dvec.rearrange("(a p) -> p a", p=128).unsqueeze(2),
                allow_slow_non_contiguous=True))
            EM(Q, [convw], [convw_sb], dma=True)(q_e.dma_start(
                out=convw_sb[:].rearrange("p (a k) -> p a k", a=NDT),
                in_=convw.rearrange("(a p) k -> p a k", p=128)))
            EM(Q, [convb], [convb_sb], dma=True)(q_e.dma_start(
                out=convb_sb[:].unsqueeze(2),
                in_=convb.rearrange("(a p) -> p a", p=128).unsqueeze(2),
                allow_slow_non_contiguous=True))
            EM(Q, [dtb], [dtb_sb], dma=True)(q_e.dma_start(
                out=dtb_sb[:].unsqueeze(2),
                in_=dtb.rearrange("(a p) -> p a", p=128).unsqueeze(2),
                allow_slow_non_contiguous=True))
            EM(Q, [sgw], [sgw_sb], dma=True)(q_e.dma_start(
                out=sgw_sb[:].unsqueeze(2),
                in_=sgw.rearrange("(a p) o -> p a o", p=128),
                allow_slow_non_contiguous=True))
            EM(V, [], [carry_sb])(v_e.memset(carry_sb[:], 0.0))

            EM(G, [xpw], [stageb], dma=True)(g_e.dma_start(
                out=stageb[:], in_=xpw[0:128, :]))
            for d in range(NDT):
                EM(T, [stageb, identb], [p_tp])(t_e.transpose(
                    p_tp[:], stageb[:, d * 128:(d + 1) * 128], identb[:]))
                EM(V, [p_tp], [xpwT_sb])(v_e.tensor_copy(
                    xpwT_sb[:, d * 160:d * 160 + 128], p_tp[:]))
            EM(G, [xpw], [stageb], dma=True)(g_e.dma_start(
                out=stageb[0:32, :], in_=xpw[128:160, :]))
            for d in range(NDT):
                EM(T, [stageb, identb], [p_tp])(t_e.transpose(
                    p_tp[:, 0:32], stageb[0:32, d * 128:(d + 1) * 128],
                    identb[0:32, 0:32]))
                EM(V, [p_tp], [xpwT_sb])(v_e.tensor_copy(
                    xpwT_sb[:, d * 160 + 128:d * 160 + 160], p_tp[:, 0:32]))
            EM(G, [dtw], [stageb], dma=True)(g_e.dma_start(
                out=stageb[:].rearrange("p (a r) -> p a r", a=NDT),
                in_=dtw.rearrange("(a p) r -> p a r", p=128)))
            for d in range(NDT):
                EM(T, [stageb, identb], [p_tp])(t_e.transpose(
                    p_tp[:], stageb[:, d * 128:(d + 1) * 128], identb[:]))
                EM(V, [p_tp], [wdtT_sb])(v_e.tensor_copy(
                    wdtT_sb[:, d * 128:(d + 1) * 128], p_tp[:]))
            EM(Q, [rw], [stage], dma=True)(q_e.dma_start(out=stage[0:32, :], in_=rw))
            for d in range(NDT):
                EM(T, [stage, identf], [p_tpf])(t_e.transpose(
                    p_tpf[:, 0:32], stage[0:32, d * 128:(d + 1) * 128],
                    identf[0:32, 0:32]))
                EM(V, [p_tpf], [rwT_sb])(v_e.tensor_copy(
                    rwT_sb[:, d * NE:(d + 1) * NE], p_tpf[:, 0:32]))
            wotst = ph.enter_context(nc.sbuf_tensor("wotst", [128, 2048], BF16))
            for hmt in range(NDT):
                for ihalf in range(2):
                    EM(G, [wop], [stageb], dma=True)(g_e.dma_start(
                        out=stageb[:],
                        in_=wop[hmt * 128:(hmt + 1) * 128,
                                ihalf * 2048:(ihalf + 1) * 2048]))
                    for ib in range(16):
                        EM(T, [stageb, identb], [p_tp])(t_e.transpose(
                            p_tp[:], stageb[:, ib * 128:(ib + 1) * 128], identb[:]))
                        EM(V, [p_tp], [wotst])(v_e.tensor_copy(
                            wotst[:, ib * 128:(ib + 1) * 128], p_tp[:]))
                    EM(Q, [wotst], [wopT_sc], dma=True)(q_e.dma_start(
                        out=wopT_sc.rearrange("(a p) c -> p a c", p=128)
                        [:, ihalf * 16:(ihalf + 1) * 16,
                         hmt * 128:(hmt + 1) * 128],
                        in_=wotst[:].rearrange("p (a t) -> p a t", a=16)))
            sc.pre(Q, [role_sb], [])
            q_e.reg_load(q_role, role_sb[0:1, 0:1])

        sc.barrier()
        for e in range(NE):
            EM(G, [wd], [wd_bf], dma=True)(g_e.dma_start(
                out=wd_bf[e], in_=wd[e]))
        for fq in range(4):
            EM(G, [swd], [swd_bf], dma=True)(g_e.dma_start(
                out=swd_bf[fq * 1024:(fq + 1) * 1024, :],
                in_=swd[fq * 1024:(fq + 1) * 1024, :]))
        with contextlib.ExitStack() as ph:
            p_tp = ph.enter_context(nc.psum_tensor("pa_tp", [128, 128], BF16))
            p_tpf = ph.enter_context(nc.psum_tensor("pa_tpf", [128, 128], F32))
            p_rt = ph.enter_context(nc.psum_tensor("pa_rt", [128, NE], F32))
            p_sg = ph.enter_context(nc.psum_tensor("pa_sg", [128, 8], F32))
            p_tc = ph.enter_context(nc.psum_tensor("pa_tc", [32, 128], F32))
            xtile = ph.enter_context(nc.sbuf_tensor("a_xt", [128, H], F32))
            xm = ph.enter_context(nc.sbuf_tensor("a_xm", [128, H], F32))
            sq = ph.enter_context(nc.sbuf_tensor("a_sq", [128, H], F32))
            cols = ph.enter_context(nc.sbuf_tensor("a_cols", [128, 8], F32))
            xnb = ph.enter_context(nc.sbuf_tensor("a_xnb", [128, H], BF16))
            xts = ph.enter_context(nc.sbuf_tensor("a_xts", [128, NDT * 128], BF16))
            xnTf = ph.enter_context(nc.sbuf_tensor("a_xnTf", [128, NDT * 128], F32))
            prb = ph.enter_context(nc.sbuf_tensor("a_prb", [128, NE], F32))
            tk = ph.enter_context(nc.sbuf_tensor("a_tk", [128, 8], F32))
            comb = ph.enter_context(nc.sbuf_tensor("a_comb", [128, NE], F32))
            eqm = ph.enter_context(nc.sbuf_tensor("a_eqm", [128, NE], F32))
            combT = ph.enter_context(nc.sbuf_tensor("a_combT", [32, 128], F32))

            def ln_tile(src_ap, xsc_col, ffn_tile):
                EM(Q, [src_ap], [xtile], dma=True)(q_e.dma_start(out=xtile[:], in_=src_ap))
                EM(V, [xtile], [cols])(v_e.reduce_sum(cols[:, 0:1], xtile[:], axis=AX.X))
                EM(V, [cols], [cols])(v_e.tensor_scalar_mul(cols[:, 1:2], cols[:, 0:1], 1.0 / H))
                EM(V, [xtile, cols], [xm])(v_e.tensor_scalar(
                    out=xm[:], in0=xtile[:], scalar1=cols[:, 1:2], scalar2=None,
                    op0=OP.subtract))
                EM(S, [xm], [sq, cols])(s_e.activation(
                    out=sq[:], in_=xm[:], func=AF.Square, accum_out=cols[:, 2:3]))
                EM(S, [cols], [cols])(s_e.activation(
                    out=cols[:, 3:4], in_=cols[:, 2:3], func=AF.Sqrt,
                    scale=1.0 / H, bias=1e-5))
                EM(V, [cols], [cols])(v_e.reciprocal(cols[:, 4:5], cols[:, 3:4]))
                EM(V, [xm, cols], [xm])(v_e.tensor_scalar(
                    out=xm[:], in0=xm[:], scalar1=cols[:, 4:5], scalar2=None,
                    op0=OP.mult))
                EM(V, [xm], [xnb])(v_e.tensor_copy(xnb[:], xm[:]))
                for ht in range(NDT):
                    EM(T, [xnb, identb], [p_tp])(t_e.transpose(
                        p_tp[:], xnb[:, ht * 128:(ht + 1) * 128], identb[:]))
                    EM(V, [p_tp], [xts])(v_e.tensor_copy(
                        xts[:, ht * 128:(ht + 1) * 128], p_tp[:]))
                if xsc_col is not None:
                    EM(Q, [xts], [xt_sc], dma=True)(q_e.dma_start(
                        out=xt_sc.rearrange("(a p) c -> p a c", p=128)
                        [:, :, xsc_col:xsc_col + 128],
                        in_=xts[:].rearrange("p (a t) -> p a t", a=NDT)))
                if ffn_tile is None:
                    return
                ft = ffn_tile
                for ht in range(NDT):
                    EM(V, [xts], [xtown])(v_e.tensor_copy(
                        xtown[:, ht * TSUB + ft * 128:ht * TSUB + (ft + 1) * 128],
                        xts[:, ht * 128:(ht + 1) * 128]))
                for ht in range(NDT):
                    EM(T, [xm, identf], [p_tpf])(t_e.transpose(
                        p_tpf[:], xm[:, ht * 128:(ht + 1) * 128], identf[:]))
                    EM(V, [p_tpf], [xnTf])(v_e.tensor_copy(
                        xnTf[:, ht * 128:(ht + 1) * 128], p_tpf[:]))
                for ht in range(NDT):
                    EM(T, [xnTf, rwT_sb], [p_rt], inc=(ht == NDT - 1))(t_e.matmul(
                        p_rt[:], xnTf[:, ht * 128:(ht + 1) * 128],
                        rwT_sb[:, ht * NE:(ht + 1) * NE],
                        start=(ht == 0), stop=(ht == NDT - 1)))
                for ht in range(NDT):
                    EM(T, [xnTf, sgw_sb], [p_sg], inc=(ht == NDT - 1))(t_e.matmul(
                        p_sg[:, 0:1], xnTf[:, ht * 128:(ht + 1) * 128],
                        sgw_sb[:, ht:ht + 1],
                        start=(ht == 0), stop=(ht == NDT - 1)))
                EM(V, [p_rt], [cols])(v_e.reduce_max(cols[:, 5:6], p_rt[:], axis=AX.X))
                EM(V, [cols], [cols])(v_e.tensor_scalar_mul(cols[:, 6:7], cols[:, 5:6], -1.0))
                EM(S, [p_rt, cols], [prb, cols])(s_e.activation(
                    out=prb[:], in_=p_rt[:], func=AF.Exp, bias=cols[:, 6:7],
                    accum_out=cols[:, 7:8]))
                EM(V, [cols], [cols])(v_e.reciprocal(cols[:, 0:1], cols[:, 7:8]))
                EM(V, [prb, cols], [prb])(v_e.tensor_scalar(
                    out=prb[:], in0=prb[:], scalar1=cols[:, 0:1], scalar2=None,
                    op0=OP.mult))
                EM(V, [], [comb])(v_e.memset(comb[:], 0.0))
                EM(V, [], [tk])(v_e.memset(tk[:, 0:1], 0.0))
                for r in range(4):
                    EM(V, [prb], [tk])(v_e.reduce_max(tk[:, 1:2], prb[:], axis=AX.X))
                    EM(V, [prb, tk], [eqm])(v_e.tensor_scalar(
                        out=eqm[:], in0=prb[:], scalar1=tk[:, 1:2], scalar2=None,
                        op0=OP.is_equal))
                    EM(V, [tk], [tk])(v_e.tensor_add(tk[:, 0:1], tk[:, 0:1], tk[:, 1:2]))
                    EM(V, [eqm, tk, comb], [comb])(v_e.scalar_tensor_tensor(
                        out=comb[:], in0=eqm[:], scalar=tk[:, 1:2], in1=comb[:],
                        op0=OP.mult, op1=OP.add))
                    EM(V, [eqm, prb], [prb])(v_e.scalar_tensor_tensor(
                        out=prb[:], in0=eqm[:], scalar=-BIG, in1=prb[:],
                        op0=OP.mult, op1=OP.add))
                EM(V, [tk], [tk])(v_e.reciprocal(tk[:, 2:3], tk[:, 0:1]))
                EM(V, [comb, tk], [comb])(v_e.tensor_scalar(
                    out=comb[:], in0=comb[:], scalar1=tk[:, 2:3], scalar2=None,
                    op0=OP.mult))
                EM(T, [comb, identf], [p_tc])(t_e.transpose(p_tc[:], comb[:], identf[:]))
                EM(V, [p_tc], [combT])(v_e.tensor_copy(combT[:], p_tc[:]))
                EM(Q, [combT], [combT_sc], dma=True)(q_e.dma_start(
                    out=combT_sc[:, ft * 128:(ft + 1) * 128], in_=combT[:]))
                EM(S, [p_sg], [cols])(s_e.activation(
                    out=cols[:, 1:2], in_=p_sg[:, 0:1], func=AF.Sigmoid))
                EM(Q, [cols], [sgT_sc], dma=True)(q_e.dma_start(
                    out=sgT_sc[0:1, ft * 128:(ft + 1) * 128].rearrange("o t -> t o"),
                    in_=cols[:, 1:2], allow_slow_non_contiguous=True))

            for tt in range(TPAD // 128):
                ln_tile(x_in[tt * 128:(tt + 1) * 128, :], tt * 128, None)
            for ft in range(TSUB // 128):
                ln_tile(x_ffn[ft * 128:(ft + 1) * 128, :], None, ft)

        sc.barrier()
        if PHASES != "toA":
         with contextlib.ExitStack() as ph:
            p_m = [ph.enter_context(nc.psum_tensor(f"pc{i}", [128, 512], F32))
                   for i in range(2)]
            p_tp = ph.enter_context(nc.psum_tensor("pc_tp", [128, 128], BF16))
            wstage = ph.enter_context(nc.sbuf_tensor("c_ws", [128, 4 * H], BF16))
            lhsT = ph.enter_context(nc.sbuf_tensor("c_lh", [128, 64 * 128], BF16))
            xtt = [ph.enter_context(nc.sbuf_tensor(f"c_xt{i}", [128, NDT * 512], BF16))
                   for i in range(2)]
            xcat = ph.enter_context(nc.sbuf_tensor("c_xc", [128, 4 * 516], F32))
            cacc = ph.enter_context(nc.sbuf_tensor("c_ca", [128, 512], F32))
            xbf = ph.enter_context(nc.sbuf_tensor("c_xb", [128, 512], BF16))
            gbf = ph.enter_context(nc.sbuf_tensor("c_gb", [128, 512], BF16))

            TBL = [(i * 512, 512) for i in range(16)] + [(8192, 128)]

            def load_lhsT(wsrc, ob):
                EM(G, [wsrc], [wstage], dma=True)(g_e.dma_start(
                    out=wstage[:].rearrange("p (a h) -> p a h", a=4),
                    in_=wsrc[ob * 512:(ob + 1) * 512, :]
                    .rearrange("(a p) h -> p a h", p=128)))
                for ot in range(4):
                    for ht in range(NDT):
                        EM(T, [wstage, identb], [p_tp])(t_e.transpose(
                            p_tp[:], wstage[:, ot * H + ht * 128:ot * H + (ht + 1) * 128],
                            identb[:]))
                        EM(V, [p_tp], [lhsT])(v_e.tensor_copy(
                            lhsT[:, (ot * 16 + ht) * 128:(ot * 16 + ht + 1) * 128],
                            p_tp[:]))

            for ob in range(4):
                load_lhsT(wip_x, ob)
                EM(V, [], [xcat])(v_e.memset(xcat[:], 0.0))
                for tbi, (c0, nt) in enumerate(TBL):
                    xb = xtt[tbi % 2]
                    EM(Q, [xt_sc], [xb], dma=True)(q_e.dma_start(
                        out=xb[:].rearrange("p (a t) -> p a t", a=NDT)[:, :, 0:nt],
                        in_=xt_sc.rearrange("(a p) c -> p a c", p=128)[:, :, c0:c0 + nt]))
                    for ot in range(4):
                        ota = ob * 4 + ot
                        pm = p_m[ot % 2]
                        for ht in range(NDT):
                            EM(T, [lhsT, xb], [pm], inc=(ht == NDT - 1))(t_e.matmul(
                                pm[:, 0:nt],
                                lhsT[:, (ot * 16 + ht) * 128:(ot * 16 + ht + 1) * 128],
                                xb[:, ht * 512:ht * 512 + nt],
                                start=(ht == 0), stop=(ht == NDT - 1)))
                        EM(V, [pm], [xcat])(v_e.tensor_copy(
                            xcat[:, ot * 516 + 3:ot * 516 + 3 + nt], pm[:, 0:nt]))
                        EM(V, [xcat, convw_sb], [cacc])(v_e.tensor_scalar(
                            out=cacc[:, 0:nt], in0=xcat[:, ot * 516:ot * 516 + nt],
                            scalar1=convw_sb[:, ota * 4:ota * 4 + 1], scalar2=None,
                            op0=OP.mult))
                        for k in range(1, 4):
                            EM(V, [xcat, convw_sb, cacc], [cacc])(v_e.scalar_tensor_tensor(
                                out=cacc[:, 0:nt],
                                in0=xcat[:, ot * 516 + k:ot * 516 + k + nt],
                                scalar=convw_sb[:, ota * 4 + k:ota * 4 + k + 1],
                                in1=cacc[:, 0:nt], op0=OP.mult, op1=OP.add))
                        EM(V, [xcat], [xcat])(v_e.tensor_copy(
                            xcat[:, ot * 516:ot * 516 + 3],
                            xcat[:, ot * 516 + nt:ot * 516 + nt + 3]))
                        EM(S, [cacc, convb_sb], [xbf])(s_e.activation(
                            out=xbf[:, 0:nt], in_=cacc[:, 0:nt], func=AF.Silu,
                            bias=convb_sb[:, ota:ota + 1]))
                        if c0 + nt > 128:
                            s0 = max(0, 128 - c0)
                            EM(Q, [xbf], [x_sc], dma=True)(q_e.dma_start(
                                out=x_sc[ota * 128:(ota + 1) * 128,
                                         c0 + s0 - 128:c0 + nt - 128],
                                in_=xbf[:, s0:nt]))

            for ob in range(4):
                load_lhsT(wip_g, ob)
                for tbi in range(8):
                    c0 = 4224 + tbi * 512
                    xb = xtt[tbi % 2]
                    EM(Q, [xt_sc], [xb], dma=True)(q_e.dma_start(
                        out=xb[:].rearrange("p (a t) -> p a t", a=NDT)[:, :, 0:512],
                        in_=xt_sc.rearrange("(a p) c -> p a c", p=128)[:, :, c0:c0 + 512]))
                    for ot in range(4):
                        ota = ob * 4 + ot
                        pm = p_m[ot % 2]
                        for ht in range(NDT):
                            EM(T, [lhsT, xb], [pm], inc=(ht == NDT - 1))(t_e.matmul(
                                pm[:],
                                lhsT[:, (ot * 16 + ht) * 128:(ot * 16 + ht + 1) * 128],
                                xb[:, ht * 512:(ht + 1) * 512],
                                start=(ht == 0), stop=(ht == NDT - 1)))
                        EM(S, [pm], [gbf])(s_e.activation(
                            out=gbf[:], in_=pm[:], func=AF.Silu))
                        EM(Q, [gbf], [gate_sc], dma=True)(q_e.dma_start(
                            out=gate_sc[ota * 128:(ota + 1) * 128,
                                        c0 - 4224:c0 - 4224 + 512],
                            in_=gbf[:]))

        sc.barrier()
        if PHASES not in ("toA", "toC"):
         with contextlib.ExitStack() as ph:
            p_xp = ph.enter_context(nc.psum_tensor("pe_xp", [128, 512], F32))
            p_xp2 = ph.enter_context(nc.psum_tensor("pe_xp2", [32, 512], F32))
            p_dt = [ph.enter_context(nc.psum_tensor(f"pe_dt{i}", [128, 512], F32))
                    for i in range(2)]
            exi = ph.enter_context(nc.sbuf_tensor("e_xi", [128, NDT * 512], BF16))
            dtin = ph.enter_context(nc.sbuf_tensor("e_dtin", [128, 512], BF16))
            bcrows = ph.enter_context(nc.sbuf_tensor("e_bc", [32, 512], BF16))
            dtt1 = ph.enter_context(nc.sbuf_tensor("e_t1", [128, 512], F32))
            dtt2 = ph.enter_context(nc.sbuf_tensor("e_t2", [128, 512], F32))
            dtbf = ph.enter_context(nc.sbuf_tensor("e_dtb", [128, 512], BF16))

            for tb in range(16):
                EM(Q, [x_sc], [exi], dma=True)(q_e.dma_start(
                    out=exi[:].rearrange("p (a t) -> p a t", a=NDT),
                    in_=x_sc.rearrange("(a p) c -> p a c", p=128)
                    [:, :, tb * 512:(tb + 1) * 512]))
                for d in range(NDT):
                    EM(T, [xpwT_sb, exi], [p_xp], inc=(d == NDT - 1))(t_e.matmul(
                        p_xp[:], xpwT_sb[:, d * 160:d * 160 + 128],
                        exi[:, d * 512:(d + 1) * 512],
                        start=(d == 0), stop=(d == NDT - 1)))
                for d in range(NDT):
                    EM(T, [xpwT_sb, exi], [p_xp2], inc=(d == NDT - 1))(t_e.matmul(
                        p_xp2[:], xpwT_sb[:, d * 160 + 128:d * 160 + 160],
                        exi[:, d * 512:(d + 1) * 512],
                        start=(d == 0), stop=(d == NDT - 1)))
                EM(V, [p_xp2], [bcrows])(v_e.tensor_copy(bcrows[:], p_xp2[:]))
                EM(Q, [bcrows], [bc_sc], dma=True)(q_e.dma_start(
                    out=bc_sc[:, tb * 512:(tb + 1) * 512], in_=bcrows[:]))
                EM(V, [p_xp], [dtin])(v_e.tensor_copy(dtin[:], p_xp[:]))
                for d in range(NDT):
                    pm = p_dt[d % 2]
                    EM(T, [wdtT_sb, dtin], [pm])(t_e.matmul(
                        pm[:], wdtT_sb[:, d * 128:(d + 1) * 128], dtin[:],
                        start=True, stop=True))
                    EM(S, [pm, dtb_sb], [dtt1])(s_e.activation(
                        out=dtt1[:], in_=pm[:], func=AF.Relu,
                        bias=dtb_sb[:, d:d + 1]))
                    EM(S, [pm, dtb_sb], [dtt2])(s_e.activation(
                        out=dtt2[:], in_=pm[:], func=AF.Abs,
                        bias=dtb_sb[:, d:d + 1]))
                    EM(S, [dtt2], [dtt2])(s_e.activation(
                        out=dtt2[:], in_=dtt2[:], func=AF.Exp, scale=-1.0))
                    EM(S, [dtt2], [dtt2])(s_e.activation(
                        out=dtt2[:], in_=dtt2[:], func=AF.Ln, bias=1.0))
                    EM(V, [dtt1, dtt2], [dtbf])(v_e.tensor_add(dtbf[:], dtt1[:], dtt2[:]))
                    EM(Q, [dtbf], [dt_sc], dma=True)(q_e.dma_start(
                        out=dt_sc[d * 128:(d + 1) * 128, tb * 512:(tb + 1) * 512],
                        in_=dtbf[:]))

        sc.barrier()
        wstack.close()
        HALF = NCH // 2
        if PHASES not in ("toE", "toA", "toC"):
         with contextlib.ExitStack() as ph:
            p_g = [ph.enter_context(nc.psum_tensor(f"pg{i}", [128, 1024], F32))
                   for i in range(2)]
            p_u = [ph.enter_context(nc.psum_tensor(f"pu{i}", [128, 1024], F32))
                   for i in range(2)]
            wgt = [ph.enter_context(nc.sbuf_tensor(f"g_wg{i}", [128, NDT * 128], BF16))
                   for i in range(2)]
            wut = [ph.enter_context(nc.sbuf_tensor(f"g_wu{i}", [128, NDT * 128], BF16))
                   for i in range(2)]
            sgl = [ph.enter_context(nc.sbuf_tensor(f"g_sgl{i}", [128, 1024], BF16))
                   for i in range(2)]
            ucb = [ph.enter_context(nc.sbuf_tensor(f"g_ucb{i}", [128, 1024], BF16))
                   for i in range(2)]
            actst = [ph.enter_context(nc.sbuf_tensor(f"g_act{i}", [128, TSUB], BF16))
                     for i in range(2)]
            cbbc = [ph.enter_context(nc.sbuf_tensor(f"g_cb{i}", [128, TSUB], BF16))
                    for i in range(2)]

            def g_unit(bi, wsrc_g, wsrc_u, col0, cbc, dst, dst_r0):
                EM(G, [wsrc_g], [wgt[bi]], dma=True)(g_e.dma_start(
                    out=wgt[bi][:].rearrange("p (a f) -> p a f", a=NDT),
                    in_=wsrc_g[:, col0:col0 + 128]
                    .rearrange("(a p) f -> p a f", p=128)))
                EM(G, [wsrc_u], [wut[bi]], dma=True)(g_e.dma_start(
                    out=wut[bi][:].rearrange("p (a f) -> p a f", a=NDT),
                    in_=wsrc_u[:, col0:col0 + 128]
                    .rearrange("(a p) f -> p a f", p=128)))
                for tb in range(2):
                    pg, pu = p_g[tb], p_u[tb]
                    for hf in range(2):
                        c0 = tb * 1024 + hf * 512
                        for ht in range(NDT):
                            EM(T, [wgt[bi], xtown], [pg],
                               inc=(ht == NDT - 1))(t_e.matmul(
                                pg[:, hf * 512:(hf + 1) * 512],
                                wgt[bi][:, ht * 128:(ht + 1) * 128],
                                xtown[:, ht * TSUB + c0:ht * TSUB + c0 + 512],
                                start=(ht == 0), stop=(ht == NDT - 1)))
                        for ht in range(NDT):
                            EM(T, [wut[bi], xtown], [pu],
                               inc=(ht == NDT - 1))(t_e.matmul(
                                pu[:, hf * 512:(hf + 1) * 512],
                                wut[bi][:, ht * 128:(ht + 1) * 128],
                                xtown[:, ht * TSUB + c0:ht * TSUB + c0 + 512],
                                start=(ht == 0), stop=(ht == NDT - 1)))
                    EM(S, [pg], [sgl[tb]])(s_e.activation(
                        out=sgl[tb][:], in_=pg[:], func=AF.Silu))
                    EM(V, [pu, cbc], [ucb[tb]])(v_e.tensor_mul(
                        ucb[tb][:], pu[:], cbc[:, tb * 1024:(tb + 1) * 1024]))
                    EM(V, [sgl[tb], ucb[tb]], [actst[bi]])(v_e.tensor_mul(
                        actst[bi][:, tb * 1024:(tb + 1) * 1024],
                        sgl[tb][:], ucb[tb][:]))
                EM(Q, [actst[bi]], [dst], dma=True)(q_e.dma_start(
                    out=dst[dst_r0:dst_r0 + 128, :], in_=actst[bi][:]))

            def gen_G():
                EM(G, [sgT_sc], [sgbc], dma=True)(g_e.dma_start(
                    out=sgbc[:].rearrange("p (o t) -> p o t", o=1),
                    in_=sgT_sc.unsqueeze(0).broadcast_to((128, 1, TSUB))))
                for e in range(NE):
                    cb = cbbc[e % 2]
                    EM(G, [combT_sc], [cb], dma=True)(g_e.dma_start(
                        out=cb[:].rearrange("p (o t) -> p o t", o=1),
                        in_=combT_sc[e:e + 1, :].unsqueeze(0)
                        .broadcast_to((128, 1, TSUB))))
                    for ftl in range(6):
                        g_unit((e * 6 + ftl) % 2, wg[e], wu[e], ftl * 128,
                               cb, act_sc[e], ftl * 128)
                        yield
                for fsl in range(32):
                    g_unit(fsl % 2, swg, swu, fsl * 128,
                           sgbc, mid_sc, fsl * 128)
                    yield

            sbuf = nc.sbuf_tensor
            Bbc = ph.enter_context(sbuf("f_Bbc", [128, NN * TC], BF16))
            Cbc = ph.enter_context(sbuf("f_Cbc", [128, NN * TC], BF16))
            fb = dict(
                dtp=[ph.enter_context(sbuf(f"f_dt{i}", [128, TC], BF16))
                     for i in range(2)],
                xp=[ph.enter_context(sbuf(f"f_x{i}", [128, TC], BF16))
                    for i in range(2)],
                gt=[ph.enter_context(sbuf(f"f_g{i}", [128, TC], BF16))
                    for i in range(2)],
                dtx=ph.enter_context(sbuf("f_dtx", [128, TC], BF16)),
                adb=ph.enter_context(sbuf("f_adb", [128, NN * TC], BF16)),
                dAf=ph.enter_context(sbuf("f_dAf", [128, NN * TC], BF16)),
                stf=ph.enter_context(sbuf("f_stf", [128, NN * TC], BF16)),
                inj=ph.enter_context(sbuf("f_inj", [128, NN], F32)),
                y1=ph.enter_context(sbuf("f_y1", [128, TC], F32)),
                y2=ph.enter_context(sbuf("f_y2", [128, TC], BF16)),
                ygb=ph.enter_context(sbuf("f_ygb", [128, TC], BF16)),
            )

            def load_piece(ch, d, slot):
                bb = fb
                t0 = ch * TC
                dtp, xp, gt = bb["dtp"][slot], bb["xp"][slot], bb["gt"][slot]
                EM(S, [dt_sc], [dtp], dma=True)(s_e.dma_start(
                    out=dtp[:], in_=dt_sc[d * 128:(d + 1) * 128, t0:t0 + TC]))
                EM(S, [x_sc], [xp], dma=True)(s_e.dma_start(
                    out=xp[:], in_=x_sc[d * 128:(d + 1) * 128, t0:t0 + TC]))
                if ch >= HALF:
                    EM(S, [gate_sc], [gt], dma=True)(s_e.dma_start(
                        out=gt[:], in_=gate_sc[d * 128:(d + 1) * 128,
                                               t0 - TOWN:t0 - TOWN + TC]))

            def piece(ch, d, slot):
                bb = fb
                ee = v_e
                engk = V
                t0 = ch * TC
                dtp, xp, gt = bb["dtp"][slot], bb["xp"][slot], bb["gt"][slot]
                av = A_bf[:, d * NN:(d + 1) * NN].unsqueeze(2).broadcast_to((128, NN, TC))
                dtv = dtp[:].unsqueeze(1).broadcast_to((128, NN, TC))
                adbv = bb["adb"][:].rearrange("p (n t) -> p n t", n=NN)
                dAv = bb["dAf"][:].rearrange("p (n t) -> p n t", n=NN)
                EM(G, [A_bf, dtp], [bb["adb"]])(g_e.tensor_tensor(
                    adbv, av, dtv, OP.mult))
                EM(S, [bb["adb"]], [bb["dAf"]])(s_e.activation(
                    out=bb["dAf"][:], in_=bb["adb"][:], func=AF.Exp))
                EM(engk, [dtp, xp], [bb["dtx"]])(ee.tensor_mul(
                    bb["dtx"][:], dtp[:], xp[:]))
                dtxv = bb["dtx"][:].unsqueeze(1).broadcast_to((128, NN, TC))
                EM(G, [bb["dtx"], Bbc], [bb["adb"]])(g_e.tensor_tensor(
                    adbv, dtxv, Bbc[:].rearrange("p (n t) -> p n t", n=NN), OP.mult))
                cslice = carry_sb[:, d * NN:(d + 1) * NN]
                EM(engk, [bb["dAf"], carry_sb], [bb["inj"]])(ee.tensor_tensor(
                    bb["inj"][:].unsqueeze(2),
                    dAv[:, :, 0:1], cslice.unsqueeze(2), OP.mult))
                EM(engk, [bb["inj"], bb["adb"]], [bb["adb"]])(ee.tensor_tensor(
                    adbv[:, :, 0:1], bb["inj"][:].unsqueeze(2),
                    adbv[:, :, 0:1], OP.add))
                EM(engk, [], [bb["dAf"]])(ee.memset(dAv[:, :, 0:1], 0.0))
                EM(engk, [bb["dAf"], bb["adb"]], [bb["stf"]])(ee.tensor_tensor_scan(
                    out=bb["stf"][:], data0=bb["dAf"][:], data1=bb["adb"][:],
                    initial=0.0, op0=OP.mult, op1=OP.add))
                stv = bb["stf"][:].rearrange("p (n t) -> p n t", n=NN)
                EM(engk, [bb["stf"]], [carry_sb])(ee.tensor_copy(
                    cslice.unsqueeze(2), stv[:, :, TC - 1:TC]))
                if ch == HALF - 1:
                    EM(engk, [carry_sb, pmask_sb], [carry_sb])(ee.tensor_scalar(
                        out=cslice, in0=cslice, scalar1=pmask_sb[:, 0:1],
                        scalar2=None, op0=OP.mult))
                if ch < HALF:
                    return
                EM(G, [bb["stf"], Cbc], [bb["adb"]])(g_e.tensor_tensor(
                    bb["adb"][:], bb["stf"][:], Cbc[:], OP.mult))
                w = NN * TC // 2
                while w >= TC:
                    EM(engk, [bb["adb"]], [bb["adb"]])(ee.tensor_add(
                        bb["adb"][:, 0:w], bb["adb"][:, 0:w],
                        bb["adb"][:, w:2 * w]))
                    w //= 2
                EM(engk, [xp, D_sb, bb["adb"]], [bb["y1"]])(ee.scalar_tensor_tensor(
                    out=bb["y1"][:], in0=xp[:], scalar=D_sb[:, d:d + 1],
                    in1=bb["adb"][:, 0:TC], op0=OP.mult, op1=OP.add))
                EM(engk, [bb["y1"], gt], [bb["ygb"]])(ee.tensor_mul(
                    bb["ygb"][:], bb["y1"][:], gt[:]))
                role_dma(Q, sh_ygg[0][d * 128:(d + 1) * 128, t0 - TOWN:t0 - TOWN + TC],
                         sh_ygg[1][d * 128:(d + 1) * 128, t0 - TOWN:t0 - TOWN + TC],
                         bb["ygb"][:], q_role)

            def gen_F():
                seq = [(ch, d) for ch in range(NCH) for d in range(NDT)]
                load_piece(*seq[0], 0)
                for i, (ch, d) in enumerate(seq):
                    if d == 0:
                        t0 = ch * TC
                        EM(S, [bc_sc], [Bbc], dma=True)(s_e.dma_start(
                            out=Bbc[:].rearrange("p (n t) -> p n t", n=NN),
                            in_=bc_sc[0:16, t0:t0 + TC].unsqueeze(0)
                            .broadcast_to((128, 16, TC))))
                        EM(S, [bc_sc], [Cbc], dma=True)(s_e.dma_start(
                            out=Cbc[:].rearrange("p (n t) -> p n t", n=NN),
                            in_=bc_sc[16:32, t0:t0 + TC].unsqueeze(0)
                            .broadcast_to((128, 16, TC))))
                    if i + 1 < len(seq):
                        load_piece(*seq[i + 1], (i + 1) % 2)
                    piece(ch, d, i % 2)
                    yield

            gens = [gen_G()]
            if PHASES != "toG":
                gens.append(gen_F())
            alive = list(gens)
            while alive:
                for gen in list(alive):
                    try:
                        next(gen)
                    except StopIteration:
                        alive.remove(gen)

        xtstack.close()
        if PHASES in ("toF", "toG", "toE", "toA", "toC"):
            sc.wait_all_dma(Q)
            globals()["_LAST_CNT"] = dict(sc.cnt)
            return
        sc.barrier()
        do_poll = PHASES not in ("nopoll",)
        if do_poll:
            sem_f, tok_f = sc.dma_pre(Q, [sh_ygg[0], sh_ygg[1], nonce_sb],
                                      [sh_flag])
            sc.wait_all_dma(Q, keys=[Q])
            with q_e.If_eq(q_role, 0):
                q_e.dma_start(out=sh_flag[0:1, 0:1],
                              in_=nonce_sb[0:1, 0:1]).then_inc(sem_f, 16)
            with q_e.Else():
                q_e.dma_start(out=sh_flag[0:1, 1:2],
                              in_=nonce_sb[0:1, 0:1]).then_inc(sem_f, 16)
            sc.dma_post(Q, tok_f, [nonce_sb], [sh_flag])

        with contextlib.ExitStack() as ph:
          if do_poll:
            r_a = ph.enter_context(g_e.register("r_a"))
            r_b = ph.enter_context(g_e.register("r_b"))
            r_2n = ph.enter_context(g_e.register("r_2n"))
            r_c = ph.enter_context(g_e.register("r_c"))
            r_it = ph.enter_context(g_e.register("r_it"))
            sc.pre(G, [nonce_sb], [pollw])
            g_e.reg_load(r_a, nonce_sb[0:1, 0:1])
            g_e.reg_add(r_2n, r_a, r_a)
            g_e.reg_mov(r_c, 0)
            g_e.reg_mov(r_it, 0)
            POOL = mybir.EngineType.Pool
            nc.br("poll_s", engines=[POOL])
            with nc.body("poll_s", valid_engines=[POOL]):
                g_e.dma_start(out=pollw[0:1, 0:2], in_=sh_flag[0:1, 0:2]) \
                   .then_inc(psem, 16)
                g_e.reg_add(r_c, r_c, 16)
                g_e.wait_ge(psem, r_c)
                g_e.reg_load(r_a, pollw[0:1, 0:1])
                g_e.reg_load(r_b, pollw[0:1, 1:2])
                g_e.reg_add(r_a, r_a, r_b)
                g_e.reg_sub(r_a, r_a, r_2n)
                g_e.reg_add(r_it, r_it, 1)
                g_e.br_eq(r_a, 0, on_true="poll_e", on_false="poll_c")
            with nc.body("poll_c", valid_engines=[POOL]):
                g_e.br_lt(r_it, MAXPOLL, on_true="poll_s", on_false="poll_f")
            with nc.body("poll_f", valid_engines=[POOL]):
                g_e.halt()
                g_e.br("poll_e")
            nc.switch_bb("poll_e")
        g_e.nop().then_inc(sems[G], 1)
        sc.cnt[G] += 1
        sc.idx[G] += 1
        sc.mark_write(sh_ygg[0], G)
        sc.mark_write(sh_ygg[1], G)

        sc.barrier()
        ENTRIES = ([("moe", e, fk) for e in range(NE) for fk in range(6)]
                   + [("sh", fs, 0) for fs in range(32)]
                   + [("op", ik, 0) for ik in range(32)])
        NSUB = 16
        with contextlib.ExitStack() as ph:
            p_mg = [ph.enter_context(nc.psum_tensor(f"pj{i}", [128, 512], F32))
                    for i in range(8)]
            lsl = [ph.enter_context(nc.sbuf_tensor(f"j_l{i}", [128, NSUB * 4 * 128],
                                                   BF16)) for i in range(2)]
            rsl = [ph.enter_context(nc.sbuf_tensor(f"j_r{i}", [128, NSUB * 1024],
                                                   BF16)) for i in range(2)]
            xres = ph.enter_context(nc.sbuf_tensor("j_xr", [128, 4 * 1024], F32))
            outs = ph.enter_context(nc.sbuf_tensor("j_o", [128, 4 * 1024], F32))

            nsubs = len(ENTRIES) // NSUB
            for hmh in range(2):
                for tg in range(4):
                    EM(Q, [x_ffn], [xres], dma=True)(q_e.dma_start(
                        out=xres[:].rearrange("p (a c) -> p a c", a=4),
                        in_=x_ffn.rearrange("(a p) c -> p a c", p=128)
                        [:, tg * 4:(tg + 1) * 4, hmh * 1024:(hmh + 1) * 1024]))
                    for sci in range(nsubs):
                        bi = sci % 2
                        ls, rs = lsl[bi], rsl[bi]
                        for k in range(NSUB):
                            kind, a1, a2 = ENTRIES[sci * NSUB + k]
                            tcols = slice(tg * 512, (tg + 1) * 512)
                            if kind == "moe":
                                EM(Q, [act_sc], [ls], dma=True)(q_e.dma_start(
                                    out=ls[:, k * 512:(k + 1) * 512],
                                    in_=act_sc[a1][a2 * 128:(a2 + 1) * 128, tcols]))
                                EM(G, [wd_bf], [rs], dma=True)(g_e.dma_start(
                                    out=rs[:, k * 1024:(k + 1) * 1024],
                                    in_=wd_bf[a1][a2 * 128:(a2 + 1) * 128,
                                               hmh * 1024:(hmh + 1) * 1024]))
                            elif kind == "sh":
                                EM(Q, [mid_sc], [ls], dma=True)(q_e.dma_start(
                                    out=ls[:, k * 512:(k + 1) * 512],
                                    in_=mid_sc[a1 * 128:(a1 + 1) * 128, tcols]))
                                EM(G, [swd_bf], [rs], dma=True)(g_e.dma_start(
                                    out=rs[:, k * 1024:(k + 1) * 1024],
                                    in_=swd_bf[a1 * 128:(a1 + 1) * 128,
                                            hmh * 1024:(hmh + 1) * 1024]))
                            else:
                                slab = sh_ygg[0] if a1 < 16 else sh_ygg[1]
                                rr = (a1 % 16)
                                sem_s, tok_s = sc.dma_pre(Q, [slab], [ls])
                                with q_e.If_eq(q_role, 0):
                                    q_e.dma_start(
                                        out=ls[:, k * 512:(k + 1) * 512],
                                        in_=slab[rr * 128:(rr + 1) * 128,
                                                 tg * 512:(tg + 1) * 512]) \
                                        .then_inc(sem_s, 16)
                                with q_e.Else():
                                    q_e.dma_start(
                                        out=ls[:, k * 512:(k + 1) * 512],
                                        in_=slab[rr * 128:(rr + 1) * 128,
                                                 TSUB + tg * 512:
                                                 TSUB + (tg + 1) * 512]) \
                                        .then_inc(sem_s, 16)
                                sc.dma_post(Q, tok_s, [slab], [ls])
                                EM(Q, [wopT_sc], [rs], dma=True)(q_e.dma_start(
                                    out=rs[:, k * 1024:(k + 1) * 1024],
                                    in_=wopT_sc[a1 * 128:(a1 + 1) * 128,
                                                hmh * 1024:(hmh + 1) * 1024]))
                        for tt in range(4):
                            for hc in range(2):
                                pm = p_mg[2 * tt + hc]
                                for k in range(NSUB):
                                    EM(T, [ls, rs], [pm],
                                       inc=(k == NSUB - 1))(t_e.matmul(
                                        pm[:],
                                        ls[:, k * 512 + tt * 128:
                                           k * 512 + (tt + 1) * 128],
                                        rs[:, k * 1024 + hc * 512:
                                           k * 1024 + (hc + 1) * 512],
                                        start=(sci == 0 and k == 0),
                                        stop=(sci == nsubs - 1
                                              and k == NSUB - 1)))
                    for tt in range(4):
                        for hc in range(2):
                            pm = p_mg[2 * tt + hc]
                            EM(V, [pm, xres], [outs])(v_e.tensor_add(
                                outs[:, tt * 1024 + hc * 512:
                                     tt * 1024 + (hc + 1) * 512],
                                pm[:],
                                xres[:, tt * 1024 + hc * 512:
                                     tt * 1024 + (hc + 1) * 512]))
                    EM(Q, [outs], [out], dma=True)(q_e.dma_start(
                        out=out.rearrange("(a p) c -> p a c", p=128)
                        [:, tg * 4:(tg + 1) * 4, hmh * 1024:(hmh + 1) * 1024],
                        in_=outs[:].rearrange("p (a c) -> p a c", a=4)))

        sc.wait_all_dma(Q)

    es.close()
    return nc


_NC_CACHE = None


def kernel(L0_attn_hidden_states, L0_attn_in_proj_w, L0_attn_conv_w,
           L0_attn_conv_b, L0_attn_x_proj_weight, L0_attn_dt_proj_weight,
           L0_attn_dt_proj_bias, L0_attn_A_log, L0_attn_D, L0_attn_out_proj_w,
           L0_ffn_router_weight, L0_ffn_w_gate, L0_ffn_w_up, L0_ffn_w_down,
           L0_ffn_sw_gate, L0_ffn_sw_up, L0_ffn_sw_down, L0_ffn_sgate_w):
    global _NC_CACHE
    f32 = np.float32
    x = np.asarray(L0_attn_hidden_states, f32)
    cc = np.ascontiguousarray
    nonce = int(np.random.randint(1, 2 ** 30))
    in_maps = []
    for c in range(8):
        b, S, j = c // 4, (c // 2) % 2, c % 2
        x_in = np.zeros((TPAD, H), f32)
        if S == 1:
            x_in[128:4224] = x[b, 0:4096]
        x_in[4224:8320] = x[b, S * 4096:(S + 1) * 4096]
        x_ffn = cc(x[b, S * 4096 + j * TSUB:S * 4096 + (j + 1) * TSUB])
        wip = np.asarray(L0_attn_in_proj_w, f32)[j * 4096:(j + 1) * 4096]
        m = dict(
            x_in=x_in, x_ffn=x_ffn,
            wip_x=cc(wip[0::2]), wip_g=cc(wip[1::2]),
            convw=cc(np.asarray(L0_attn_conv_w, f32)[j * H:(j + 1) * H]),
            convb=cc(np.asarray(L0_attn_conv_b, f32)[j * H:(j + 1) * H]),
            xpw=cc(np.asarray(L0_attn_x_proj_weight, f32)[j]),
            dtw=cc(np.asarray(L0_attn_dt_proj_weight, f32)[j]),
            dtb=cc(np.asarray(L0_attn_dt_proj_bias, f32)[j]),
            alog=cc(np.asarray(L0_attn_A_log, f32)[j]),
            dvec=cc(np.asarray(L0_attn_D, f32)[j]),
            wop=cc(np.asarray(L0_attn_out_proj_w, f32)),
            rw=cc(np.asarray(L0_ffn_router_weight, f32)),
            wg=cc(np.asarray(L0_ffn_w_gate, f32)),
            wu=cc(np.asarray(L0_ffn_w_up, f32)),
            wd=cc(np.asarray(L0_ffn_w_down, f32)),
            swg=cc(np.asarray(L0_ffn_sw_gate, f32)),
            swu=cc(np.asarray(L0_ffn_sw_up, f32)),
            swd=cc(np.asarray(L0_ffn_sw_down, f32)),
            sgw=cc(np.asarray(L0_ffn_sgate_w, f32)),
            ident=np.eye(128, dtype=f32),
            constv=np.tile(np.array([[1e-5, 1.0, -1.0, 1.0 / H, 0.0, 0.0, 0.0, 0.0]], f32), (128, 1)),
            pmask=np.full((128, 1), float(S), f32),
            rolev=np.full((1, 4), j, np.int32),
            noncev=np.full((1, 4), nonce, np.int32),
        )
        in_maps.append(m)
    if _NC_CACHE is None:
        _NC_CACHE = build_nc()
    res = run_bass_kernel_spmd(_NC_CACHE, in_maps, core_ids=list(range(8)))
    out_full = np.zeros((2, 8192, H), f32)
    for c in range(8):
        b, S, j = c // 4, (c // 2) % 2, c % 2
        r0 = S * 4096 + j * TSUB
        out_full[b, r0:r0 + TSUB, :] = res.results[c]["out"]
    return out_full


# version 07
# speedup vs baseline: 1.6710x; 1.3350x over previous
import contextlib
import numpy as np
import concourse.bass as bass
import concourse.mybir as mybir
from concourse.bass_utils import run_bass_kernel_spmd

F32 = mybir.dt.float32
BF16 = mybir.dt.bfloat16
I32 = mybir.dt.int32
AF = mybir.ActivationFunctionType
OP = mybir.AluOpType
AX = mybir.AxisListType

H = 2048
TPAD = 8320
NT = 8192
TOWN = 4096
TSUB = 2048
NE = 32
FF = 768
FS = 4096
NN = 16
TC = 512
NDT = 16
NCH = NT // TC
GP_DT = set()
BIG = 1.0e30
MAXPOLL = 500000
PHASES = "all"
DBG_LOCAL = False
_LAST_CNT = {}


class Sched:

    def __init__(self, nc, sems, eng, pools):
        self.nc = nc
        self.sems = sems
        self.eng = eng
        self.pool = pools
        self.cnt = {k: 0 for k in sems}
        self.idx = {k: 0 for k in sems}
        self.pu = {k: [0] * len(v) for k, v in pools.items()}
        self.pi = {k: 0 for k in pools}
        self.writes = {}
        self.reads = {}
        self.waited = {}

    @staticmethod
    def _nm(a):
        if a is None or isinstance(a, str):
            return a
        t = getattr(a, "tensor", None)
        if t is not None:
            return t.name
        return a.name

    def _sem(self, tok):
        if isinstance(tok, tuple):
            return self.pool[tok[0]][tok[1]]
        return self.sems[tok]

    def _wait(self, wk, tok, v):
        if v <= 0:
            return
        if self.waited.get((wk, tok), 0) >= v:
            return
        self.eng[wk].wait_ge(self._sem(tok), v)
        self.waited[(wk, tok)] = v

    def pre(self, key, reads, writes, dma=False):
        for t in map(self._nm, reads):
            for tok, (wc, wdma, widx, wkey) in self.writes.get(t, {}).items():
                if wdma:
                    self._wait(key, tok, wc)
                elif wkey == key:
                    if key != 'T':
                        self._wait(key, key, wc)
                else:
                    self._wait(key, tok, wc)
        for t in map(self._nm, writes):
            for rtok, (rc, rdma, ridx, rkey) in self.reads.get(t, {}).items():
                if rdma:
                    self._wait(key, rtok, rc)
                elif rkey != key:
                    self._wait(key, rtok, rc)
                elif key != 'T':
                    self._wait(key, key, rc)
            for tok, (wc, wdma, widx, wkey) in self.writes.get(t, {}).items():
                if wdma:
                    self._wait(key, tok, wc)
                elif wkey != key:
                    self._wait(key, tok, wc)
                elif key != 'T':
                    self._wait(key, key, wc)
        if dma:
            i = self.pi[key]
            self.pi[key] = (i + 1) % len(self.pool[key])
            if self.pu[key][i] > 0:
                self._wait(key, (key, i), 16 * self.pu[key][i])
            self._pend = (key, i)
            return self._pend
        return None

    def post(self, key, inst, reads, writes, dma, inc, tok=None):
        self.idx[key] += 1
        if dma:
            assert inc
            k, i = tok if tok is not None else self._pend
            if inst is not None:
                inst.then_inc(self.pool[k][i], 16)
            self.pu[k][i] += 1
            val = 16 * self.pu[k][i]
            wtok = (k, i)
        else:
            if inc:
                inst.then_inc(self.sems[key], 1)
                self.cnt[key] += 1
            val = self.cnt[key] if inc else self.cnt[key] + 1
            wtok = key
        for t in map(self._nm, reads):
            self.reads.setdefault(t, {})[wtok] = (val, dma, self.idx[key], key)
        for t in map(self._nm, writes):
            self.writes.setdefault(t, {})[wtok] = (val, dma, self.idx[key], key)
        return inst

    def EM(self, key, reads=(), writes=(), dma=False, inc=True):
        self.pre(key, reads, writes, dma=dma)
        return lambda inst: self.post(key, inst, reads, writes, dma, inc)

    def dma_pre(self, key, reads, writes):
        tok = self.pre(key, reads, writes, dma=True)
        return self.pool[tok[0]][tok[1]], tok

    def dma_post(self, key, tok, reads, writes):
        self.post(key, None, reads, writes, True, True, tok=tok)

    def mark_write(self, name, key):
        self.writes.setdefault(self._nm(name), {})[key] = (
            self.cnt[key], False, self.idx[key], key)

    def wait_all_dma(self, wk, keys=None):
        for pk in (keys if keys is not None else self.pool):
            for i, u in enumerate(self.pu[pk]):
                self._wait(wk, (pk, i), 16 * u)

    def barrier(self):
        for wk in self.sems:
            for tk in self.sems:
                if wk != tk:
                    self._wait(wk, tk, self.cnt[tk])
            self.wait_all_dma(wk)


def build_nc():
    nc = bass.Bass("TRN2", target_bir_lowering=False, debug=False)
    dram = nc.dram_tensor

    x_in = dram("x_in", [TPAD, H], F32, kind="ExternalInput").ap()
    x_ffn = dram("x_ffn", [TSUB, H], F32, kind="ExternalInput").ap()
    wip_x = dram("wip_x", [H, H], F32, kind="ExternalInput").ap()
    wip_g = dram("wip_g", [H, H], F32, kind="ExternalInput").ap()
    convw = dram("convw", [H, 4], F32, kind="ExternalInput").ap()
    convb = dram("convb", [H], F32, kind="ExternalInput").ap()
    xpw = dram("xpw", [160, H], F32, kind="ExternalInput").ap()
    dtw = dram("dtw", [H, 128], F32, kind="ExternalInput").ap()
    dtb = dram("dtb", [H], F32, kind="ExternalInput").ap()
    alog = dram("alog", [H, NN], F32, kind="ExternalInput").ap()
    dvec = dram("dvec", [H], F32, kind="ExternalInput").ap()
    wop = dram("wop", [H, 2 * H], F32, kind="ExternalInput").ap()
    rw = dram("rw", [NE, H], F32, kind="ExternalInput").ap()
    wg = dram("wg", [NE, H, FF], F32, kind="ExternalInput").ap()
    wu = dram("wu", [NE, H, FF], F32, kind="ExternalInput").ap()
    wd = dram("wd", [NE, FF, H], F32, kind="ExternalInput").ap()
    swg = dram("swg", [H, FS], F32, kind="ExternalInput").ap()
    swu = dram("swu", [H, FS], F32, kind="ExternalInput").ap()
    swd = dram("swd", [FS, H], F32, kind="ExternalInput").ap()
    sgw = dram("sgw", [H, 1], F32, kind="ExternalInput").ap()
    ident = dram("ident", [128, 128], F32, kind="ExternalInput").ap()
    pmask = dram("pmask", [128, 1], F32, kind="ExternalInput").ap()
    rolev = dram("rolev", [1, 4], I32, kind="ExternalInput").ap()
    noncev = dram("noncev", [1, 4], I32, kind="ExternalInput").ap()
    constv = dram("constv", [128, 8], F32, kind="ExternalInput").ap()
    out = dram("out", [TSUB, H], F32, kind="ExternalOutput").ap()

    xt_sc = dram("xt_sc", [H, TPAD], BF16).ap()
    x_sc = dram("x_sc", [H, NT], BF16).ap()
    dt_sc = dram("dt_sc", [H, NT], BF16).ap()
    gate_sc = dram("gate_sc", [H, TOWN], BF16).ap()
    act_sc = dram("act_sc", [NE, FF, TSUB], BF16).ap()
    mid_sc = dram("mid_sc", [FS, TSUB], BF16).ap()
    wopT_sc = dram("wopT_sc", [2 * H, H], BF16).ap()
    bc_sc = dram("bc_sc", [32, NT], BF16).ap()
    wd_bf = dram("wd_bf", [NE, FF, H], BF16).ap()
    swd_bf = dram("swd_bf", [FS, H], BF16).ap()
    combT_sc = dram("combT_sc", [NE, TSUB], F32).ap()
    sgT_sc = dram("sgT_sc", [1, TSUB], F32).ap()
    _shared = {} if DBG_LOCAL else {"addr_space": "Shared"}
    sh_ygg = dram("sh_ygg", [2, H, TOWN], BF16, **_shared).ap()
    sh_flag = dram("sh_flag", [1, 16], I32, **_shared).ap()

    es = contextlib.ExitStack()
    sb = lambda n, s, d: es.enter_context(nc.sbuf_tensor(n, s, d))

    identf = sb("identf", [128, 128], F32)
    identb = sb("identb", [128, 128], BF16)
    A_f32 = sb("A_f32", [128, NDT * NN], F32)
    D_sb = sb("D_sb", [128, NDT], F32)
    convw_sb = sb("convw_sb", [128, NDT * 4], F32)
    convb_sb = sb("convb_sb", [128, NDT], F32)
    dtb_sb = sb("dtb_sb", [128, NDT], F32)
    pmask_sb = sb("pmask_sb", [128, 1], F32)
    role_sb = sb("role_sb", [1, 4], I32)
    nonce_sb = sb("nonce_sb", [1, 4], I32)
    pollw = sb("pollw", [1, 8], I32)
    sgw_sb = sb("sgw_sb", [128, NDT], F32)
    carry_sb = sb("carry_sb", [128, NDT * NN], F32)
    sgbc = sb("sgbc", [128, TSUB], BF16)
    const_sb = sb("const_sb", [128, 8], F32)
    nc.const_aps.register(F32, 1e-5, const_sb[:, 0:1])
    nc.const_aps.register(F32, 1.0, const_sb[:, 1:2])
    nc.const_aps.register(F32, -1.0, const_sb[:, 2:3])
    nc.const_aps.register(F32, 1.0 / H, const_sb[:, 3:4])
    nc.const_aps.register(F32, 0.0, const_sb[:, 4:5])

    xtstack = contextlib.ExitStack()
    xtown = xtstack.enter_context(nc.sbuf_tensor("xtown", [128, NDT * TSUB], BF16))
    wstack = contextlib.ExitStack()
    wsb = lambda n, s, d: wstack.enter_context(nc.sbuf_tensor(n, s, d))
    xpwT_sb = wsb("xpwT_sb", [128, NDT * 160], BF16)
    wdtT_sb = wsb("wdtT_sb", [128, NDT * 128], BF16)
    rwT_sb = wsb("rwT_sb", [128, NDT * NE], F32)

    sems = {k: es.enter_context(nc.semaphore("sem_" + k)) for k in
            ("T", "V", "S", "G", "Q")}
    pools = {
        "Q": [es.enter_context(nc.semaphore(f"pq{i}")) for i in range(24)],
        "G": [es.enter_context(nc.semaphore(f"pg{i}")) for i in range(12)],
        "S": [es.enter_context(nc.semaphore(f"ps{i}")) for i in range(8)],
    }
    psem = es.enter_context(nc.semaphore("sem_poll"))
    block = es.enter_context(nc.Block())

    @block.sync
    def _(sync):
        T, V, S, G, Q = "T", "V", "S", "G", "Q"
        t_e, v_e, s_e, g_e, q_e = nc.tensor, nc.vector, nc.scalar, nc.gpsimd, nc.sync
        sc = Sched(nc, sems, {"T": t_e, "V": v_e, "S": s_e, "G": g_e, "Q": q_e},
                   pools)
        EM = sc.EM

        q_role = es.enter_context(q_e.register("q_role"))

        def role_dma(key, out0, out1, in_ap, reg):
            sem, tok = sc.dma_pre(key, [in_ap], [out0, out1])
            eng = sc.eng[key]
            with eng.If_eq(reg, 0):
                eng.dma_start(out=out0, in_=in_ap).then_inc(sem, 16)
            with eng.Else():
                eng.dma_start(out=out1, in_=in_ap).then_inc(sem, 16)
            sc.dma_post(key, tok, [in_ap], [out0, out1])

        with contextlib.ExitStack() as ph:
            p_tp = ph.enter_context(nc.psum_tensor("p0_tp", [128, 128], BF16))
            p_tpf = ph.enter_context(nc.psum_tensor("p0_tpf", [128, 128], F32))
            stage = ph.enter_context(nc.sbuf_tensor("st0", [128, 2048], F32))
            stageb = ph.enter_context(nc.sbuf_tensor("st0b", [128, 2048], BF16))

            EM(Q, [constv], [const_sb], dma=True)(q_e.dma_start(out=const_sb[:], in_=constv))
            EM(Q, [ident], [identf], dma=True)(q_e.dma_start(out=identf[:], in_=ident))
            sc.barrier()
            EM(G, [ident], [identb], dma=True)(g_e.dma_start(out=identb[:], in_=ident))
            EM(Q, [pmask], [pmask_sb], dma=True)(q_e.dma_start(out=pmask_sb[:], in_=pmask))
            EM(Q, [rolev], [role_sb], dma=True)(q_e.dma_start(out=role_sb[:], in_=rolev))
            EM(Q, [noncev], [nonce_sb], dma=True)(q_e.dma_start(out=nonce_sb[:], in_=noncev))
            EM(Q, [alog], [stage], dma=True)(q_e.dma_start(
                out=stage[:, 0:NDT * NN].rearrange("p (a n) -> p a n", a=NDT),
                in_=alog.rearrange("(a p) n -> p a n", p=128)))
            EM(S, [stage], [stage])(s_e.activation(
                out=stage[:, 0:NDT * NN], in_=stage[:, 0:NDT * NN], func=AF.Exp))
            EM(V, [stage], [A_f32])(v_e.tensor_scalar_mul(
                A_f32[:], stage[:, 0:NDT * NN], -1.0))
            EM(Q, [dvec], [D_sb], dma=True)(q_e.dma_start(
                out=D_sb[:].unsqueeze(2),
                in_=dvec.rearrange("(a p) -> p a", p=128).unsqueeze(2),
                allow_slow_non_contiguous=True))
            EM(Q, [convw], [convw_sb], dma=True)(q_e.dma_start(
                out=convw_sb[:].rearrange("p (a k) -> p a k", a=NDT),
                in_=convw.rearrange("(a p) k -> p a k", p=128)))
            EM(Q, [convb], [convb_sb], dma=True)(q_e.dma_start(
                out=convb_sb[:].unsqueeze(2),
                in_=convb.rearrange("(a p) -> p a", p=128).unsqueeze(2),
                allow_slow_non_contiguous=True))
            EM(Q, [dtb], [dtb_sb], dma=True)(q_e.dma_start(
                out=dtb_sb[:].unsqueeze(2),
                in_=dtb.rearrange("(a p) -> p a", p=128).unsqueeze(2),
                allow_slow_non_contiguous=True))
            EM(Q, [sgw], [sgw_sb], dma=True)(q_e.dma_start(
                out=sgw_sb[:].unsqueeze(2),
                in_=sgw.rearrange("(a p) o -> p a o", p=128),
                allow_slow_non_contiguous=True))
            EM(V, [], [carry_sb])(v_e.memset(carry_sb[:], 0.0))

            EM(G, [xpw], [stageb], dma=True)(g_e.dma_start(
                out=stageb[:], in_=xpw[0:128, :]))
            for d in range(NDT):
                EM(T, [stageb, identb], [p_tp])(t_e.transpose(
                    p_tp[:], stageb[:, d * 128:(d + 1) * 128], identb[:]))
                EM(V, [p_tp], [xpwT_sb])(v_e.tensor_copy(
                    xpwT_sb[:, d * 160:d * 160 + 128], p_tp[:]))
            EM(G, [xpw], [stageb], dma=True)(g_e.dma_start(
                out=stageb[0:32, :], in_=xpw[128:160, :]))
            for d in range(NDT):
                EM(T, [stageb, identb], [p_tp])(t_e.transpose(
                    p_tp[:, 0:32], stageb[0:32, d * 128:(d + 1) * 128],
                    identb[0:32, 0:32]))
                EM(V, [p_tp], [xpwT_sb])(v_e.tensor_copy(
                    xpwT_sb[:, d * 160 + 128:d * 160 + 160], p_tp[:, 0:32]))
            EM(G, [dtw], [stageb], dma=True)(g_e.dma_start(
                out=stageb[:].rearrange("p (a r) -> p a r", a=NDT),
                in_=dtw.rearrange("(a p) r -> p a r", p=128)))
            for d in range(NDT):
                EM(T, [stageb, identb], [p_tp])(t_e.transpose(
                    p_tp[:], stageb[:, d * 128:(d + 1) * 128], identb[:]))
                EM(V, [p_tp], [wdtT_sb])(v_e.tensor_copy(
                    wdtT_sb[:, d * 128:(d + 1) * 128], p_tp[:]))
            EM(Q, [rw], [stage], dma=True)(q_e.dma_start(out=stage[0:32, :], in_=rw))
            for d in range(NDT):
                EM(T, [stage, identf], [p_tpf])(t_e.transpose(
                    p_tpf[:, 0:32], stage[0:32, d * 128:(d + 1) * 128],
                    identf[0:32, 0:32]))
                EM(V, [p_tpf], [rwT_sb])(v_e.tensor_copy(
                    rwT_sb[:, d * NE:(d + 1) * NE], p_tpf[:, 0:32]))
            wotst = ph.enter_context(nc.sbuf_tensor("wotst", [128, 2048], BF16))
            for hmt in range(NDT):
                for ihalf in range(2):
                    EM(G, [wop], [stageb], dma=True)(g_e.dma_start(
                        out=stageb[:],
                        in_=wop[hmt * 128:(hmt + 1) * 128,
                                ihalf * 2048:(ihalf + 1) * 2048]))
                    for ib in range(16):
                        EM(T, [stageb, identb], [p_tp])(t_e.transpose(
                            p_tp[:], stageb[:, ib * 128:(ib + 1) * 128], identb[:]))
                        EM(V, [p_tp], [wotst])(v_e.tensor_copy(
                            wotst[:, ib * 128:(ib + 1) * 128], p_tp[:]))
                    EM(Q, [wotst], [wopT_sc], dma=True)(q_e.dma_start(
                        out=wopT_sc.rearrange("(a p) c -> p a c", p=128)
                        [:, ihalf * 16:(ihalf + 1) * 16,
                         hmt * 128:(hmt + 1) * 128],
                        in_=wotst[:].rearrange("p (a t) -> p a t", a=16)))
            sc.pre(Q, [role_sb], [])
            q_e.reg_load(q_role, role_sb[0:1, 0:1])

        sc.barrier()
        for e in range(NE):
            EM(G, [wd], [wd_bf], dma=True)(g_e.dma_start(
                out=wd_bf[e], in_=wd[e]))
        for fq in range(4):
            EM(G, [swd], [swd_bf], dma=True)(g_e.dma_start(
                out=swd_bf[fq * 1024:(fq + 1) * 1024, :],
                in_=swd[fq * 1024:(fq + 1) * 1024, :]))
        with contextlib.ExitStack() as ph:
            p_tp = ph.enter_context(nc.psum_tensor("pa_tp", [128, 128], BF16))
            p_tpf = ph.enter_context(nc.psum_tensor("pa_tpf", [128, 128], F32))
            p_rt = ph.enter_context(nc.psum_tensor("pa_rt", [128, NE], F32))
            p_sg = ph.enter_context(nc.psum_tensor("pa_sg", [128, 8], F32))
            p_tc = ph.enter_context(nc.psum_tensor("pa_tc", [32, 128], F32))
            xtile = ph.enter_context(nc.sbuf_tensor("a_xt", [128, H], F32))
            xm = ph.enter_context(nc.sbuf_tensor("a_xm", [128, H], F32))
            sq = ph.enter_context(nc.sbuf_tensor("a_sq", [128, H], F32))
            cols = ph.enter_context(nc.sbuf_tensor("a_cols", [128, 8], F32))
            xnb = ph.enter_context(nc.sbuf_tensor("a_xnb", [128, H], BF16))
            xts = ph.enter_context(nc.sbuf_tensor("a_xts", [128, NDT * 128], BF16))
            xnTf = ph.enter_context(nc.sbuf_tensor("a_xnTf", [128, NDT * 128], F32))
            prb = ph.enter_context(nc.sbuf_tensor("a_prb", [128, NE], F32))
            tk = ph.enter_context(nc.sbuf_tensor("a_tk", [128, 8], F32))
            comb = ph.enter_context(nc.sbuf_tensor("a_comb", [128, NE], F32))
            eqm = ph.enter_context(nc.sbuf_tensor("a_eqm", [128, NE], F32))
            combT = ph.enter_context(nc.sbuf_tensor("a_combT", [32, 128], F32))

            def ln_tile(src_ap, xsc_col, ffn_tile):
                EM(Q, [src_ap], [xtile], dma=True)(q_e.dma_start(out=xtile[:], in_=src_ap))
                EM(V, [xtile], [cols])(v_e.reduce_sum(cols[:, 0:1], xtile[:], axis=AX.X))
                EM(V, [cols], [cols])(v_e.tensor_scalar_mul(cols[:, 1:2], cols[:, 0:1], 1.0 / H))
                EM(V, [xtile, cols], [xm])(v_e.tensor_scalar(
                    out=xm[:], in0=xtile[:], scalar1=cols[:, 1:2], scalar2=None,
                    op0=OP.subtract))
                EM(S, [xm], [sq, cols])(s_e.activation(
                    out=sq[:], in_=xm[:], func=AF.Square, accum_out=cols[:, 2:3]))
                EM(S, [cols], [cols])(s_e.activation(
                    out=cols[:, 3:4], in_=cols[:, 2:3], func=AF.Sqrt,
                    scale=1.0 / H, bias=1e-5))
                EM(V, [cols], [cols])(v_e.reciprocal(cols[:, 4:5], cols[:, 3:4]))
                EM(V, [xm, cols], [xm])(v_e.tensor_scalar(
                    out=xm[:], in0=xm[:], scalar1=cols[:, 4:5], scalar2=None,
                    op0=OP.mult))
                EM(V, [xm], [xnb])(v_e.tensor_copy(xnb[:], xm[:]))
                for ht in range(NDT):
                    EM(T, [xnb, identb], [p_tp])(t_e.transpose(
                        p_tp[:], xnb[:, ht * 128:(ht + 1) * 128], identb[:]))
                    EM(V, [p_tp], [xts])(v_e.tensor_copy(
                        xts[:, ht * 128:(ht + 1) * 128], p_tp[:]))
                if xsc_col is not None:
                    EM(Q, [xts], [xt_sc], dma=True)(q_e.dma_start(
                        out=xt_sc.rearrange("(a p) c -> p a c", p=128)
                        [:, :, xsc_col:xsc_col + 128],
                        in_=xts[:].rearrange("p (a t) -> p a t", a=NDT)))
                if ffn_tile is None:
                    return
                ft = ffn_tile
                for ht in range(NDT):
                    EM(V, [xts], [xtown])(v_e.tensor_copy(
                        xtown[:, ht * TSUB + ft * 128:ht * TSUB + (ft + 1) * 128],
                        xts[:, ht * 128:(ht + 1) * 128]))
                for ht in range(NDT):
                    EM(T, [xm, identf], [p_tpf])(t_e.transpose(
                        p_tpf[:], xm[:, ht * 128:(ht + 1) * 128], identf[:]))
                    EM(V, [p_tpf], [xnTf])(v_e.tensor_copy(
                        xnTf[:, ht * 128:(ht + 1) * 128], p_tpf[:]))
                for ht in range(NDT):
                    EM(T, [xnTf, rwT_sb], [p_rt], inc=(ht == NDT - 1))(t_e.matmul(
                        p_rt[:], xnTf[:, ht * 128:(ht + 1) * 128],
                        rwT_sb[:, ht * NE:(ht + 1) * NE],
                        start=(ht == 0), stop=(ht == NDT - 1)))
                for ht in range(NDT):
                    EM(T, [xnTf, sgw_sb], [p_sg], inc=(ht == NDT - 1))(t_e.matmul(
                        p_sg[:, 0:1], xnTf[:, ht * 128:(ht + 1) * 128],
                        sgw_sb[:, ht:ht + 1],
                        start=(ht == 0), stop=(ht == NDT - 1)))
                EM(V, [p_rt], [cols])(v_e.reduce_max(cols[:, 5:6], p_rt[:], axis=AX.X))
                EM(V, [cols], [cols])(v_e.tensor_scalar_mul(cols[:, 6:7], cols[:, 5:6], -1.0))
                EM(S, [p_rt, cols], [prb, cols])(s_e.activation(
                    out=prb[:], in_=p_rt[:], func=AF.Exp, bias=cols[:, 6:7],
                    accum_out=cols[:, 7:8]))
                EM(V, [cols], [cols])(v_e.reciprocal(cols[:, 0:1], cols[:, 7:8]))
                EM(V, [prb, cols], [prb])(v_e.tensor_scalar(
                    out=prb[:], in0=prb[:], scalar1=cols[:, 0:1], scalar2=None,
                    op0=OP.mult))
                EM(V, [], [comb])(v_e.memset(comb[:], 0.0))
                EM(V, [], [tk])(v_e.memset(tk[:, 0:1], 0.0))
                for r in range(4):
                    EM(V, [prb], [tk])(v_e.reduce_max(tk[:, 1:2], prb[:], axis=AX.X))
                    EM(V, [prb, tk], [eqm])(v_e.tensor_scalar(
                        out=eqm[:], in0=prb[:], scalar1=tk[:, 1:2], scalar2=None,
                        op0=OP.is_equal))
                    EM(V, [tk], [tk])(v_e.tensor_add(tk[:, 0:1], tk[:, 0:1], tk[:, 1:2]))
                    EM(V, [eqm, tk, comb], [comb])(v_e.scalar_tensor_tensor(
                        out=comb[:], in0=eqm[:], scalar=tk[:, 1:2], in1=comb[:],
                        op0=OP.mult, op1=OP.add))
                    EM(V, [eqm, prb], [prb])(v_e.scalar_tensor_tensor(
                        out=prb[:], in0=eqm[:], scalar=-BIG, in1=prb[:],
                        op0=OP.mult, op1=OP.add))
                EM(V, [tk], [tk])(v_e.reciprocal(tk[:, 2:3], tk[:, 0:1]))
                EM(V, [comb, tk], [comb])(v_e.tensor_scalar(
                    out=comb[:], in0=comb[:], scalar1=tk[:, 2:3], scalar2=None,
                    op0=OP.mult))
                EM(T, [comb, identf], [p_tc])(t_e.transpose(p_tc[:], comb[:], identf[:]))
                EM(V, [p_tc], [combT])(v_e.tensor_copy(combT[:], p_tc[:]))
                EM(Q, [combT], [combT_sc], dma=True)(q_e.dma_start(
                    out=combT_sc[:, ft * 128:(ft + 1) * 128], in_=combT[:]))
                EM(S, [p_sg], [cols])(s_e.activation(
                    out=cols[:, 1:2], in_=p_sg[:, 0:1], func=AF.Sigmoid))
                EM(Q, [cols], [sgT_sc], dma=True)(q_e.dma_start(
                    out=sgT_sc[0:1, ft * 128:(ft + 1) * 128].rearrange("o t -> t o"),
                    in_=cols[:, 1:2], allow_slow_non_contiguous=True))

            for tt in range(TPAD // 128):
                ln_tile(x_in[tt * 128:(tt + 1) * 128, :], tt * 128, None)
            for ft in range(TSUB // 128):
                ln_tile(x_ffn[ft * 128:(ft + 1) * 128, :], None, ft)

        sc.barrier()
        if PHASES != "toA":
         with contextlib.ExitStack() as ph:
            p_m = [ph.enter_context(nc.psum_tensor(f"pc{i}", [128, 512], F32))
                   for i in range(2)]
            p_tp = ph.enter_context(nc.psum_tensor("pc_tp", [128, 128], BF16))
            wstage = ph.enter_context(nc.sbuf_tensor("c_ws", [128, 4 * H], BF16))
            lhsT = ph.enter_context(nc.sbuf_tensor("c_lh", [128, 64 * 128], BF16))
            xtt = [ph.enter_context(nc.sbuf_tensor(f"c_xt{i}", [128, NDT * 512], BF16))
                   for i in range(2)]
            xcat = ph.enter_context(nc.sbuf_tensor("c_xc", [128, 4 * 516], F32))
            cacc = ph.enter_context(nc.sbuf_tensor("c_ca", [128, 512], F32))
            xbf = ph.enter_context(nc.sbuf_tensor("c_xb", [128, 512], BF16))
            gbf = ph.enter_context(nc.sbuf_tensor("c_gb", [128, 512], BF16))

            TBL = [(i * 512, 512) for i in range(16)] + [(8192, 128)]

            def load_lhsT(wsrc, ob):
                EM(G, [wsrc], [wstage], dma=True)(g_e.dma_start(
                    out=wstage[:].rearrange("p (a h) -> p a h", a=4),
                    in_=wsrc[ob * 512:(ob + 1) * 512, :]
                    .rearrange("(a p) h -> p a h", p=128)))
                for ot in range(4):
                    for ht in range(NDT):
                        EM(T, [wstage, identb], [p_tp])(t_e.transpose(
                            p_tp[:], wstage[:, ot * H + ht * 128:ot * H + (ht + 1) * 128],
                            identb[:]))
                        EM(V, [p_tp], [lhsT])(v_e.tensor_copy(
                            lhsT[:, (ot * 16 + ht) * 128:(ot * 16 + ht + 1) * 128],
                            p_tp[:]))

            for ob in range(4):
                load_lhsT(wip_x, ob)
                EM(V, [], [xcat])(v_e.memset(xcat[:], 0.0))
                for tbi, (c0, nt) in enumerate(TBL):
                    xb = xtt[tbi % 2]
                    EM(Q, [xt_sc], [xb], dma=True)(q_e.dma_start(
                        out=xb[:].rearrange("p (a t) -> p a t", a=NDT)[:, :, 0:nt],
                        in_=xt_sc.rearrange("(a p) c -> p a c", p=128)[:, :, c0:c0 + nt]))
                    for ot in range(4):
                        ota = ob * 4 + ot
                        pm = p_m[ot % 2]
                        for ht in range(NDT):
                            EM(T, [lhsT, xb], [pm], inc=(ht == NDT - 1))(t_e.matmul(
                                pm[:, 0:nt],
                                lhsT[:, (ot * 16 + ht) * 128:(ot * 16 + ht + 1) * 128],
                                xb[:, ht * 512:ht * 512 + nt],
                                start=(ht == 0), stop=(ht == NDT - 1)))
                        EM(V, [pm], [xcat])(v_e.tensor_copy(
                            xcat[:, ot * 516 + 3:ot * 516 + 3 + nt], pm[:, 0:nt]))
                        EM(V, [xcat, convw_sb], [cacc])(v_e.tensor_scalar(
                            out=cacc[:, 0:nt], in0=xcat[:, ot * 516:ot * 516 + nt],
                            scalar1=convw_sb[:, ota * 4:ota * 4 + 1], scalar2=None,
                            op0=OP.mult))
                        for k in range(1, 4):
                            EM(V, [xcat, convw_sb, cacc], [cacc])(v_e.scalar_tensor_tensor(
                                out=cacc[:, 0:nt],
                                in0=xcat[:, ot * 516 + k:ot * 516 + k + nt],
                                scalar=convw_sb[:, ota * 4 + k:ota * 4 + k + 1],
                                in1=cacc[:, 0:nt], op0=OP.mult, op1=OP.add))
                        EM(V, [xcat], [xcat])(v_e.tensor_copy(
                            xcat[:, ot * 516:ot * 516 + 3],
                            xcat[:, ot * 516 + nt:ot * 516 + nt + 3]))
                        EM(S, [cacc, convb_sb], [xbf])(s_e.activation(
                            out=xbf[:, 0:nt], in_=cacc[:, 0:nt], func=AF.Silu,
                            bias=convb_sb[:, ota:ota + 1]))
                        if c0 + nt > 128:
                            s0 = max(0, 128 - c0)
                            EM(Q, [xbf], [x_sc], dma=True)(q_e.dma_start(
                                out=x_sc[ota * 128:(ota + 1) * 128,
                                         c0 + s0 - 128:c0 + nt - 128],
                                in_=xbf[:, s0:nt]))

            for ob in range(4):
                load_lhsT(wip_g, ob)
                for tbi in range(8):
                    c0 = 4224 + tbi * 512
                    xb = xtt[tbi % 2]
                    EM(Q, [xt_sc], [xb], dma=True)(q_e.dma_start(
                        out=xb[:].rearrange("p (a t) -> p a t", a=NDT)[:, :, 0:512],
                        in_=xt_sc.rearrange("(a p) c -> p a c", p=128)[:, :, c0:c0 + 512]))
                    for ot in range(4):
                        ota = ob * 4 + ot
                        pm = p_m[ot % 2]
                        for ht in range(NDT):
                            EM(T, [lhsT, xb], [pm], inc=(ht == NDT - 1))(t_e.matmul(
                                pm[:],
                                lhsT[:, (ot * 16 + ht) * 128:(ot * 16 + ht + 1) * 128],
                                xb[:, ht * 512:(ht + 1) * 512],
                                start=(ht == 0), stop=(ht == NDT - 1)))
                        EM(S, [pm], [gbf])(s_e.activation(
                            out=gbf[:], in_=pm[:], func=AF.Silu))
                        EM(Q, [gbf], [gate_sc], dma=True)(q_e.dma_start(
                            out=gate_sc[ota * 128:(ota + 1) * 128,
                                        c0 - 4224:c0 - 4224 + 512],
                            in_=gbf[:]))

        sc.barrier()
        if PHASES not in ("toA", "toC"):
         with contextlib.ExitStack() as ph:
            p_xp = ph.enter_context(nc.psum_tensor("pe_xp", [128, 512], F32))
            p_xp2 = ph.enter_context(nc.psum_tensor("pe_xp2", [32, 512], F32))
            p_dt = [ph.enter_context(nc.psum_tensor(f"pe_dt{i}", [128, 512], F32))
                    for i in range(2)]
            exi = ph.enter_context(nc.sbuf_tensor("e_xi", [128, NDT * 512], BF16))
            dtin = ph.enter_context(nc.sbuf_tensor("e_dtin", [128, 512], BF16))
            bcrows = ph.enter_context(nc.sbuf_tensor("e_bc", [32, 512], BF16))
            dtt1 = ph.enter_context(nc.sbuf_tensor("e_t1", [128, 512], F32))
            dtt2 = ph.enter_context(nc.sbuf_tensor("e_t2", [128, 512], F32))
            dtbf = ph.enter_context(nc.sbuf_tensor("e_dtb", [128, 512], BF16))

            for tb in range(16):
                EM(Q, [x_sc], [exi], dma=True)(q_e.dma_start(
                    out=exi[:].rearrange("p (a t) -> p a t", a=NDT),
                    in_=x_sc.rearrange("(a p) c -> p a c", p=128)
                    [:, :, tb * 512:(tb + 1) * 512]))
                for d in range(NDT):
                    EM(T, [xpwT_sb, exi], [p_xp], inc=(d == NDT - 1))(t_e.matmul(
                        p_xp[:], xpwT_sb[:, d * 160:d * 160 + 128],
                        exi[:, d * 512:(d + 1) * 512],
                        start=(d == 0), stop=(d == NDT - 1)))
                for d in range(NDT):
                    EM(T, [xpwT_sb, exi], [p_xp2], inc=(d == NDT - 1))(t_e.matmul(
                        p_xp2[:], xpwT_sb[:, d * 160 + 128:d * 160 + 160],
                        exi[:, d * 512:(d + 1) * 512],
                        start=(d == 0), stop=(d == NDT - 1)))
                EM(V, [p_xp2], [bcrows])(v_e.tensor_copy(bcrows[:], p_xp2[:]))
                EM(Q, [bcrows], [bc_sc], dma=True)(q_e.dma_start(
                    out=bc_sc[:, tb * 512:(tb + 1) * 512], in_=bcrows[:]))
                EM(V, [p_xp], [dtin])(v_e.tensor_copy(dtin[:], p_xp[:]))
                for d in range(NDT):
                    pm = p_dt[d % 2]
                    EM(T, [wdtT_sb, dtin], [pm])(t_e.matmul(
                        pm[:], wdtT_sb[:, d * 128:(d + 1) * 128], dtin[:],
                        start=True, stop=True))
                    EM(S, [pm, dtb_sb], [dtt1])(s_e.activation(
                        out=dtt1[:], in_=pm[:], func=AF.Relu,
                        bias=dtb_sb[:, d:d + 1]))
                    EM(S, [pm, dtb_sb], [dtt2])(s_e.activation(
                        out=dtt2[:], in_=pm[:], func=AF.Abs,
                        bias=dtb_sb[:, d:d + 1]))
                    EM(S, [dtt2], [dtt2])(s_e.activation(
                        out=dtt2[:], in_=dtt2[:], func=AF.Exp, scale=-1.0))
                    EM(S, [dtt2], [dtt2])(s_e.activation(
                        out=dtt2[:], in_=dtt2[:], func=AF.Ln, bias=1.0))
                    EM(V, [dtt1, dtt2], [dtbf])(v_e.tensor_add(dtbf[:], dtt1[:], dtt2[:]))
                    EM(Q, [dtbf], [dt_sc], dma=True)(q_e.dma_start(
                        out=dt_sc[d * 128:(d + 1) * 128, tb * 512:(tb + 1) * 512],
                        in_=dtbf[:]))

        sc.barrier()
        wstack.close()
        HALF = NCH // 2
        if PHASES not in ("toE", "toA", "toC"):
         with contextlib.ExitStack() as ph:
            p_g = [ph.enter_context(nc.psum_tensor(f"pg{i}", [128, 1024], F32))
                   for i in range(2)]
            p_u = [ph.enter_context(nc.psum_tensor(f"pu{i}", [128, 1024], F32))
                   for i in range(2)]
            wgt = [ph.enter_context(nc.sbuf_tensor(f"g_wg{i}", [128, NDT * 128], BF16))
                   for i in range(2)]
            wut = [ph.enter_context(nc.sbuf_tensor(f"g_wu{i}", [128, NDT * 128], BF16))
                   for i in range(2)]
            sgl = [ph.enter_context(nc.sbuf_tensor(f"g_sgl{i}", [128, 1024], BF16))
                   for i in range(2)]
            ucb = [ph.enter_context(nc.sbuf_tensor(f"g_ucb{i}", [128, 1024], BF16))
                   for i in range(2)]
            actst = [ph.enter_context(nc.sbuf_tensor(f"g_act{i}", [128, TSUB], BF16))
                     for i in range(2)]
            cbbc = [ph.enter_context(nc.sbuf_tensor(f"g_cb{i}", [128, TSUB], BF16))
                    for i in range(2)]

            def g_unit(bi, wsrc_g, wsrc_u, col0, cbc, dst, dst_r0):
                EM(G, [wsrc_g], [wgt[bi]], dma=True)(g_e.dma_start(
                    out=wgt[bi][:].rearrange("p (a f) -> p a f", a=NDT),
                    in_=wsrc_g[:, col0:col0 + 128]
                    .rearrange("(a p) f -> p a f", p=128)))
                EM(G, [wsrc_u], [wut[bi]], dma=True)(g_e.dma_start(
                    out=wut[bi][:].rearrange("p (a f) -> p a f", a=NDT),
                    in_=wsrc_u[:, col0:col0 + 128]
                    .rearrange("(a p) f -> p a f", p=128)))
                for ht in range(NDT):
                    for tb in range(2):
                        for hf in range(2):
                            c0 = tb * 1024 + hf * 512
                            EM(T, [wgt[bi], xtown], [p_g[tb]],
                               inc=(ht == NDT - 1))(t_e.matmul(
                                p_g[tb][:, hf * 512:(hf + 1) * 512],
                                wgt[bi][:, ht * 128:(ht + 1) * 128],
                                xtown[:, ht * TSUB + c0:ht * TSUB + c0 + 512],
                                start=(ht == 0), stop=(ht == NDT - 1)))
                    for tb in range(2):
                        for hf in range(2):
                            c0 = tb * 1024 + hf * 512
                            EM(T, [wut[bi], xtown], [p_u[tb]],
                               inc=(ht == NDT - 1))(t_e.matmul(
                                p_u[tb][:, hf * 512:(hf + 1) * 512],
                                wut[bi][:, ht * 128:(ht + 1) * 128],
                                xtown[:, ht * TSUB + c0:ht * TSUB + c0 + 512],
                                start=(ht == 0), stop=(ht == NDT - 1)))
                for tb in range(2):
                    pg, pu = p_g[tb], p_u[tb]
                    EM(S, [pg], [sgl[tb]])(s_e.activation(
                        out=sgl[tb][:], in_=pg[:], func=AF.Silu))
                    EM(V, [pu, cbc], [ucb[tb]])(v_e.tensor_mul(
                        ucb[tb][:], pu[:], cbc[:, tb * 1024:(tb + 1) * 1024]))
                    EM(V, [sgl[tb], ucb[tb]], [actst[bi]])(v_e.tensor_mul(
                        actst[bi][:, tb * 1024:(tb + 1) * 1024],
                        sgl[tb][:], ucb[tb][:]))
                EM(Q, [actst[bi]], [dst], dma=True)(q_e.dma_start(
                    out=dst[dst_r0:dst_r0 + 128, :], in_=actst[bi][:]))

            def gen_G():
                EM(G, [sgT_sc], [sgbc], dma=True)(g_e.dma_start(
                    out=sgbc[:].rearrange("p (o t) -> p o t", o=1),
                    in_=sgT_sc.unsqueeze(0).broadcast_to((128, 1, TSUB))))
                for e in range(NE):
                    cb = cbbc[e % 2]
                    EM(G, [combT_sc], [cb], dma=True)(g_e.dma_start(
                        out=cb[:].rearrange("p (o t) -> p o t", o=1),
                        in_=combT_sc[e:e + 1, :].unsqueeze(0)
                        .broadcast_to((128, 1, TSUB))))
                    for ftl in range(6):
                        g_unit((e * 6 + ftl) % 2, wg[e], wu[e], ftl * 128,
                               cb, act_sc[e], ftl * 128)
                        yield
                for fsl in range(32):
                    g_unit(fsl % 2, swg, swu, fsl * 128,
                           sgbc, mid_sc, fsl * 128)
                    yield

            sbuf = nc.sbuf_tensor
            Bbc = ph.enter_context(sbuf("f_Bbc", [128, NN * TC], BF16))
            Cbc = ph.enter_context(sbuf("f_Cbc", [128, NN * TC], BF16))
            fb = dict(
                dtp=[ph.enter_context(sbuf(f"f_dt{i}", [128, TC], BF16))
                     for i in range(2)],
                xp=[ph.enter_context(sbuf(f"f_x{i}", [128, TC], BF16))
                    for i in range(2)],
                gt=[ph.enter_context(sbuf(f"f_g{i}", [128, TC], BF16))
                    for i in range(2)],
                dtx=ph.enter_context(sbuf("f_dtx", [128, TC], BF16)),
                adb=ph.enter_context(sbuf("f_adb", [128, NN * TC], BF16)),
                dAf=ph.enter_context(sbuf("f_dAf", [128, NN * TC], BF16)),
                stf=ph.enter_context(sbuf("f_stf", [128, NN * TC], BF16)),
                inj=ph.enter_context(sbuf("f_inj", [128, NN], F32)),
                y1=ph.enter_context(sbuf("f_y1", [128, TC], F32)),
                y2=ph.enter_context(sbuf("f_y2", [128, TC], BF16)),
                ygb=ph.enter_context(sbuf("f_ygb", [128, TC], BF16)),
            )

            def load_piece(ch, d, slot):
                bb = fb
                t0 = ch * TC
                dtp, xp, gt = bb["dtp"][slot], bb["xp"][slot], bb["gt"][slot]
                EM(S, [dt_sc], [dtp], dma=True)(s_e.dma_start(
                    out=dtp[:], in_=dt_sc[d * 128:(d + 1) * 128, t0:t0 + TC]))
                EM(S, [x_sc], [xp], dma=True)(s_e.dma_start(
                    out=xp[:], in_=x_sc[d * 128:(d + 1) * 128, t0:t0 + TC]))
                if ch >= HALF:
                    EM(S, [gate_sc], [gt], dma=True)(s_e.dma_start(
                        out=gt[:], in_=gate_sc[d * 128:(d + 1) * 128,
                                               t0 - TOWN:t0 - TOWN + TC]))

            def piece(ch, d, slot):
                bb = fb
                ee = v_e
                engk = V
                t0 = ch * TC
                dtp, xp, gt = bb["dtp"][slot], bb["xp"][slot], bb["gt"][slot]
                adbv = bb["adb"][:].rearrange("p (n t) -> p n t", n=NN)
                dAv = bb["dAf"][:].rearrange("p (n t) -> p n t", n=NN)
                for n in range(NN):
                    EM(S, [dtp, A_f32], [bb["dAf"]])(s_e.activation(
                        out=bb["dAf"][:, n * TC:(n + 1) * TC], in_=dtp[:],
                        func=AF.Exp,
                        scale=A_f32[:, d * NN + n:d * NN + n + 1]))
                EM(engk, [dtp, xp], [bb["dtx"]])(ee.tensor_mul(
                    bb["dtx"][:], dtp[:], xp[:]))
                dtxv = bb["dtx"][:].unsqueeze(1).broadcast_to((128, NN, TC))
                EM(engk, [bb["dtx"], Bbc], [bb["adb"]])(ee.tensor_tensor(
                    adbv, dtxv, Bbc[:].rearrange("p (n t) -> p n t", n=NN), OP.mult))
                cslice = carry_sb[:, d * NN:(d + 1) * NN]
                EM(engk, [bb["dAf"], carry_sb], [bb["inj"]])(ee.tensor_tensor(
                    bb["inj"][:].unsqueeze(2),
                    dAv[:, :, 0:1], cslice.unsqueeze(2), OP.mult))
                EM(engk, [bb["inj"], bb["adb"]], [bb["adb"]])(ee.tensor_tensor(
                    adbv[:, :, 0:1], bb["inj"][:].unsqueeze(2),
                    adbv[:, :, 0:1], OP.add))
                EM(engk, [], [bb["dAf"]])(ee.memset(dAv[:, :, 0:1], 0.0))
                EM(engk, [bb["dAf"], bb["adb"]], [bb["stf"]])(ee.tensor_tensor_scan(
                    out=bb["stf"][:], data0=bb["dAf"][:], data1=bb["adb"][:],
                    initial=0.0, op0=OP.mult, op1=OP.add))
                stv = bb["stf"][:].rearrange("p (n t) -> p n t", n=NN)
                EM(engk, [bb["stf"]], [carry_sb])(ee.tensor_copy(
                    cslice.unsqueeze(2), stv[:, :, TC - 1:TC]))
                if ch == HALF - 1:
                    EM(engk, [carry_sb, pmask_sb], [carry_sb])(ee.tensor_scalar(
                        out=cslice, in0=cslice, scalar1=pmask_sb[:, 0:1],
                        scalar2=None, op0=OP.mult))
                if ch < HALF:
                    return
                EM(engk, [bb["stf"], Cbc], [bb["adb"]])(ee.tensor_tensor(
                    bb["adb"][:], bb["stf"][:], Cbc[:], OP.mult))
                w = NN * TC // 2
                while w >= TC:
                    EM(engk, [bb["adb"]], [bb["adb"]])(ee.tensor_add(
                        bb["adb"][:, 0:w], bb["adb"][:, 0:w],
                        bb["adb"][:, w:2 * w]))
                    w //= 2
                EM(engk, [xp, D_sb, bb["adb"]], [bb["y1"]])(ee.scalar_tensor_tensor(
                    out=bb["y1"][:], in0=xp[:], scalar=D_sb[:, d:d + 1],
                    in1=bb["adb"][:, 0:TC], op0=OP.mult, op1=OP.add))
                EM(engk, [bb["y1"], gt], [bb["ygb"]])(ee.tensor_mul(
                    bb["ygb"][:], bb["y1"][:], gt[:]))
                role_dma(Q, sh_ygg[0][d * 128:(d + 1) * 128, t0 - TOWN:t0 - TOWN + TC],
                         sh_ygg[1][d * 128:(d + 1) * 128, t0 - TOWN:t0 - TOWN + TC],
                         bb["ygb"][:], q_role)

            def gen_F():
                seq = [(ch, d) for ch in range(NCH) for d in range(NDT)]
                load_piece(*seq[0], 0)
                for i, (ch, d) in enumerate(seq):
                    if d == 0:
                        t0 = ch * TC
                        EM(S, [bc_sc], [Bbc], dma=True)(s_e.dma_start(
                            out=Bbc[:].rearrange("p (n t) -> p n t", n=NN),
                            in_=bc_sc[0:16, t0:t0 + TC].unsqueeze(0)
                            .broadcast_to((128, 16, TC))))
                        EM(S, [bc_sc], [Cbc], dma=True)(s_e.dma_start(
                            out=Cbc[:].rearrange("p (n t) -> p n t", n=NN),
                            in_=bc_sc[16:32, t0:t0 + TC].unsqueeze(0)
                            .broadcast_to((128, 16, TC))))
                    if i + 1 < len(seq):
                        load_piece(*seq[i + 1], (i + 1) % 2)
                    piece(ch, d, i % 2)
                    yield

            gens = [gen_G()]
            if PHASES != "toG":
                gens.append(gen_F())
            alive = list(gens)
            while alive:
                for gen in list(alive):
                    try:
                        next(gen)
                    except StopIteration:
                        alive.remove(gen)

        xtstack.close()
        if PHASES in ("toF", "toG", "toE", "toA", "toC"):
            sc.wait_all_dma(Q)
            globals()["_LAST_CNT"] = dict(sc.cnt)
            return
        sc.barrier()
        do_poll = PHASES not in ("nopoll",)
        if do_poll:
            sem_f, tok_f = sc.dma_pre(Q, [sh_ygg[0], sh_ygg[1], nonce_sb],
                                      [sh_flag])
            sc.wait_all_dma(Q, keys=[Q])
            with q_e.If_eq(q_role, 0):
                q_e.dma_start(out=sh_flag[0:1, 0:1],
                              in_=nonce_sb[0:1, 0:1]).then_inc(sem_f, 16)
            with q_e.Else():
                q_e.dma_start(out=sh_flag[0:1, 1:2],
                              in_=nonce_sb[0:1, 0:1]).then_inc(sem_f, 16)
            sc.dma_post(Q, tok_f, [nonce_sb], [sh_flag])

        with contextlib.ExitStack() as ph:
          if do_poll:
            r_a = ph.enter_context(g_e.register("r_a"))
            r_b = ph.enter_context(g_e.register("r_b"))
            r_2n = ph.enter_context(g_e.register("r_2n"))
            r_c = ph.enter_context(g_e.register("r_c"))
            r_it = ph.enter_context(g_e.register("r_it"))
            sc.pre(G, [nonce_sb], [pollw])
            g_e.reg_load(r_a, nonce_sb[0:1, 0:1])
            g_e.reg_add(r_2n, r_a, r_a)
            g_e.reg_mov(r_c, 0)
            g_e.reg_mov(r_it, 0)
            POOL = mybir.EngineType.Pool
            nc.br("poll_s", engines=[POOL])
            with nc.body("poll_s", valid_engines=[POOL]):
                g_e.dma_start(out=pollw[0:1, 0:2], in_=sh_flag[0:1, 0:2]) \
                   .then_inc(psem, 16)
                g_e.reg_add(r_c, r_c, 16)
                g_e.wait_ge(psem, r_c)
                g_e.reg_load(r_a, pollw[0:1, 0:1])
                g_e.reg_load(r_b, pollw[0:1, 1:2])
                g_e.reg_add(r_a, r_a, r_b)
                g_e.reg_sub(r_a, r_a, r_2n)
                g_e.reg_add(r_it, r_it, 1)
                g_e.br_eq(r_a, 0, on_true="poll_e", on_false="poll_c")
            with nc.body("poll_c", valid_engines=[POOL]):
                g_e.br_lt(r_it, MAXPOLL, on_true="poll_s", on_false="poll_f")
            with nc.body("poll_f", valid_engines=[POOL]):
                g_e.halt()
                g_e.br("poll_e")
            nc.switch_bb("poll_e")
        g_e.nop().then_inc(sems[G], 1)
        sc.cnt[G] += 1
        sc.idx[G] += 1
        sc.mark_write(sh_ygg[0], G)
        sc.mark_write(sh_ygg[1], G)

        sc.barrier()
        ENTRIES = ([("moe", e, fk) for e in range(NE) for fk in range(6)]
                   + [("sh", fs, 0) for fs in range(32)]
                   + [("op", ik, 0) for ik in range(32)])
        NSUB = 16
        with contextlib.ExitStack() as ph:
            p_mg = [ph.enter_context(nc.psum_tensor(f"pj{i}", [128, 512], F32))
                    for i in range(8)]
            lsl = [ph.enter_context(nc.sbuf_tensor(f"j_l{i}", [128, NSUB * 4 * 128],
                                                   BF16)) for i in range(2)]
            rsl = [ph.enter_context(nc.sbuf_tensor(f"j_r{i}", [128, NSUB * 1024],
                                                   BF16)) for i in range(2)]
            xres = ph.enter_context(nc.sbuf_tensor("j_xr", [128, 4 * 1024], F32))
            outs = ph.enter_context(nc.sbuf_tensor("j_o", [128, 4 * 1024], F32))

            nsubs = len(ENTRIES) // NSUB
            for hmh in range(2):
                for tg in range(4):
                    EM(Q, [x_ffn], [xres], dma=True)(q_e.dma_start(
                        out=xres[:].rearrange("p (a c) -> p a c", a=4),
                        in_=x_ffn.rearrange("(a p) c -> p a c", p=128)
                        [:, tg * 4:(tg + 1) * 4, hmh * 1024:(hmh + 1) * 1024]))
                    for sci in range(nsubs):
                        bi = sci % 2
                        ls, rs = lsl[bi], rsl[bi]
                        for k in range(NSUB):
                            kind, a1, a2 = ENTRIES[sci * NSUB + k]
                            tcols = slice(tg * 512, (tg + 1) * 512)
                            if kind == "moe":
                                EM(Q, [act_sc], [ls], dma=True)(q_e.dma_start(
                                    out=ls[:, k * 512:(k + 1) * 512],
                                    in_=act_sc[a1][a2 * 128:(a2 + 1) * 128, tcols]))
                                EM(G, [wd_bf], [rs], dma=True)(g_e.dma_start(
                                    out=rs[:, k * 1024:(k + 1) * 1024],
                                    in_=wd_bf[a1][a2 * 128:(a2 + 1) * 128,
                                               hmh * 1024:(hmh + 1) * 1024]))
                            elif kind == "sh":
                                EM(Q, [mid_sc], [ls], dma=True)(q_e.dma_start(
                                    out=ls[:, k * 512:(k + 1) * 512],
                                    in_=mid_sc[a1 * 128:(a1 + 1) * 128, tcols]))
                                EM(G, [swd_bf], [rs], dma=True)(g_e.dma_start(
                                    out=rs[:, k * 1024:(k + 1) * 1024],
                                    in_=swd_bf[a1 * 128:(a1 + 1) * 128,
                                            hmh * 1024:(hmh + 1) * 1024]))
                            else:
                                slab = sh_ygg[0] if a1 < 16 else sh_ygg[1]
                                rr = (a1 % 16)
                                sem_s, tok_s = sc.dma_pre(Q, [slab], [ls])
                                with q_e.If_eq(q_role, 0):
                                    q_e.dma_start(
                                        out=ls[:, k * 512:(k + 1) * 512],
                                        in_=slab[rr * 128:(rr + 1) * 128,
                                                 tg * 512:(tg + 1) * 512]) \
                                        .then_inc(sem_s, 16)
                                with q_e.Else():
                                    q_e.dma_start(
                                        out=ls[:, k * 512:(k + 1) * 512],
                                        in_=slab[rr * 128:(rr + 1) * 128,
                                                 TSUB + tg * 512:
                                                 TSUB + (tg + 1) * 512]) \
                                        .then_inc(sem_s, 16)
                                sc.dma_post(Q, tok_s, [slab], [ls])
                                EM(Q, [wopT_sc], [rs], dma=True)(q_e.dma_start(
                                    out=rs[:, k * 1024:(k + 1) * 1024],
                                    in_=wopT_sc[a1 * 128:(a1 + 1) * 128,
                                                hmh * 1024:(hmh + 1) * 1024]))
                        for tt in range(4):
                            for k in range(NSUB):
                                for hc in range(2):
                                    pm = p_mg[2 * tt + hc]
                                    EM(T, [ls, rs], [pm],
                                       inc=(k == NSUB - 1))(t_e.matmul(
                                        pm[:],
                                        ls[:, k * 512 + tt * 128:
                                           k * 512 + (tt + 1) * 128],
                                        rs[:, k * 1024 + hc * 512:
                                           k * 1024 + (hc + 1) * 512],
                                        start=(sci == 0 and k == 0),
                                        stop=(sci == nsubs - 1
                                              and k == NSUB - 1)))
                    for tt in range(4):
                        for hc in range(2):
                            pm = p_mg[2 * tt + hc]
                            EM(V, [pm, xres], [outs])(v_e.tensor_add(
                                outs[:, tt * 1024 + hc * 512:
                                     tt * 1024 + (hc + 1) * 512],
                                pm[:],
                                xres[:, tt * 1024 + hc * 512:
                                     tt * 1024 + (hc + 1) * 512]))
                    EM(Q, [outs], [out], dma=True)(q_e.dma_start(
                        out=out.rearrange("(a p) c -> p a c", p=128)
                        [:, tg * 4:(tg + 1) * 4, hmh * 1024:(hmh + 1) * 1024],
                        in_=outs[:].rearrange("p (a c) -> p a c", a=4)))

        sc.wait_all_dma(Q)

    es.close()
    return nc


_NC_CACHE = None


def kernel(L0_attn_hidden_states, L0_attn_in_proj_w, L0_attn_conv_w,
           L0_attn_conv_b, L0_attn_x_proj_weight, L0_attn_dt_proj_weight,
           L0_attn_dt_proj_bias, L0_attn_A_log, L0_attn_D, L0_attn_out_proj_w,
           L0_ffn_router_weight, L0_ffn_w_gate, L0_ffn_w_up, L0_ffn_w_down,
           L0_ffn_sw_gate, L0_ffn_sw_up, L0_ffn_sw_down, L0_ffn_sgate_w):
    global _NC_CACHE
    f32 = np.float32
    x = np.asarray(L0_attn_hidden_states, f32)
    cc = np.ascontiguousarray
    nonce = int(np.random.randint(1, 2 ** 30))
    in_maps = []
    for c in range(8):
        b, S, j = c // 4, (c // 2) % 2, c % 2
        x_in = np.zeros((TPAD, H), f32)
        if S == 1:
            x_in[128:4224] = x[b, 0:4096]
        x_in[4224:8320] = x[b, S * 4096:(S + 1) * 4096]
        x_ffn = cc(x[b, S * 4096 + j * TSUB:S * 4096 + (j + 1) * TSUB])
        wip = np.asarray(L0_attn_in_proj_w, f32)[j * 4096:(j + 1) * 4096]
        m = dict(
            x_in=x_in, x_ffn=x_ffn,
            wip_x=cc(wip[0::2]), wip_g=cc(wip[1::2]),
            convw=cc(np.asarray(L0_attn_conv_w, f32)[j * H:(j + 1) * H]),
            convb=cc(np.asarray(L0_attn_conv_b, f32)[j * H:(j + 1) * H]),
            xpw=cc(np.asarray(L0_attn_x_proj_weight, f32)[j]),
            dtw=cc(np.asarray(L0_attn_dt_proj_weight, f32)[j]),
            dtb=cc(np.asarray(L0_attn_dt_proj_bias, f32)[j]),
            alog=cc(np.asarray(L0_attn_A_log, f32)[j]),
            dvec=cc(np.asarray(L0_attn_D, f32)[j]),
            wop=cc(np.asarray(L0_attn_out_proj_w, f32)),
            rw=cc(np.asarray(L0_ffn_router_weight, f32)),
            wg=cc(np.asarray(L0_ffn_w_gate, f32)),
            wu=cc(np.asarray(L0_ffn_w_up, f32)),
            wd=cc(np.asarray(L0_ffn_w_down, f32)),
            swg=cc(np.asarray(L0_ffn_sw_gate, f32)),
            swu=cc(np.asarray(L0_ffn_sw_up, f32)),
            swd=cc(np.asarray(L0_ffn_sw_down, f32)),
            sgw=cc(np.asarray(L0_ffn_sgate_w, f32)),
            ident=np.eye(128, dtype=f32),
            constv=np.tile(np.array([[1e-5, 1.0, -1.0, 1.0 / H, 0.0, 0.0, 0.0, 0.0]], f32), (128, 1)),
            pmask=np.full((128, 1), float(S), f32),
            rolev=np.full((1, 4), j, np.int32),
            noncev=np.full((1, 4), nonce, np.int32),
        )
        in_maps.append(m)
    if _NC_CACHE is None:
        _NC_CACHE = build_nc()
    res = run_bass_kernel_spmd(_NC_CACHE, in_maps, core_ids=list(range(8)))
    out_full = np.zeros((2, 8192, H), f32)
    for c in range(8):
        b, S, j = c // 4, (c // 2) % 2, c % 2
        r0 = S * 4096 + j * TSUB
        out_full[b, r0:r0 + TSUB, :] = res.results[c]["out"]
    return out_full


# version 09
# speedup vs baseline: 1.7927x; 1.0728x over previous
import contextlib
import numpy as np
import concourse.bass as bass
import concourse.mybir as mybir
from concourse.bass_utils import run_bass_kernel_spmd

F32 = mybir.dt.float32
BF16 = mybir.dt.bfloat16
I32 = mybir.dt.int32
AF = mybir.ActivationFunctionType
OP = mybir.AluOpType
AX = mybir.AxisListType

H = 2048
TPAD = 8320
NT = 8192
TOWN = 4096
TSUB = 2048
NE = 32
FF = 768
FS = 4096
NN = 16
TC = 512
NDT = 16
NCH = NT // TC
GP_DT = set()
BIG = 1.0e30
MAXPOLL = 500000
PHASES = "all"
DBG_LOCAL = False
_LAST_CNT = {}


class Sched:

    def __init__(self, nc, sems, eng, pools):
        self.nc = nc
        self.sems = sems
        self.eng = eng
        self.pool = pools
        self.cnt = {k: 0 for k in sems}
        self.idx = {k: 0 for k in sems}
        self.pu = {k: [0] * len(v) for k, v in pools.items()}
        self.pi = {k: 0 for k in pools}
        self.writes = {}
        self.reads = {}
        self.waited = {}

    @staticmethod
    def _nm(a):
        if a is None or isinstance(a, str):
            return a
        t = getattr(a, "tensor", None)
        if t is not None:
            return t.name
        return a.name

    def _sem(self, tok):
        if isinstance(tok, tuple):
            return self.pool[tok[0]][tok[1]]
        return self.sems[tok]

    def _wait(self, wk, tok, v):
        if v <= 0:
            return
        if self.waited.get((wk, tok), 0) >= v:
            return
        self.eng[wk].wait_ge(self._sem(tok), v)
        self.waited[(wk, tok)] = v

    def pre(self, key, reads, writes, dma=False):
        for t in map(self._nm, reads):
            for tok, (wc, wdma, widx, wkey) in self.writes.get(t, {}).items():
                if wdma:
                    self._wait(key, tok, wc)
                elif wkey == key:
                    if key != 'T':
                        self._wait(key, key, wc)
                else:
                    self._wait(key, tok, wc)
        for t in map(self._nm, writes):
            for rtok, (rc, rdma, ridx, rkey) in self.reads.get(t, {}).items():
                if rdma:
                    self._wait(key, rtok, rc)
                elif rkey != key:
                    self._wait(key, rtok, rc)
                elif key != 'T':
                    self._wait(key, key, rc)
            for tok, (wc, wdma, widx, wkey) in self.writes.get(t, {}).items():
                if wdma:
                    self._wait(key, tok, wc)
                elif wkey != key:
                    self._wait(key, tok, wc)
                elif key != 'T':
                    self._wait(key, key, wc)
        if dma:
            i = self.pi[key]
            self.pi[key] = (i + 1) % len(self.pool[key])
            if self.pu[key][i] > 0:
                self._wait(key, (key, i), 16 * self.pu[key][i])
            self._pend = (key, i)
            return self._pend
        return None

    def post(self, key, inst, reads, writes, dma, inc, tok=None):
        self.idx[key] += 1
        if dma:
            assert inc
            k, i = tok if tok is not None else self._pend
            if inst is not None:
                inst.then_inc(self.pool[k][i], 16)
            self.pu[k][i] += 1
            val = 16 * self.pu[k][i]
            wtok = (k, i)
        else:
            if inc:
                inst.then_inc(self.sems[key], 1)
                self.cnt[key] += 1
            val = self.cnt[key] if inc else self.cnt[key] + 1
            wtok = key
        for t in map(self._nm, reads):
            self.reads.setdefault(t, {})[wtok] = (val, dma, self.idx[key], key)
        for t in map(self._nm, writes):
            self.writes.setdefault(t, {})[wtok] = (val, dma, self.idx[key], key)
        return inst

    def EM(self, key, reads=(), writes=(), dma=False, inc=True):
        self.pre(key, reads, writes, dma=dma)
        return lambda inst: self.post(key, inst, reads, writes, dma, inc)

    def dma_pre(self, key, reads, writes):
        tok = self.pre(key, reads, writes, dma=True)
        return self.pool[tok[0]][tok[1]], tok

    def dma_post(self, key, tok, reads, writes):
        self.post(key, None, reads, writes, True, True, tok=tok)

    def mark_write(self, name, key):
        self.writes.setdefault(self._nm(name), {})[key] = (
            self.cnt[key], False, self.idx[key], key)

    def wait_all_dma(self, wk, keys=None):
        for pk in (keys if keys is not None else self.pool):
            for i, u in enumerate(self.pu[pk]):
                self._wait(wk, (pk, i), 16 * u)

    def barrier(self):
        for wk in self.sems:
            for tk in self.sems:
                if wk != tk:
                    self._wait(wk, tk, self.cnt[tk])
            self.wait_all_dma(wk)


def build_nc():
    nc = bass.Bass("TRN2", target_bir_lowering=False, debug=False)
    dram = nc.dram_tensor

    x_in = dram("x_in", [TPAD, H], F32, kind="ExternalInput").ap()
    x_ffn = dram("x_ffn", [TSUB, H], F32, kind="ExternalInput").ap()
    wip_x = dram("wip_x", [H, H], F32, kind="ExternalInput").ap()
    wip_g = dram("wip_g", [H, H], F32, kind="ExternalInput").ap()
    convw = dram("convw", [H, 4], F32, kind="ExternalInput").ap()
    convb = dram("convb", [H], F32, kind="ExternalInput").ap()
    xpw = dram("xpw", [160, H], F32, kind="ExternalInput").ap()
    dtw = dram("dtw", [H, 128], F32, kind="ExternalInput").ap()
    dtb = dram("dtb", [H], F32, kind="ExternalInput").ap()
    alog = dram("alog", [H, NN], F32, kind="ExternalInput").ap()
    dvec = dram("dvec", [H], F32, kind="ExternalInput").ap()
    wop = dram("wop", [H, 2 * H], F32, kind="ExternalInput").ap()
    rw = dram("rw", [NE, H], F32, kind="ExternalInput").ap()
    wg = dram("wg", [NE, H, FF], F32, kind="ExternalInput").ap()
    wu = dram("wu", [NE, H, FF], F32, kind="ExternalInput").ap()
    wd = dram("wd", [NE, FF, H], F32, kind="ExternalInput").ap()
    swg = dram("swg", [H, FS], F32, kind="ExternalInput").ap()
    swu = dram("swu", [H, FS], F32, kind="ExternalInput").ap()
    swd = dram("swd", [FS, H], F32, kind="ExternalInput").ap()
    sgw = dram("sgw", [H, 1], F32, kind="ExternalInput").ap()
    ident = dram("ident", [128, 128], F32, kind="ExternalInput").ap()
    pmask = dram("pmask", [128, 1], F32, kind="ExternalInput").ap()
    rolev = dram("rolev", [1, 4], I32, kind="ExternalInput").ap()
    noncev = dram("noncev", [1, 4], I32, kind="ExternalInput").ap()
    constv = dram("constv", [128, 8], F32, kind="ExternalInput").ap()
    out = dram("out", [TSUB, H], F32, kind="ExternalOutput").ap()

    xt_sc = dram("xt_sc", [H, TPAD], BF16).ap()
    x_sc = dram("x_sc", [H, NT], BF16).ap()
    dt_sc = dram("dt_sc", [H, NT], BF16).ap()
    gate_sc = dram("gate_sc", [H, TOWN], BF16).ap()
    act_sc = dram("act_sc", [NE, FF, TSUB], BF16).ap()
    mid_sc = dram("mid_sc", [FS, TSUB], BF16).ap()
    wopT_sc = dram("wopT_sc", [2 * H, H], BF16).ap()
    bc_sc = dram("bc_sc", [32, NT], BF16).ap()
    wd_bf = dram("wd_bf", [NE, FF, H], BF16).ap()
    swd_bf = dram("swd_bf", [FS, H], BF16).ap()
    combT_sc = dram("combT_sc", [NE, TSUB], F32).ap()
    sgT_sc = dram("sgT_sc", [1, TSUB], F32).ap()
    _shared = {} if DBG_LOCAL else {"addr_space": "Shared"}
    sh_ygg = dram("sh_ygg", [2, H, TOWN], BF16, **_shared).ap()
    sh_flag = dram("sh_flag", [1, 16], I32, **_shared).ap()

    es = contextlib.ExitStack()
    sb = lambda n, s, d: es.enter_context(nc.sbuf_tensor(n, s, d))

    identf = sb("identf", [128, 128], F32)
    identb = sb("identb", [128, 128], BF16)
    A_f32 = sb("A_f32", [128, NDT * NN], F32)
    D_sb = sb("D_sb", [128, NDT], F32)
    convw_sb = sb("convw_sb", [128, NDT * 4], F32)
    convb_sb = sb("convb_sb", [128, NDT], F32)
    dtb_sb = sb("dtb_sb", [128, NDT], F32)
    pmask_sb = sb("pmask_sb", [128, 1], F32)
    role_sb = sb("role_sb", [1, 4], I32)
    nonce_sb = sb("nonce_sb", [1, 4], I32)
    pollw = sb("pollw", [1, 8], I32)
    sgw_sb = sb("sgw_sb", [128, NDT], F32)
    carry_sb = sb("carry_sb", [128, NDT * NN], F32)
    sgbc = sb("sgbc", [128, TSUB], BF16)
    const_sb = sb("const_sb", [128, 8], F32)
    nc.const_aps.register(F32, 1e-5, const_sb[:, 0:1])
    nc.const_aps.register(F32, 1.0, const_sb[:, 1:2])
    nc.const_aps.register(F32, -1.0, const_sb[:, 2:3])
    nc.const_aps.register(F32, 1.0 / H, const_sb[:, 3:4])
    nc.const_aps.register(F32, 0.0, const_sb[:, 4:5])

    xtstack = contextlib.ExitStack()
    xtown = xtstack.enter_context(nc.sbuf_tensor("xtown", [128, NDT * TSUB], BF16))
    wstack = contextlib.ExitStack()
    wsb = lambda n, s, d: wstack.enter_context(nc.sbuf_tensor(n, s, d))
    xpwT_sb = wsb("xpwT_sb", [128, NDT * 160], BF16)
    wdtT_sb = wsb("wdtT_sb", [128, NDT * 128], BF16)
    rwT_sb = wsb("rwT_sb", [128, NDT * NE], F32)

    sems = {k: es.enter_context(nc.semaphore("sem_" + k)) for k in
            ("T", "V", "S", "G", "Q")}
    pools = {
        "Q": [es.enter_context(nc.semaphore(f"pq{i}")) for i in range(24)],
        "G": [es.enter_context(nc.semaphore(f"pg{i}")) for i in range(12)],
        "S": [es.enter_context(nc.semaphore(f"ps{i}")) for i in range(8)],
    }
    psem = es.enter_context(nc.semaphore("sem_poll"))
    block = es.enter_context(nc.Block())

    @block.sync
    def _(sync):
        T, V, S, G, Q = "T", "V", "S", "G", "Q"
        t_e, v_e, s_e, g_e, q_e = nc.tensor, nc.vector, nc.scalar, nc.gpsimd, nc.sync
        sc = Sched(nc, sems, {"T": t_e, "V": v_e, "S": s_e, "G": g_e, "Q": q_e},
                   pools)
        EM = sc.EM

        q_role = es.enter_context(q_e.register("q_role"))

        def role_dma(key, out0, out1, in_ap, reg):
            sem, tok = sc.dma_pre(key, [in_ap], [out0, out1])
            eng = sc.eng[key]
            with eng.If_eq(reg, 0):
                eng.dma_start(out=out0, in_=in_ap).then_inc(sem, 16)
            with eng.Else():
                eng.dma_start(out=out1, in_=in_ap).then_inc(sem, 16)
            sc.dma_post(key, tok, [in_ap], [out0, out1])

        with contextlib.ExitStack() as ph:
            p_tp = ph.enter_context(nc.psum_tensor("p0_tp", [128, 128], BF16))
            p_tpf = ph.enter_context(nc.psum_tensor("p0_tpf", [128, 128], F32))
            stage = ph.enter_context(nc.sbuf_tensor("st0", [128, 2048], F32))
            stageb = ph.enter_context(nc.sbuf_tensor("st0b", [128, 2048], BF16))

            EM(Q, [constv], [const_sb], dma=True)(q_e.dma_start(out=const_sb[:], in_=constv))
            EM(Q, [ident], [identf], dma=True)(q_e.dma_start(out=identf[:], in_=ident))
            sc.barrier()
            EM(G, [ident], [identb], dma=True)(g_e.dma_start(out=identb[:], in_=ident))
            EM(Q, [pmask], [pmask_sb], dma=True)(q_e.dma_start(out=pmask_sb[:], in_=pmask))
            EM(Q, [rolev], [role_sb], dma=True)(q_e.dma_start(out=role_sb[:], in_=rolev))
            EM(Q, [noncev], [nonce_sb], dma=True)(q_e.dma_start(out=nonce_sb[:], in_=noncev))
            EM(Q, [alog], [stage], dma=True)(q_e.dma_start(
                out=stage[:, 0:NDT * NN].rearrange("p (a n) -> p a n", a=NDT),
                in_=alog.rearrange("(a p) n -> p a n", p=128)))
            EM(S, [stage], [stage])(s_e.activation(
                out=stage[:, 0:NDT * NN], in_=stage[:, 0:NDT * NN], func=AF.Exp))
            EM(V, [stage], [A_f32])(v_e.tensor_scalar_mul(
                A_f32[:], stage[:, 0:NDT * NN], -1.0))
            EM(Q, [dvec], [D_sb], dma=True)(q_e.dma_start(
                out=D_sb[:].unsqueeze(2),
                in_=dvec.rearrange("(a p) -> p a", p=128).unsqueeze(2),
                allow_slow_non_contiguous=True))
            EM(Q, [convw], [convw_sb], dma=True)(q_e.dma_start(
                out=convw_sb[:].rearrange("p (a k) -> p a k", a=NDT),
                in_=convw.rearrange("(a p) k -> p a k", p=128)))
            EM(Q, [convb], [convb_sb], dma=True)(q_e.dma_start(
                out=convb_sb[:].unsqueeze(2),
                in_=convb.rearrange("(a p) -> p a", p=128).unsqueeze(2),
                allow_slow_non_contiguous=True))
            EM(Q, [dtb], [dtb_sb], dma=True)(q_e.dma_start(
                out=dtb_sb[:].unsqueeze(2),
                in_=dtb.rearrange("(a p) -> p a", p=128).unsqueeze(2),
                allow_slow_non_contiguous=True))
            EM(Q, [sgw], [sgw_sb], dma=True)(q_e.dma_start(
                out=sgw_sb[:].unsqueeze(2),
                in_=sgw.rearrange("(a p) o -> p a o", p=128),
                allow_slow_non_contiguous=True))
            EM(V, [], [carry_sb])(v_e.memset(carry_sb[:], 0.0))

            EM(G, [xpw], [stageb], dma=True)(g_e.dma_start(
                out=stageb[:], in_=xpw[0:128, :]))
            for d in range(NDT):
                EM(T, [stageb, identb], [p_tp])(t_e.transpose(
                    p_tp[:], stageb[:, d * 128:(d + 1) * 128], identb[:]))
                EM(V, [p_tp], [xpwT_sb])(v_e.tensor_copy(
                    xpwT_sb[:, d * 160:d * 160 + 128], p_tp[:]))
            EM(G, [xpw], [stageb], dma=True)(g_e.dma_start(
                out=stageb[0:32, :], in_=xpw[128:160, :]))
            for d in range(NDT):
                EM(T, [stageb, identb], [p_tp])(t_e.transpose(
                    p_tp[:, 0:32], stageb[0:32, d * 128:(d + 1) * 128],
                    identb[0:32, 0:32]))
                EM(V, [p_tp], [xpwT_sb])(v_e.tensor_copy(
                    xpwT_sb[:, d * 160 + 128:d * 160 + 160], p_tp[:, 0:32]))
            EM(G, [dtw], [stageb], dma=True)(g_e.dma_start(
                out=stageb[:].rearrange("p (a r) -> p a r", a=NDT),
                in_=dtw.rearrange("(a p) r -> p a r", p=128)))
            for d in range(NDT):
                EM(T, [stageb, identb], [p_tp])(t_e.transpose(
                    p_tp[:], stageb[:, d * 128:(d + 1) * 128], identb[:]))
                EM(V, [p_tp], [wdtT_sb])(v_e.tensor_copy(
                    wdtT_sb[:, d * 128:(d + 1) * 128], p_tp[:]))
            EM(Q, [rw], [stage], dma=True)(q_e.dma_start(out=stage[0:32, :], in_=rw))
            for d in range(NDT):
                EM(T, [stage, identf], [p_tpf])(t_e.transpose(
                    p_tpf[:, 0:32], stage[0:32, d * 128:(d + 1) * 128],
                    identf[0:32, 0:32]))
                EM(V, [p_tpf], [rwT_sb])(v_e.tensor_copy(
                    rwT_sb[:, d * NE:(d + 1) * NE], p_tpf[:, 0:32]))
            wotst = ph.enter_context(nc.sbuf_tensor("wotst", [128, 2048], BF16))
            for hmt in range(NDT):
                for ihalf in range(2):
                    EM(G, [wop], [stageb], dma=True)(g_e.dma_start(
                        out=stageb[:],
                        in_=wop[hmt * 128:(hmt + 1) * 128,
                                ihalf * 2048:(ihalf + 1) * 2048]))
                    for ib in range(16):
                        EM(T, [stageb, identb], [p_tp])(t_e.transpose(
                            p_tp[:], stageb[:, ib * 128:(ib + 1) * 128], identb[:]))
                        EM(V, [p_tp], [wotst])(v_e.tensor_copy(
                            wotst[:, ib * 128:(ib + 1) * 128], p_tp[:]))
                    EM(Q, [wotst], [wopT_sc], dma=True)(q_e.dma_start(
                        out=wopT_sc.rearrange("(a p) c -> p a c", p=128)
                        [:, ihalf * 16:(ihalf + 1) * 16,
                         hmt * 128:(hmt + 1) * 128],
                        in_=wotst[:].rearrange("p (a t) -> p a t", a=16)))
            sc.pre(Q, [role_sb], [])
            q_e.reg_load(q_role, role_sb[0:1, 0:1])

        sc.barrier()
        for e in range(NE):
            EM(G, [wd], [wd_bf], dma=True)(g_e.dma_start(
                out=wd_bf[e], in_=wd[e]))
        for fq in range(4):
            EM(G, [swd], [swd_bf], dma=True)(g_e.dma_start(
                out=swd_bf[fq * 1024:(fq + 1) * 1024, :],
                in_=swd[fq * 1024:(fq + 1) * 1024, :]))
        with contextlib.ExitStack() as ph:
            p_tp = [ph.enter_context(nc.psum_tensor(f"pa_tp{i}", [128, 512], BF16))
                    for i in range(2)]
            p_tpf = [ph.enter_context(nc.psum_tensor(f"pa_tpf{i}", [128, 512], F32))
                     for i in range(2)]
            p_rt = ph.enter_context(nc.psum_tensor("pa_rt", [128, NE], F32))
            p_sg = ph.enter_context(nc.psum_tensor("pa_sg", [128, 8], F32))
            p_tc = ph.enter_context(nc.psum_tensor("pa_tc", [32, 128], F32))
            xtile = [ph.enter_context(nc.sbuf_tensor(f"a_xt{i}", [128, H], F32))
                     for i in range(2)]
            xm = [ph.enter_context(nc.sbuf_tensor(f"a_xm{i}", [128, H], F32))
                  for i in range(2)]
            cols = [ph.enter_context(nc.sbuf_tensor(f"a_cols{i}", [128, 8], F32))
                    for i in range(2)]
            xnb = [ph.enter_context(nc.sbuf_tensor(f"a_xnb{i}", [128, H], BF16))
                   for i in range(2)]
            xts = [ph.enter_context(nc.sbuf_tensor(f"a_xts{i}", [128, NDT * 128],
                                                   BF16)) for i in range(2)]
            xnTf = ph.enter_context(nc.sbuf_tensor("a_xnTf", [128, NDT * 128], F32))
            prb = ph.enter_context(nc.sbuf_tensor("a_prb", [128, NE], F32))
            tk = ph.enter_context(nc.sbuf_tensor("a_tk", [128, 8], F32))
            comb = ph.enter_context(nc.sbuf_tensor("a_comb", [128, NE], F32))
            eqm = ph.enter_context(nc.sbuf_tensor("a_eqm", [128, NE], F32))
            combT = ph.enter_context(nc.sbuf_tensor("a_combT", [32, 128], F32))

            def ln_tile(src_ap, xsc_col, ffn_tile, par):
                xt_, xm_, co_ = xtile[par], xm[par], cols[par]
                xb_, xs_ = xnb[par], xts[par]
                EM(Q, [src_ap], [xt_], dma=True)(q_e.dma_start(out=xt_[:], in_=src_ap))
                yield
                EM(V, [xt_], [co_])(v_e.reduce_sum(co_[:, 0:1], xt_[:], axis=AX.X))
                EM(V, [co_], [co_])(v_e.tensor_scalar_mul(co_[:, 1:2], co_[:, 0:1], 1.0 / H))
                EM(V, [xt_, co_], [xm_])(v_e.tensor_scalar(
                    out=xm_[:], in0=xt_[:], scalar1=co_[:, 1:2], scalar2=None,
                    op0=OP.subtract))
                EM(S, [xm_], [xt_, co_])(s_e.activation(
                    out=xt_[:], in_=xm_[:], func=AF.Square, accum_out=co_[:, 2:3]))
                EM(S, [co_], [co_])(s_e.activation(
                    out=co_[:, 3:4], in_=co_[:, 2:3], func=AF.Sqrt,
                    scale=1.0 / H, bias=1e-5))
                EM(V, [co_], [co_])(v_e.reciprocal(co_[:, 4:5], co_[:, 3:4]))
                EM(V, [xm_, co_], [xm_])(v_e.tensor_scalar(
                    out=xm_[:], in0=xm_[:], scalar1=co_[:, 4:5], scalar2=None,
                    op0=OP.mult))
                EM(V, [xm_], [xb_])(v_e.tensor_copy(xb_[:], xm_[:]))
                yield
                for q in range(4):
                    pt = p_tp[q % 2]
                    for j in range(4):
                        ht = q * 4 + j
                        EM(T, [xb_, identb], [pt])(t_e.transpose(
                            pt[:, j * 128:(j + 1) * 128],
                            xb_[:, ht * 128:(ht + 1) * 128], identb[:]))
                    EM(V, [pt], [xs_])(v_e.tensor_copy(
                        xs_[:, q * 512:(q + 1) * 512], pt[:]))
                if xsc_col is not None:
                    EM(Q, [xs_], [xt_sc], dma=True)(q_e.dma_start(
                        out=xt_sc.rearrange("(a p) c -> p a c", p=128)
                        [:, :, xsc_col:xsc_col + 128],
                        in_=xs_[:].rearrange("p (a t) -> p a t", a=NDT)))
                yield
                if ffn_tile is None:
                    return
                ft = ffn_tile
                EM(V, [xs_], [xtown])(v_e.tensor_copy(
                    xtown[:].rearrange("p (a t) -> p a t", a=NDT)
                    [:, :, ft * 128:(ft + 1) * 128],
                    xs_[:].rearrange("p (a t) -> p a t", a=NDT)))
                for q in range(4):
                    ptf = p_tpf[q % 2]
                    for j in range(4):
                        ht = q * 4 + j
                        EM(T, [xm_, identf], [ptf])(t_e.transpose(
                            ptf[:, j * 128:(j + 1) * 128],
                            xm_[:, ht * 128:(ht + 1) * 128], identf[:]))
                    EM(V, [ptf], [xnTf])(v_e.tensor_copy(
                        xnTf[:, q * 512:(q + 1) * 512], ptf[:]))
                yield
                for ht in range(NDT):
                    EM(T, [xnTf, rwT_sb], [p_rt], inc=(ht == NDT - 1))(t_e.matmul(
                        p_rt[:], xnTf[:, ht * 128:(ht + 1) * 128],
                        rwT_sb[:, ht * NE:(ht + 1) * NE],
                        start=(ht == 0), stop=(ht == NDT - 1)))
                for ht in range(NDT):
                    EM(T, [xnTf, sgw_sb], [p_sg], inc=(ht == NDT - 1))(t_e.matmul(
                        p_sg[:, 0:1], xnTf[:, ht * 128:(ht + 1) * 128],
                        sgw_sb[:, ht:ht + 1],
                        start=(ht == 0), stop=(ht == NDT - 1)))
                EM(V, [p_rt], [co_])(v_e.reduce_max(co_[:, 5:6], p_rt[:], axis=AX.X))
                EM(V, [co_], [co_])(v_e.tensor_scalar_mul(co_[:, 6:7], co_[:, 5:6], -1.0))
                EM(S, [p_rt, co_], [prb, co_])(s_e.activation(
                    out=prb[:], in_=p_rt[:], func=AF.Exp, bias=co_[:, 6:7],
                    accum_out=co_[:, 7:8]))
                EM(V, [co_], [co_])(v_e.reciprocal(co_[:, 0:1], co_[:, 7:8]))
                EM(V, [prb, co_], [prb])(v_e.tensor_scalar(
                    out=prb[:], in0=prb[:], scalar1=co_[:, 0:1], scalar2=None,
                    op0=OP.mult))
                EM(V, [], [comb])(v_e.memset(comb[:], 0.0))
                EM(V, [], [tk])(v_e.memset(tk[:, 0:1], 0.0))
                for r in range(4):
                    EM(V, [prb], [tk])(v_e.reduce_max(tk[:, 1:2], prb[:], axis=AX.X))
                    EM(V, [prb, tk], [eqm])(v_e.tensor_scalar(
                        out=eqm[:], in0=prb[:], scalar1=tk[:, 1:2], scalar2=None,
                        op0=OP.is_equal))
                    EM(V, [tk], [tk])(v_e.tensor_add(tk[:, 0:1], tk[:, 0:1], tk[:, 1:2]))
                    EM(V, [eqm, tk, comb], [comb])(v_e.scalar_tensor_tensor(
                        out=comb[:], in0=eqm[:], scalar=tk[:, 1:2], in1=comb[:],
                        op0=OP.mult, op1=OP.add))
                    EM(V, [eqm, prb], [prb])(v_e.scalar_tensor_tensor(
                        out=prb[:], in0=eqm[:], scalar=-BIG, in1=prb[:],
                        op0=OP.mult, op1=OP.add))
                EM(V, [tk], [tk])(v_e.reciprocal(tk[:, 2:3], tk[:, 0:1]))
                EM(V, [comb, tk], [comb])(v_e.tensor_scalar(
                    out=comb[:], in0=comb[:], scalar1=tk[:, 2:3], scalar2=None,
                    op0=OP.mult))
                EM(T, [comb, identf], [p_tc])(t_e.transpose(p_tc[:], comb[:], identf[:]))
                EM(V, [p_tc], [combT])(v_e.tensor_copy(combT[:], p_tc[:]))
                EM(Q, [combT], [combT_sc], dma=True)(q_e.dma_start(
                    out=combT_sc[:, ft * 128:(ft + 1) * 128], in_=combT[:]))
                EM(S, [p_sg], [co_])(s_e.activation(
                    out=co_[:, 1:2], in_=p_sg[:, 0:1], func=AF.Sigmoid))
                EM(Q, [co_], [sgT_sc], dma=True)(q_e.dma_start(
                    out=sgT_sc[0:1, ft * 128:(ft + 1) * 128].rearrange("o t -> t o"),
                    in_=co_[:, 1:2], allow_slow_non_contiguous=True))

            tiles = ([(x_in[tt * 128:(tt + 1) * 128, :], tt * 128, None)
                      for tt in range(TPAD // 128)]
                     + [(x_ffn[ft * 128:(ft + 1) * 128, :], None, ft)
                        for ft in range(TSUB // 128)])
            window = []
            idx = 0
            while window or idx < len(tiles):
                while len(window) < 2 and idx < len(tiles):
                    src_ap, col, ftl = tiles[idx]
                    g = ln_tile(src_ap, col, ftl, idx % 2)
                    idx += 1
                    next(g)
                    window.append(g)
                for g in list(window):
                    try:
                        next(g)
                    except StopIteration:
                        window.remove(g)

        sc.barrier()
        if PHASES != "toA":
         with contextlib.ExitStack() as ph:
            p_m = [ph.enter_context(nc.psum_tensor(f"pc{i}", [128, 512], F32))
                   for i in range(2)]
            p_tp = ph.enter_context(nc.psum_tensor("pc_tp", [128, 128], BF16))
            wstage = ph.enter_context(nc.sbuf_tensor("c_ws", [128, 4 * H], BF16))
            lhsT = ph.enter_context(nc.sbuf_tensor("c_lh", [128, 64 * 128], BF16))
            xtt = [ph.enter_context(nc.sbuf_tensor(f"c_xt{i}", [128, NDT * 512], BF16))
                   for i in range(2)]
            xcat = ph.enter_context(nc.sbuf_tensor("c_xc", [128, 4 * 516], F32))
            cacc = ph.enter_context(nc.sbuf_tensor("c_ca", [128, 512], F32))
            xbf = ph.enter_context(nc.sbuf_tensor("c_xb", [128, 512], BF16))
            gbf = ph.enter_context(nc.sbuf_tensor("c_gb", [128, 512], BF16))

            TBL = [(i * 512, 512) for i in range(16)] + [(8192, 128)]

            def load_lhsT(wsrc, ob):
                EM(G, [wsrc], [wstage], dma=True)(g_e.dma_start(
                    out=wstage[:].rearrange("p (a h) -> p a h", a=4),
                    in_=wsrc[ob * 512:(ob + 1) * 512, :]
                    .rearrange("(a p) h -> p a h", p=128)))
                for ot in range(4):
                    for ht in range(NDT):
                        EM(T, [wstage, identb], [p_tp])(t_e.transpose(
                            p_tp[:], wstage[:, ot * H + ht * 128:ot * H + (ht + 1) * 128],
                            identb[:]))
                        EM(V, [p_tp], [lhsT])(v_e.tensor_copy(
                            lhsT[:, (ot * 16 + ht) * 128:(ot * 16 + ht + 1) * 128],
                            p_tp[:]))

            for ob in range(4):
                load_lhsT(wip_x, ob)
                EM(V, [], [xcat])(v_e.memset(xcat[:], 0.0))
                for tbi, (c0, nt) in enumerate(TBL):
                    xb = xtt[tbi % 2]
                    EM(Q, [xt_sc], [xb], dma=True)(q_e.dma_start(
                        out=xb[:].rearrange("p (a t) -> p a t", a=NDT)[:, :, 0:nt],
                        in_=xt_sc.rearrange("(a p) c -> p a c", p=128)[:, :, c0:c0 + nt]))
                    for ot in range(4):
                        ota = ob * 4 + ot
                        pm = p_m[ot % 2]
                        for ht in range(NDT):
                            EM(T, [lhsT, xb], [pm], inc=(ht == NDT - 1))(t_e.matmul(
                                pm[:, 0:nt],
                                lhsT[:, (ot * 16 + ht) * 128:(ot * 16 + ht + 1) * 128],
                                xb[:, ht * 512:ht * 512 + nt],
                                start=(ht == 0), stop=(ht == NDT - 1)))
                        EM(V, [pm], [xcat])(v_e.tensor_copy(
                            xcat[:, ot * 516 + 3:ot * 516 + 3 + nt], pm[:, 0:nt]))
                        EM(V, [xcat, convw_sb], [cacc])(v_e.tensor_scalar(
                            out=cacc[:, 0:nt], in0=xcat[:, ot * 516:ot * 516 + nt],
                            scalar1=convw_sb[:, ota * 4:ota * 4 + 1], scalar2=None,
                            op0=OP.mult))
                        for k in range(1, 4):
                            EM(V, [xcat, convw_sb, cacc], [cacc])(v_e.scalar_tensor_tensor(
                                out=cacc[:, 0:nt],
                                in0=xcat[:, ot * 516 + k:ot * 516 + k + nt],
                                scalar=convw_sb[:, ota * 4 + k:ota * 4 + k + 1],
                                in1=cacc[:, 0:nt], op0=OP.mult, op1=OP.add))
                        EM(V, [xcat], [xcat])(v_e.tensor_copy(
                            xcat[:, ot * 516:ot * 516 + 3],
                            xcat[:, ot * 516 + nt:ot * 516 + nt + 3]))
                        EM(S, [cacc, convb_sb], [xbf])(s_e.activation(
                            out=xbf[:, 0:nt], in_=cacc[:, 0:nt], func=AF.Silu,
                            bias=convb_sb[:, ota:ota + 1]))
                        if c0 + nt > 128:
                            s0 = max(0, 128 - c0)
                            EM(Q, [xbf], [x_sc], dma=True)(q_e.dma_start(
                                out=x_sc[ota * 128:(ota + 1) * 128,
                                         c0 + s0 - 128:c0 + nt - 128],
                                in_=xbf[:, s0:nt]))

            for ob in range(4):
                load_lhsT(wip_g, ob)
                for tbi in range(8):
                    c0 = 4224 + tbi * 512
                    xb = xtt[tbi % 2]
                    EM(Q, [xt_sc], [xb], dma=True)(q_e.dma_start(
                        out=xb[:].rearrange("p (a t) -> p a t", a=NDT)[:, :, 0:512],
                        in_=xt_sc.rearrange("(a p) c -> p a c", p=128)[:, :, c0:c0 + 512]))
                    for ot in range(4):
                        ota = ob * 4 + ot
                        pm = p_m[ot % 2]
                        for ht in range(NDT):
                            EM(T, [lhsT, xb], [pm], inc=(ht == NDT - 1))(t_e.matmul(
                                pm[:],
                                lhsT[:, (ot * 16 + ht) * 128:(ot * 16 + ht + 1) * 128],
                                xb[:, ht * 512:(ht + 1) * 512],
                                start=(ht == 0), stop=(ht == NDT - 1)))
                        EM(S, [pm], [gbf])(s_e.activation(
                            out=gbf[:], in_=pm[:], func=AF.Silu))
                        EM(Q, [gbf], [gate_sc], dma=True)(q_e.dma_start(
                            out=gate_sc[ota * 128:(ota + 1) * 128,
                                        c0 - 4224:c0 - 4224 + 512],
                            in_=gbf[:]))

        sc.barrier()
        if PHASES not in ("toA", "toC"):
         with contextlib.ExitStack() as ph:
            p_xp = ph.enter_context(nc.psum_tensor("pe_xp", [128, 512], F32))
            p_xp2 = ph.enter_context(nc.psum_tensor("pe_xp2", [32, 512], F32))
            p_dt = [ph.enter_context(nc.psum_tensor(f"pe_dt{i}", [128, 512], F32))
                    for i in range(2)]
            exi = ph.enter_context(nc.sbuf_tensor("e_xi", [128, NDT * 512], BF16))
            dtin = ph.enter_context(nc.sbuf_tensor("e_dtin", [128, 512], BF16))
            bcrows = ph.enter_context(nc.sbuf_tensor("e_bc", [32, 512], BF16))
            dtt1 = ph.enter_context(nc.sbuf_tensor("e_t1", [128, 512], F32))
            dtt2 = ph.enter_context(nc.sbuf_tensor("e_t2", [128, 512], F32))
            dtbf = ph.enter_context(nc.sbuf_tensor("e_dtb", [128, 512], BF16))

            for tb in range(16):
                EM(Q, [x_sc], [exi], dma=True)(q_e.dma_start(
                    out=exi[:].rearrange("p (a t) -> p a t", a=NDT),
                    in_=x_sc.rearrange("(a p) c -> p a c", p=128)
                    [:, :, tb * 512:(tb + 1) * 512]))
                for d in range(NDT):
                    EM(T, [xpwT_sb, exi], [p_xp], inc=(d == NDT - 1))(t_e.matmul(
                        p_xp[:], xpwT_sb[:, d * 160:d * 160 + 128],
                        exi[:, d * 512:(d + 1) * 512],
                        start=(d == 0), stop=(d == NDT - 1)))
                for d in range(NDT):
                    EM(T, [xpwT_sb, exi], [p_xp2], inc=(d == NDT - 1))(t_e.matmul(
                        p_xp2[:], xpwT_sb[:, d * 160 + 128:d * 160 + 160],
                        exi[:, d * 512:(d + 1) * 512],
                        start=(d == 0), stop=(d == NDT - 1)))
                EM(V, [p_xp2], [bcrows])(v_e.tensor_copy(bcrows[:], p_xp2[:]))
                EM(Q, [bcrows], [bc_sc], dma=True)(q_e.dma_start(
                    out=bc_sc[:, tb * 512:(tb + 1) * 512], in_=bcrows[:]))
                EM(V, [p_xp], [dtin])(v_e.tensor_copy(dtin[:], p_xp[:]))
                for d in range(NDT):
                    pm = p_dt[d % 2]
                    EM(T, [wdtT_sb, dtin], [pm])(t_e.matmul(
                        pm[:], wdtT_sb[:, d * 128:(d + 1) * 128], dtin[:],
                        start=True, stop=True))
                    EM(S, [pm, dtb_sb], [dtt1])(s_e.activation(
                        out=dtt1[:], in_=pm[:], func=AF.Relu,
                        bias=dtb_sb[:, d:d + 1]))
                    EM(S, [pm, dtb_sb], [dtt2])(s_e.activation(
                        out=dtt2[:], in_=pm[:], func=AF.Abs,
                        bias=dtb_sb[:, d:d + 1]))
                    EM(S, [dtt2], [dtt2])(s_e.activation(
                        out=dtt2[:], in_=dtt2[:], func=AF.Exp, scale=-1.0))
                    EM(S, [dtt2], [dtt2])(s_e.activation(
                        out=dtt2[:], in_=dtt2[:], func=AF.Ln, bias=1.0))
                    EM(V, [dtt1, dtt2], [dtbf])(v_e.tensor_add(dtbf[:], dtt1[:], dtt2[:]))
                    EM(Q, [dtbf], [dt_sc], dma=True)(q_e.dma_start(
                        out=dt_sc[d * 128:(d + 1) * 128, tb * 512:(tb + 1) * 512],
                        in_=dtbf[:]))

        sc.barrier()
        wstack.close()
        HALF = NCH // 2
        if PHASES not in ("toE", "toA", "toC"):
         with contextlib.ExitStack() as ph:
            p_g = [ph.enter_context(nc.psum_tensor(f"pg{i}", [128, 1024], F32))
                   for i in range(2)]
            p_u = [ph.enter_context(nc.psum_tensor(f"pu{i}", [128, 1024], F32))
                   for i in range(2)]
            wgt = [ph.enter_context(nc.sbuf_tensor(f"g_wg{i}", [128, NDT * 128], BF16))
                   for i in range(2)]
            wut = [ph.enter_context(nc.sbuf_tensor(f"g_wu{i}", [128, NDT * 128], BF16))
                   for i in range(2)]
            sgl = [ph.enter_context(nc.sbuf_tensor(f"g_sgl{i}", [128, 1024], BF16))
                   for i in range(2)]
            ucb = [ph.enter_context(nc.sbuf_tensor(f"g_ucb{i}", [128, 1024], BF16))
                   for i in range(2)]
            actst = [ph.enter_context(nc.sbuf_tensor(f"g_act{i}", [128, TSUB], BF16))
                     for i in range(2)]
            cbbc = [ph.enter_context(nc.sbuf_tensor(f"g_cb{i}", [128, TSUB], BF16))
                    for i in range(2)]

            def g_unit(bi, wsrc_g, wsrc_u, col0, cbc, dst, dst_r0):
                EM(G, [wsrc_g], [wgt[bi]], dma=True)(g_e.dma_start(
                    out=wgt[bi][:].rearrange("p (a f) -> p a f", a=NDT),
                    in_=wsrc_g[:, col0:col0 + 128]
                    .rearrange("(a p) f -> p a f", p=128)))
                EM(G, [wsrc_u], [wut[bi]], dma=True)(g_e.dma_start(
                    out=wut[bi][:].rearrange("p (a f) -> p a f", a=NDT),
                    in_=wsrc_u[:, col0:col0 + 128]
                    .rearrange("(a p) f -> p a f", p=128)))
                for ht in range(NDT):
                    for tb in range(2):
                        for hf in range(2):
                            c0 = tb * 1024 + hf * 512
                            EM(T, [wgt[bi], xtown], [p_g[tb]],
                               inc=(ht == NDT - 1))(t_e.matmul(
                                p_g[tb][:, hf * 512:(hf + 1) * 512],
                                wgt[bi][:, ht * 128:(ht + 1) * 128],
                                xtown[:, ht * TSUB + c0:ht * TSUB + c0 + 512],
                                start=(ht == 0), stop=(ht == NDT - 1)))
                    for tb in range(2):
                        for hf in range(2):
                            c0 = tb * 1024 + hf * 512
                            EM(T, [wut[bi], xtown], [p_u[tb]],
                               inc=(ht == NDT - 1))(t_e.matmul(
                                p_u[tb][:, hf * 512:(hf + 1) * 512],
                                wut[bi][:, ht * 128:(ht + 1) * 128],
                                xtown[:, ht * TSUB + c0:ht * TSUB + c0 + 512],
                                start=(ht == 0), stop=(ht == NDT - 1)))
                for tb in range(2):
                    pg, pu = p_g[tb], p_u[tb]
                    EM(S, [pg], [sgl[tb]])(s_e.activation(
                        out=sgl[tb][:], in_=pg[:], func=AF.Silu))
                    EM(V, [pu, cbc], [ucb[tb]])(v_e.tensor_mul(
                        ucb[tb][:], pu[:], cbc[:, tb * 1024:(tb + 1) * 1024]))
                    EM(V, [sgl[tb], ucb[tb]], [actst[bi]])(v_e.tensor_mul(
                        actst[bi][:, tb * 1024:(tb + 1) * 1024],
                        sgl[tb][:], ucb[tb][:]))
                EM(Q, [actst[bi]], [dst], dma=True)(q_e.dma_start(
                    out=dst[dst_r0:dst_r0 + 128, :], in_=actst[bi][:]))

            def gen_G():
                EM(G, [sgT_sc], [sgbc], dma=True)(g_e.dma_start(
                    out=sgbc[:].rearrange("p (o t) -> p o t", o=1),
                    in_=sgT_sc.unsqueeze(0).broadcast_to((128, 1, TSUB))))
                for e in range(NE):
                    cb = cbbc[e % 2]
                    EM(G, [combT_sc], [cb], dma=True)(g_e.dma_start(
                        out=cb[:].rearrange("p (o t) -> p o t", o=1),
                        in_=combT_sc[e:e + 1, :].unsqueeze(0)
                        .broadcast_to((128, 1, TSUB))))
                    for ftl in range(6):
                        g_unit((e * 6 + ftl) % 2, wg[e], wu[e], ftl * 128,
                               cb, act_sc[e], ftl * 128)
                        yield
                for fsl in range(32):
                    g_unit(fsl % 2, swg, swu, fsl * 128,
                           sgbc, mid_sc, fsl * 128)
                    yield

            sbuf = nc.sbuf_tensor
            Bbc = ph.enter_context(sbuf("f_Bbc", [128, NN * TC], BF16))
            Cbc = ph.enter_context(sbuf("f_Cbc", [128, NN * TC], BF16))
            fb = dict(
                dtp=[ph.enter_context(sbuf(f"f_dt{i}", [128, TC], BF16))
                     for i in range(2)],
                xp=[ph.enter_context(sbuf(f"f_x{i}", [128, TC], BF16))
                    for i in range(2)],
                gt=[ph.enter_context(sbuf(f"f_g{i}", [128, TC], BF16))
                    for i in range(2)],
                dtx=ph.enter_context(sbuf("f_dtx", [128, TC], BF16)),
                adb=ph.enter_context(sbuf("f_adb", [128, NN * TC], BF16)),
                dAf=ph.enter_context(sbuf("f_dAf", [128, NN * TC], BF16)),
                stf=ph.enter_context(sbuf("f_stf", [128, NN * TC], BF16)),
                inj=ph.enter_context(sbuf("f_inj", [128, NN], F32)),
                y1=ph.enter_context(sbuf("f_y1", [128, TC], F32)),
                y2=ph.enter_context(sbuf("f_y2", [128, TC], BF16)),
                ygb=ph.enter_context(sbuf("f_ygb", [128, TC], BF16)),
            )

            def load_piece(ch, d, slot):
                bb = fb
                t0 = ch * TC
                dtp, xp, gt = bb["dtp"][slot], bb["xp"][slot], bb["gt"][slot]
                EM(S, [dt_sc], [dtp], dma=True)(s_e.dma_start(
                    out=dtp[:], in_=dt_sc[d * 128:(d + 1) * 128, t0:t0 + TC]))
                EM(S, [x_sc], [xp], dma=True)(s_e.dma_start(
                    out=xp[:], in_=x_sc[d * 128:(d + 1) * 128, t0:t0 + TC]))
                if ch >= HALF:
                    EM(S, [gate_sc], [gt], dma=True)(s_e.dma_start(
                        out=gt[:], in_=gate_sc[d * 128:(d + 1) * 128,
                                               t0 - TOWN:t0 - TOWN + TC]))

            def piece(ch, d, slot):
                bb = fb
                ee = v_e
                engk = V
                t0 = ch * TC
                dtp, xp, gt = bb["dtp"][slot], bb["xp"][slot], bb["gt"][slot]
                adbv = bb["adb"][:].rearrange("p (n t) -> p n t", n=NN)
                dAv = bb["dAf"][:].rearrange("p (n t) -> p n t", n=NN)
                for n in range(NN):
                    EM(S, [dtp, A_f32], [bb["dAf"]])(s_e.activation(
                        out=bb["dAf"][:, n * TC:(n + 1) * TC], in_=dtp[:],
                        func=AF.Exp,
                        scale=A_f32[:, d * NN + n:d * NN + n + 1]))
                EM(engk, [dtp, xp], [bb["dtx"]])(ee.tensor_mul(
                    bb["dtx"][:], dtp[:], xp[:]))
                dtxv = bb["dtx"][:].unsqueeze(1).broadcast_to((128, NN, TC))
                EM(engk, [bb["dtx"], Bbc], [bb["adb"]])(ee.tensor_tensor(
                    adbv, dtxv, Bbc[:].rearrange("p (n t) -> p n t", n=NN), OP.mult))
                cslice = carry_sb[:, d * NN:(d + 1) * NN]
                EM(engk, [bb["dAf"], carry_sb], [bb["inj"]])(ee.tensor_tensor(
                    bb["inj"][:].unsqueeze(2),
                    dAv[:, :, 0:1], cslice.unsqueeze(2), OP.mult))
                EM(engk, [bb["inj"], bb["adb"]], [bb["adb"]])(ee.tensor_tensor(
                    adbv[:, :, 0:1], bb["inj"][:].unsqueeze(2),
                    adbv[:, :, 0:1], OP.add))
                EM(engk, [], [bb["dAf"]])(ee.memset(dAv[:, :, 0:1], 0.0))
                EM(engk, [bb["dAf"], bb["adb"]], [bb["stf"]])(ee.tensor_tensor_scan(
                    out=bb["stf"][:], data0=bb["dAf"][:], data1=bb["adb"][:],
                    initial=0.0, op0=OP.mult, op1=OP.add))
                stv = bb["stf"][:].rearrange("p (n t) -> p n t", n=NN)
                EM(engk, [bb["stf"]], [carry_sb])(ee.tensor_copy(
                    cslice.unsqueeze(2), stv[:, :, TC - 1:TC]))
                if ch == HALF - 1:
                    EM(engk, [carry_sb, pmask_sb], [carry_sb])(ee.tensor_scalar(
                        out=cslice, in0=cslice, scalar1=pmask_sb[:, 0:1],
                        scalar2=None, op0=OP.mult))
                if ch < HALF:
                    return
                EM(engk, [bb["stf"], Cbc], [bb["adb"]])(ee.tensor_tensor(
                    bb["adb"][:], bb["stf"][:], Cbc[:], OP.mult))
                w = NN * TC // 2
                while w >= TC:
                    EM(engk, [bb["adb"]], [bb["adb"]])(ee.tensor_add(
                        bb["adb"][:, 0:w], bb["adb"][:, 0:w],
                        bb["adb"][:, w:2 * w]))
                    w //= 2
                EM(engk, [xp, D_sb, bb["adb"]], [bb["y1"]])(ee.scalar_tensor_tensor(
                    out=bb["y1"][:], in0=xp[:], scalar=D_sb[:, d:d + 1],
                    in1=bb["adb"][:, 0:TC], op0=OP.mult, op1=OP.add))
                EM(engk, [bb["y1"], gt], [bb["ygb"]])(ee.tensor_mul(
                    bb["ygb"][:], bb["y1"][:], gt[:]))
                role_dma(Q, sh_ygg[0][d * 128:(d + 1) * 128, t0 - TOWN:t0 - TOWN + TC],
                         sh_ygg[1][d * 128:(d + 1) * 128, t0 - TOWN:t0 - TOWN + TC],
                         bb["ygb"][:], q_role)

            def gen_F():
                seq = [(ch, d) for ch in range(NCH) for d in range(NDT)]
                load_piece(*seq[0], 0)
                for i, (ch, d) in enumerate(seq):
                    if d == 0:
                        t0 = ch * TC
                        EM(S, [bc_sc], [Bbc], dma=True)(s_e.dma_start(
                            out=Bbc[:].rearrange("p (n t) -> p n t", n=NN),
                            in_=bc_sc[0:16, t0:t0 + TC].unsqueeze(0)
                            .broadcast_to((128, 16, TC))))
                        EM(S, [bc_sc], [Cbc], dma=True)(s_e.dma_start(
                            out=Cbc[:].rearrange("p (n t) -> p n t", n=NN),
                            in_=bc_sc[16:32, t0:t0 + TC].unsqueeze(0)
                            .broadcast_to((128, 16, TC))))
                    if i + 1 < len(seq):
                        load_piece(*seq[i + 1], (i + 1) % 2)
                    piece(ch, d, i % 2)
                    yield

            gens = [gen_G()]
            if PHASES != "toG":
                gens.append(gen_F())
            alive = list(gens)
            while alive:
                for gen in list(alive):
                    try:
                        next(gen)
                    except StopIteration:
                        alive.remove(gen)

        xtstack.close()
        if PHASES in ("toF", "toG", "toE", "toA", "toC"):
            sc.wait_all_dma(Q)
            globals()["_LAST_CNT"] = dict(sc.cnt)
            return
        sc.barrier()
        do_poll = PHASES not in ("nopoll",)
        if do_poll:
            sem_f, tok_f = sc.dma_pre(Q, [sh_ygg[0], sh_ygg[1], nonce_sb],
                                      [sh_flag])
            sc.wait_all_dma(Q, keys=[Q])
            with q_e.If_eq(q_role, 0):
                q_e.dma_start(out=sh_flag[0:1, 0:1],
                              in_=nonce_sb[0:1, 0:1]).then_inc(sem_f, 16)
            with q_e.Else():
                q_e.dma_start(out=sh_flag[0:1, 1:2],
                              in_=nonce_sb[0:1, 0:1]).then_inc(sem_f, 16)
            sc.dma_post(Q, tok_f, [nonce_sb], [sh_flag])

        with contextlib.ExitStack() as ph:
          if do_poll:
            r_a = ph.enter_context(g_e.register("r_a"))
            r_b = ph.enter_context(g_e.register("r_b"))
            r_2n = ph.enter_context(g_e.register("r_2n"))
            r_c = ph.enter_context(g_e.register("r_c"))
            r_it = ph.enter_context(g_e.register("r_it"))
            sc.pre(G, [nonce_sb], [pollw])
            g_e.reg_load(r_a, nonce_sb[0:1, 0:1])
            g_e.reg_add(r_2n, r_a, r_a)
            g_e.reg_mov(r_c, 0)
            g_e.reg_mov(r_it, 0)
            POOL = mybir.EngineType.Pool
            nc.br("poll_s", engines=[POOL])
            with nc.body("poll_s", valid_engines=[POOL]):
                g_e.dma_start(out=pollw[0:1, 0:2], in_=sh_flag[0:1, 0:2]) \
                   .then_inc(psem, 16)
                g_e.reg_add(r_c, r_c, 16)
                g_e.wait_ge(psem, r_c)
                g_e.reg_load(r_a, pollw[0:1, 0:1])
                g_e.reg_load(r_b, pollw[0:1, 1:2])
                g_e.reg_add(r_a, r_a, r_b)
                g_e.reg_sub(r_a, r_a, r_2n)
                g_e.reg_add(r_it, r_it, 1)
                g_e.br_eq(r_a, 0, on_true="poll_e", on_false="poll_c")
            with nc.body("poll_c", valid_engines=[POOL]):
                g_e.br_lt(r_it, MAXPOLL, on_true="poll_s", on_false="poll_f")
            with nc.body("poll_f", valid_engines=[POOL]):
                g_e.halt()
                g_e.br("poll_e")
            nc.switch_bb("poll_e")
        g_e.nop().then_inc(sems[G], 1)
        sc.cnt[G] += 1
        sc.idx[G] += 1
        sc.mark_write(sh_ygg[0], G)
        sc.mark_write(sh_ygg[1], G)

        sc.barrier()
        ENTRIES = ([("moe", e, fk) for e in range(NE) for fk in range(6)]
                   + [("sh", fs, 0) for fs in range(32)]
                   + [("op", ik, 0) for ik in range(32)])
        NSUB = 16
        with contextlib.ExitStack() as ph:
            p_mg = [ph.enter_context(nc.psum_tensor(f"pj{i}", [128, 512], F32))
                    for i in range(8)]
            lsl = [ph.enter_context(nc.sbuf_tensor(f"j_l{i}", [128, NSUB * 4 * 128],
                                                   BF16)) for i in range(2)]
            rsl = [ph.enter_context(nc.sbuf_tensor(f"j_r{i}", [128, NSUB * 1024],
                                                   BF16)) for i in range(2)]
            xres = ph.enter_context(nc.sbuf_tensor("j_xr", [128, 4 * 1024], F32))
            outs = ph.enter_context(nc.sbuf_tensor("j_o", [128, 4 * 1024], F32))

            nsubs = len(ENTRIES) // NSUB
            for hmh in range(2):
                for tg in range(4):
                    EM(Q, [x_ffn], [xres], dma=True)(q_e.dma_start(
                        out=xres[:].rearrange("p (a c) -> p a c", a=4),
                        in_=x_ffn.rearrange("(a p) c -> p a c", p=128)
                        [:, tg * 4:(tg + 1) * 4, hmh * 1024:(hmh + 1) * 1024]))
                    for sci in range(nsubs):
                        bi = sci % 2
                        ls, rs = lsl[bi], rsl[bi]
                        for k in range(NSUB):
                            kind, a1, a2 = ENTRIES[sci * NSUB + k]
                            tcols = slice(tg * 512, (tg + 1) * 512)
                            rk, r_e = (G, g_e) if k % 2 == 0 else (S, s_e)
                            if kind == "moe":
                                EM(Q, [act_sc], [ls], dma=True)(q_e.dma_start(
                                    out=ls[:, k * 512:(k + 1) * 512],
                                    in_=act_sc[a1][a2 * 128:(a2 + 1) * 128, tcols]))
                                EM(rk, [wd_bf], [rs], dma=True)(r_e.dma_start(
                                    out=rs[:, k * 1024:(k + 1) * 1024],
                                    in_=wd_bf[a1][a2 * 128:(a2 + 1) * 128,
                                               hmh * 1024:(hmh + 1) * 1024]))
                            elif kind == "sh":
                                EM(Q, [mid_sc], [ls], dma=True)(q_e.dma_start(
                                    out=ls[:, k * 512:(k + 1) * 512],
                                    in_=mid_sc[a1 * 128:(a1 + 1) * 128, tcols]))
                                EM(rk, [swd_bf], [rs], dma=True)(r_e.dma_start(
                                    out=rs[:, k * 1024:(k + 1) * 1024],
                                    in_=swd_bf[a1 * 128:(a1 + 1) * 128,
                                            hmh * 1024:(hmh + 1) * 1024]))
                            else:
                                slab = sh_ygg[0] if a1 < 16 else sh_ygg[1]
                                rr = (a1 % 16)
                                sem_s, tok_s = sc.dma_pre(Q, [slab], [ls])
                                with q_e.If_eq(q_role, 0):
                                    q_e.dma_start(
                                        out=ls[:, k * 512:(k + 1) * 512],
                                        in_=slab[rr * 128:(rr + 1) * 128,
                                                 tg * 512:(tg + 1) * 512]) \
                                        .then_inc(sem_s, 16)
                                with q_e.Else():
                                    q_e.dma_start(
                                        out=ls[:, k * 512:(k + 1) * 512],
                                        in_=slab[rr * 128:(rr + 1) * 128,
                                                 TSUB + tg * 512:
                                                 TSUB + (tg + 1) * 512]) \
                                        .then_inc(sem_s, 16)
                                sc.dma_post(Q, tok_s, [slab], [ls])
                                EM(Q, [wopT_sc], [rs], dma=True)(q_e.dma_start(
                                    out=rs[:, k * 1024:(k + 1) * 1024],
                                    in_=wopT_sc[a1 * 128:(a1 + 1) * 128,
                                                hmh * 1024:(hmh + 1) * 1024]))
                        for tt in range(4):
                            for k in range(NSUB):
                                for hc in range(2):
                                    pm = p_mg[2 * tt + hc]
                                    EM(T, [ls, rs], [pm],
                                       inc=(k == NSUB - 1))(t_e.matmul(
                                        pm[:],
                                        ls[:, k * 512 + tt * 128:
                                           k * 512 + (tt + 1) * 128],
                                        rs[:, k * 1024 + hc * 512:
                                           k * 1024 + (hc + 1) * 512],
                                        start=(sci == 0 and k == 0),
                                        stop=(sci == nsubs - 1
                                              and k == NSUB - 1)))
                    for tt in range(4):
                        for hc in range(2):
                            pm = p_mg[2 * tt + hc]
                            EM(V, [pm, xres], [outs])(v_e.tensor_add(
                                outs[:, tt * 1024 + hc * 512:
                                     tt * 1024 + (hc + 1) * 512],
                                pm[:],
                                xres[:, tt * 1024 + hc * 512:
                                     tt * 1024 + (hc + 1) * 512]))
                    EM(Q, [outs], [out], dma=True)(q_e.dma_start(
                        out=out.rearrange("(a p) c -> p a c", p=128)
                        [:, tg * 4:(tg + 1) * 4, hmh * 1024:(hmh + 1) * 1024],
                        in_=outs[:].rearrange("p (a c) -> p a c", a=4)))

        sc.wait_all_dma(Q)

    es.close()
    return nc


_NC_CACHE = None


def kernel(L0_attn_hidden_states, L0_attn_in_proj_w, L0_attn_conv_w,
           L0_attn_conv_b, L0_attn_x_proj_weight, L0_attn_dt_proj_weight,
           L0_attn_dt_proj_bias, L0_attn_A_log, L0_attn_D, L0_attn_out_proj_w,
           L0_ffn_router_weight, L0_ffn_w_gate, L0_ffn_w_up, L0_ffn_w_down,
           L0_ffn_sw_gate, L0_ffn_sw_up, L0_ffn_sw_down, L0_ffn_sgate_w):
    global _NC_CACHE
    f32 = np.float32
    x = np.asarray(L0_attn_hidden_states, f32)
    cc = np.ascontiguousarray
    nonce = int(np.random.randint(1, 2 ** 30))
    in_maps = []
    for c in range(8):
        b, S, j = c // 4, (c // 2) % 2, c % 2
        x_in = np.zeros((TPAD, H), f32)
        if S == 1:
            x_in[128:4224] = x[b, 0:4096]
        x_in[4224:8320] = x[b, S * 4096:(S + 1) * 4096]
        x_ffn = cc(x[b, S * 4096 + j * TSUB:S * 4096 + (j + 1) * TSUB])
        wip = np.asarray(L0_attn_in_proj_w, f32)[j * 4096:(j + 1) * 4096]
        m = dict(
            x_in=x_in, x_ffn=x_ffn,
            wip_x=cc(wip[0::2]), wip_g=cc(wip[1::2]),
            convw=cc(np.asarray(L0_attn_conv_w, f32)[j * H:(j + 1) * H]),
            convb=cc(np.asarray(L0_attn_conv_b, f32)[j * H:(j + 1) * H]),
            xpw=cc(np.asarray(L0_attn_x_proj_weight, f32)[j]),
            dtw=cc(np.asarray(L0_attn_dt_proj_weight, f32)[j]),
            dtb=cc(np.asarray(L0_attn_dt_proj_bias, f32)[j]),
            alog=cc(np.asarray(L0_attn_A_log, f32)[j]),
            dvec=cc(np.asarray(L0_attn_D, f32)[j]),
            wop=cc(np.asarray(L0_attn_out_proj_w, f32)),
            rw=cc(np.asarray(L0_ffn_router_weight, f32)),
            wg=cc(np.asarray(L0_ffn_w_gate, f32)),
            wu=cc(np.asarray(L0_ffn_w_up, f32)),
            wd=cc(np.asarray(L0_ffn_w_down, f32)),
            swg=cc(np.asarray(L0_ffn_sw_gate, f32)),
            swu=cc(np.asarray(L0_ffn_sw_up, f32)),
            swd=cc(np.asarray(L0_ffn_sw_down, f32)),
            sgw=cc(np.asarray(L0_ffn_sgate_w, f32)),
            ident=np.eye(128, dtype=f32),
            constv=np.tile(np.array([[1e-5, 1.0, -1.0, 1.0 / H, 0.0, 0.0, 0.0, 0.0]], f32), (128, 1)),
            pmask=np.full((128, 1), float(S), f32),
            rolev=np.full((1, 4), j, np.int32),
            noncev=np.full((1, 4), nonce, np.int32),
        )
        in_maps.append(m)
    if _NC_CACHE is None:
        _NC_CACHE = build_nc()
    res = run_bass_kernel_spmd(_NC_CACHE, in_maps, core_ids=list(range(8)))
    out_full = np.zeros((2, 8192, H), f32)
    for c in range(8):
        b, S, j = c // 4, (c // 2) % 2, c % 2
        r0 = S * 4096 + j * TSUB
        out_full[b, r0:r0 + TSUB, :] = res.results[c]["out"]
    return out_full
